# version 16
# baseline (speedup 1.0000x reference)
import contextlib
import numpy as np
import concourse.bass as bass
import concourse.mybir as mybir
from concourse.bass_utils import run_bass_kernel_spmd

F32 = mybir.dt.float32
BF16 = mybir.dt.bfloat16
I32 = mybir.dt.int32
AF = mybir.ActivationFunctionType
ALU = mybir.AluOpType
AX = mybir.AxisListType.X

NCORES = 8
T = 16384
C = 1024
H = 16
N = 64
Q = 256
TPC = T // NCORES
NCH = TPC // Q
CCH = C // 128
HID = 4096
HCH = HID // 128
MODS = 6 * C // NCORES
W1SH = HID // NCORES
W2SH = C // NCORES
EPS = 1e-5
INV_C = 1.0 / C


def build(dbg=(), stop_after=None):
    nc = bass.Bass("TRN2", target_bir_lowering=False)
    dbg = set(dbg)
    dbg_map = {}

    def din(name, shape, dt=F32):
        return nc.dram_tensor(name, shape, dt, kind="ExternalInput").ap()

    xs = din("xs", [C, TPC + 1])
    wrT = din("wrT", [C, C]); wkT = din("wkT", [C, C])
    wvT = din("wvT", [C, C]); woT = din("woT", [C, C])
    w1T = din("w1T", [C, HID]); w2T = din("w2T", [HID, C])
    w1s = din("w1s", [C, W1SH]); w2s = din("w2s", [HID, W2SH])
    wmod_s = din("wmod_s", [C, MODS])
    c_fm = din("c_fm", [128, CCH])
    maa = {w: din(f"maa_{w}", [128, CCH]) for w in "kvr"}
    bv_row = din("bv_row", [1, C]); bo_row = din("bo_row", [1, C])
    br_row = din("br_row", [1, C]); bk_row = din("bk_row", [1, C])
    td_fm = din("td_fm", [128, CCH])
    td_hb = din("td_hb", [128, H]); uf_hb = din("uf_hb", [128, H])
    cwv = {k: din(k, [128, HCH]) for k in ("cw0", "cw1", "cw2", "cb_fm")}
    expw = din("expw", [128, NCORES])
    selL = din("selL", [128, NCORES]); selR = din("selR", [128, NCORES])
    c0f = din("c0f", [128, 1])

    outT = nc.dram_tensor("outT", [C, TPC], F32, kind="ExternalOutput").ap()

    mod_in = nc.dram_tensor("mod_in", [1, MODS + 2], F32).ap()
    mod_out = nc.dram_tensor("mod_out", [NCORES, MODS + 2], F32).ap()
    mod_flat = nc.dram_tensor("mod_flat", [NCORES, MODS], F32).ap()
    dsp_in = nc.dram_tensor("dsp_in", [128, NCH * N], F32).ap()
    dsp_out = nc.dram_tensor("dsp_out", [NCORES * 128, NCH * N], F32).ap()
    halo_in = nc.dram_tensor("halo_in", [128, HCH * 2], F32).ap()
    halo_out = nc.dram_tensor("halo_out", [NCORES * 128, HCH * 2], F32).ap()
    w1q_d = nc.dram_tensor("w1q_d", [CCH, 128, HID], BF16).ap()
    w2q_d = nc.dram_tensor("w2q_d", [HCH, 128, C], BF16).ap()
    x2_d = nc.dram_tensor("x2_d", [CCH, 128, TPC], F32).ap()
    xqs_d = nc.dram_tensor("xqs_d", [CCH, 128, TPC], BF16).ap()
    xqh_d = nc.dram_tensor("xqh_d", [HCH, 128, TPC], BF16).ap()

    def SB(stack, name, shape, dt=F32):
        return stack.enter_context(nc.sbuf_tensor(name, shape, dt))

    def PS(stack, name, shape, dt=F32):
        return stack.enter_context(nc.psum_tensor(name, shape, dt))

    est0 = contextlib.ExitStack()
    ones_b = SB(est0, "ones_b", [128, 128], BF16)
    ones_r = SB(est0, "ones_r", [1, 256], BF16)
    ones_fc = SB(est0, "ones_fc", [128, 1], F32)
    ones_fr = SB(est0, "ones_fr", [1, 128], F32)
    ident_f = SB(est0, "ident_f", [128, 128], F32)
    vecs = SB(est0, "vecs", [128, 18 * CCH], F32)
    VN = ["sa", "ca1", "alk", "alv", "alr", "bek", "bev", "ber", "ga", "gb",
          "cb1", "sbv", "lnwd", "z0", "t0", "t1", "t2", "t3"]
    V_ = {nm: vecs.ap()[:, i * CCH:(i + 1) * CCH] for i, nm in enumerate(VN)}
    z0b = SB(est0, "z0b", [128, CCH], BF16)
    lnwd_hb = SB(est0, "lnwd_hb", [128, H], F32)
    ufhb = SB(est0, "ufhb", [128, H], F32)
    expw_sb = SB(est0, "expw_sb", [128, NCORES], F32)
    convs = SB(est0, "convs", [128, 4 * HCH], F32)
    CV = {nm: convs.ap()[:, i * HCH:(i + 1) * HCH]
          for i, nm in enumerate(["w0", "w1", "w2", "b"])}
    wspow = SB(est0, "wspow", [128, CCH, NCH], F32)
    s0w = SB(est0, "s0w", [128, CCH, NCORES], F32)
    selLb = SB(est0, "selLb", [128, NCORES], F32)
    selRb = SB(est0, "selRb", [128, NCORES], F32)
    bv_rb = SB(est0, "bv_rb", [1, C], BF16)
    bo_rb = SB(est0, "bo_rb", [1, C], BF16)
    br_rb = SB(est0, "br_rb", [1, C], BF16)
    bk_rb = SB(est0, "bk_rb", [1, C], BF16)
    mwb = SB(est0, "mwb", [128, 2], F32)
    rwb = SB(est0, "rwb", [128, 2], F32)
    c2row = SB(est0, "c2row", [128, 16], F32)
    c0fb = SB(est0, "c0fb", [128, 1], F32)
    magicT = SB(est0, "magicT", [128, 264], I32)
    eLh = SB(est0, "eLh", [128, HCH], F32)
    eRh = SB(est0, "eRh", [128, HCH], F32)

    estC = contextlib.ExitStack()
    yp_f = SB(estC, "yp_f", [128, CCH * TPC], BF16)
    estB = contextlib.ExitStack()
    rpp_f = SB(estB, "rpp_f", [128, CCH * TPC], BF16)

    def rppv(j, lo, hi):
        return rpp_f.ap()[:, j * TPC + lo: j * TPC + hi]

    def ypv(j, lo, hi):
        return yp_f.ap()[:, j * TPC + lo: j * TPC + hi]

    estA = contextlib.ExitStack()
    wrT_sb = SB(estA, "wrT_sb", [128, CCH, C], BF16)
    wkT_sb = SB(estA, "wkT_sb", [128, CCH, C], BF16)
    wvT_sb = SB(estA, "wvT_sb", [128, CCH, C], BF16)
    masks0 = SB(estA, "masks0", [128, H * Q], BF16)
    masks1 = SB(estA, "masks1", [128, H * 128], BF16)
    wbB = SB(estA, "wbB", [128, CCH * Q], BF16)
    wkBt = SB(estA, "wkBt", [128, 2 * C], BF16)
    Lc = SB(estA, "Lc", [128, 2 * CCH * N], F32)
    Lb = SB(estA, "Lb", [128, CCH, N], BF16)
    xch = SB(estA, "xch", [128, CCH, Q + 1], F32)
    xbf = SB(estA, "xbf", [128, CCH, Q + 1], BF16)
    xsq = SB(estA, "xsq", [128, CCH, Q + 1], BF16)
    uch = SB(estA, "uch", [128, CCH, Q + 1], BF16)
    mixb = SB(estA, "mixb", [128, CCH, Q], BF16)
    statm = SB(estA, "statm", [128, Q + 1], F32)
    statv = SB(estA, "statv", [128, Q + 1], F32)
    statw = SB(estA, "statw", [128, Q + 1], F32)
    rfm = SB(estA, "rfm", [128, CCH, Q], BF16)
    kfm = SB(estA, "kfm", [128, CCH, Q], BF16)
    ktm = SB(estA, "ktm", [128, 2, C], BF16)
    vtm = SB(estA, "vtm", [128, 2, C], BF16)
    Asb = SB(estA, "Asb", [128, 2, Q + 128], BF16)

    estP0 = contextlib.ExitStack()
    siluc = SB(estP0, "siluc", [128, CCH], F32)
    mod_sb = SB(estP0, "mod_sb", [1, MODS + 2], F32)
    modg = SB(estP0, "modg", [NCORES, MODS + 2], F32)
    iotaQ = SB(estP0, "iotaQ", [128, Q], I32)
    dQ = SB(estP0, "dQ", [128, Q], F32)
    I0 = SB(estP0, "I0", [128, Q], F32)
    tauQ = SB(estP0, "tauQ", [128, Q], F32)
    expP = SB(estP0, "expP", [128, 2 * N], F32)
    iotaC = SB(estP0, "iotaC", [128, NCH], I32)
    iotaCf = SB(estP0, "iotaCf", [128, NCH], F32)
    wstage = SB(estP0, "wstage", [128, 1, 512], F32)

    def lcv(s, hp=None):
        b = (s % 2) * CCH * N
        if hp is None:
            return Lc.ap()[:, b:b + CCH * N]
        return Lc.ap()[:, b + hp * N: b + (hp + 1) * N]

    cnt = {}
    sems = {}
    semstack = contextlib.ExitStack()
    for s in ["dq", "dq2", "dq3", "sv", "ss", "sg", "st", "cc", "so"]:
        sems[s] = semstack.enter_context(nc.semaphore(s))
        cnt[s] = 0

    blkg = nc.Block()
    block = blkg.__enter__()

    def inc(instr, name, n=1):
        instr.then_inc(sems[name], n)
        cnt[name] += n
        return cnt[name]

    def W(e, *names):
        for name in names:
            e.wait_ge(sems[name], cnt[name])

    def selfw(e, name):
        inc(e.drain(), name)
        e.wait_ge(sems[name], cnt[name])

    DBGN = [0]

    def dump(key, ap, dt=F32):
        if key not in dbg:
            return
        i = DBGN[0]; DBGN[0] += 1
        d = nc.dram_tensor(f"dbg{i}", list(ap.shape), dt, kind="ExternalOutput").ap()
        dbg_map[key] = f"dbg{i}"

        def f(e):
            W(e, "sv", "ss", "sg", "st", "dq", "dq2", "dq3")
            inc(e.dma_start(out=d, in_=ap), "dq3", 16)
            W(e, "dq3")
        block.sync(f)

    def barrier():
        for eng in ("sync", "vector", "scalar", "gpsimd", "tensor"):
            def f(e):
                W(e, "sv", "ss", "sg", "st", "dq", "dq2", "dq3", "cc", "so")
            getattr(block, eng)(f)

    ps0 = contextlib.ExitStack()
    ps_red = PS(ps0, "ps_red", [1, 512])[:, 0:2]
    ps_bc = PS(ps0, "ps_bc", [128, 512])[:, 0:2]
    ps_mod1 = PS(ps0, "ps_mod1", [128, 512])[0:1, 0:256]
    ps_mod0 = PS(ps0, "ps_mod0", [128, 512])[0:1, :]

    def f(e):
        inc(e.dma_start(out=V_["t0"], in_=c_fm), "dq", 16)
        for w in "kvr":
            inc(e.dma_start(out=V_["al" + w], in_=maa[w]), "dq", 16)
        inc(e.dma_start(out=V_["t1"], in_=td_fm), "dq", 16)
        inc(e.dma_start(out=lnwd_hb[:], in_=td_hb), "dq", 16)
        inc(e.dma_start(out=ufhb[:], in_=uf_hb), "dq", 16)
        inc(e.dma_start(out=selLb[:], in_=selL), "dq", 16)
        inc(e.dma_start(out=selRb[:], in_=selR), "dq", 16)
        inc(e.dma_start(out=c0fb[:], in_=c0f), "dq", 16)
        inc(e.dma_start(out=expw_sb[:], in_=expw), "dq", 16)
        for k_, nmv in (("cw0", "w0"), ("cw1", "w1"), ("cw2", "w2"), ("cb_fm", "b")):
            inc(e.dma_start(out=CV[nmv], in_=cwv[k_]), "dq", 16)
    block.sync(f)

    def f(e):
        inc(e.dma_start(out=bv_rb[:], in_=bv_row), "dq2", 16)
        inc(e.dma_start(out=bo_rb[:], in_=bo_row), "dq2", 16)
        inc(e.dma_start(out=br_rb[:], in_=br_row), "dq2", 16)
        inc(e.dma_start(out=bk_rb[:], in_=bk_row), "dq2", 16)
        for wsb, wsrc in ((wrT_sb, wrT), (wkT_sb, wkT), (wvT_sb, wvT)):
            inc(e.dma_start(out=wsb.ap(),
                            in_=wsrc.rearrange("(k p) o -> p k o", p=128)), "dq2", 16)
    block.gpsimd(f)

    def f(e):
        e.memset(ones_b[:], 1.0)
        e.memset(ones_r[:], 1.0)
        e.memset(ones_fc[:], 1.0)
        e.memset(ones_fr[:], 1.0)
        e.memset(magicT[:], 0x5f3759df)
        e.memset(Lc[:], 0.0)
        e.memset(Lb[:], 0.0)
        e.iota(iotaQ[:], pattern=[[1, Q]], base=-1, channel_multiplier=-1)
        e.iota(iotaC[:], pattern=[[Q, NCH]], base=0, channel_multiplier=0)
        inc(e.drain(), "sg")
    block.gpsimd(f)

    def f(e):
        W(e, "sg")
        e.tensor_copy(dQ[:], iotaQ[:])
        e.tensor_copy(iotaCf[:], iotaC[:])
        selfw(e, "sv")
        e.tensor_single_scalar(I0[:], dQ[:], -1.0, ALU.is_equal)
        e.tensor_scalar_max(dQ[:], dQ[:], -1.0)
        selfw(e, "sv")
        e.tensor_copy(ident_f[:], I0[:, 0:128])
        inc(e.drain(), "sv")
    block.vector(f)

    def f(e):
        W(e, "sv")
        e.iota(iotaQ[:], pattern=[[1, Q]], base=0, channel_multiplier=0)
        selfw(e, "sg")
        inc(e.drain(), "sg")
    block.gpsimd(f)

    def f(e):
        W(e, "sg")
        e.tensor_copy(tauQ[:], iotaQ[:])
        inc(e.drain(), "sv")
    block.vector(f)

    def f(e):
        W(e, "sv")
        for tb in range(2):
            e.iota(iotaQ[:, tb * N:(tb + 1) * N], pattern=[[0, N]],
                   base=Q - 1 - 128 * tb, channel_multiplier=-1)
        inc(e.drain(), "sg")
    block.gpsimd(f)

    def f(e):
        W(e, "sg")
        e.tensor_copy(expP[:], iotaQ[:, 0:2 * N])
        inc(e.drain(), "sv")
    block.vector(f)

    def f(e):
        W(e, "dq")
        e.activation(siluc[:], V_["t0"], AF.Silu)
        e.activation(V_["lnwd"], V_["t1"], AF.Exp)
        e.activation(statw[:, 0:H], lnwd_hb[:], AF.Exp)
        inc(e.drain(), "ss")
    block.scalar(f)

    def f(e):
        W(e, "ss")
        e.tensor_scalar_mul(V_["lnwd"], V_["lnwd"], -1.0)
        e.tensor_scalar_mul(lnwd_hb[:], statw[:, 0:H], -1.0)
        inc(e.drain(), "sv")
    block.vector(f)

    for k in range(CCH):
        for half, (clo, chi) in enumerate(((0, 512), (512, MODS))):
            def f(e, k=k, clo=clo, chi=chi):
                W(e, "st")
                inc(e.dma_start(out=wstage.ap()[:, 0, 0:chi - clo],
                                in_=wmod_s[128 * k:128 * (k + 1), clo:chi]), "dq", 16)
            block.sync(f)

            def f(e, k=k, half=half, clo=clo, chi=chi):
                W(e, "dq", "ss", "sv")
                pp = ps_mod0 if half == 0 else ps_mod1
                inc(e.matmul(pp[:, 0:chi - clo], siluc.ap()[:, k:k + 1],
                             wstage.ap()[:, 0, 0:chi - clo], start=True, stop=True),
                    "st")
            block.tensor(f)

            def f(e, k=k, half=half, clo=clo, chi=chi):
                W(e, "st")
                pp = ps_mod0 if half == 0 else ps_mod1
                if k == 0:
                    e.tensor_copy(mod_sb.ap()[:, clo:chi], pp[:, 0:chi - clo])
                else:
                    e.tensor_add(mod_sb.ap()[:, clo:chi], mod_sb.ap()[:, clo:chi],
                                 pp[:, 0:chi - clo])
                inc(e.drain(), "sv")
            block.vector(f)

    def wsum_reduce(src, kch, width, col):
        view = src.rearrange("(k p) m -> p k m", p=128)
        for kk in range(kch):
            def f(e, kk=kk):
                W(e, "sv")
                inc(e.dma_start(out=wstage.ap()[:, 0, 0:width],
                                in_=view[:, kk, :]), "dq", 16)
            block.sync(f)

            def f(e, kk=kk, col=col):
                W(e, "dq")
                e.reduce_sum(V_["t3"][:, 0:1], wstage.ap()[:, 0, 0:width],
                             AX, apply_absolute_value=True)
                selfw(e, "sv")
                if kk == 0:
                    e.tensor_copy(V_["t2"][:, col:col + 1], V_["t3"][:, 0:1])
                else:
                    e.tensor_add(V_["t2"][:, col:col + 1], V_["t2"][:, col:col + 1],
                                 V_["t3"][:, 0:1])
                selfw(e, "sv")
            block.vector(f)

    wsum_reduce(w1s, CCH, W1SH, 0)
    wsum_reduce(w2s, HCH, W2SH, 1)

    def f(e):
        W(e, "sv")
        e.matmul(ps_red[:], ones_fc[:], V_["t2"][:, 0:2], start=True, stop=True)
        inc(e.drain(), "st")
    block.tensor(f)

    def f(e):
        W(e, "st")
        e.tensor_copy(mod_sb.ap()[:, MODS:MODS + 2], ps_red[:])
        inc(e.drain(), "sv")
    block.vector(f)

    dump("modsb", mod_sb[:])
    dump("siluc", siluc[:])

    def f(e):
        W(e, "sv")
        inc(e.dma_start(out=mod_in, in_=mod_sb[:]), "dq", 16)
    block.sync(f)

    def f(e):
        W(e, "dq")
        inc(e.collective_compute("AllGather", ALU.bypass,
                                 replica_groups=[list(range(NCORES))],
                                 ins=[mod_in], outs=[mod_out]), "cc", 1)
        W(e, "cc")
    block.gpsimd(f)

    def f(e):
        W(e, "cc")
        inc(e.dma_start(out=mod_flat, in_=mod_out[:, 0:MODS]), "dq", 16)
        inc(e.dma_start(out=modg[:], in_=mod_out), "dq", 16)
        W(e, "dq")
        flat = mod_flat.rearrange("a b -> (a b)")
        for i, nm in enumerate(["sa", "ca1", "ga", "sbv", "cb1", "gb"]):
            inc(e.dma_start(out=V_[nm],
                            in_=flat[i * C:(i + 1) * C].rearrange("(j p) -> p j", p=128),
                            allow_slow_non_contiguous=True), "dq", 16)
    block.sync(f)

    def f(e):
        W(e, "dq")
        e.matmul(ps_red[:], ones_fc[0:NCORES, :], modg.ap()[:, MODS:MODS + 2],
                 start=True, stop=True)
        inc(e.drain(), "st")
    block.tensor(f)

    def f(e):
        W(e, "st")
        e.tensor_scalar(V_["t3"][0:1, 0:2], ps_red[:], 1.0 / (C * HID), 1e-5,
                        ALU.mult, ALU.max)
        inc(e.drain(), "sv")
    block.vector(f)

    def f(e):
        W(e, "sv")
        e.matmul(ps_bc[:], ones_fr[:], V_["t3"][0:1, 0:2], start=True, stop=True)
        inc(e.drain(), "st")
    block.tensor(f)

    def f(e):
        W(e, "st")
        e.tensor_copy(mwb[:], ps_bc[:])
        selfw(e, "sv")
        e.reciprocal(rwb[:], mwb[:])
        inc(e.drain(), "sv")
    block.vector(f)

    def f(e):
        W(e, "sv", "dq")
        e.tensor_scalar_add(V_["ca1"], V_["ca1"], 1.0)
        e.tensor_scalar_add(V_["cb1"], V_["cb1"], 1.0)
        selfw(e, "sv")
        for w in "kvr":
            e.tensor_mul(V_["be" + w], V_["al" + w], V_["ca1"])
        selfw(e, "sv")
        for w in "kvr":
            e.tensor_sub(V_["al" + w], V_["ca1"], V_["be" + w])
        e.reciprocal(V_["t0"], V_["ca1"])
        selfw(e, "sv")
        e.tensor_mul(V_["z0"], V_["sa"], V_["t0"])
        selfw(e, "sv")
        e.tensor_scalar_mul(V_["z0"], V_["z0"], -1.0)
        selfw(e, "sv")
        e.tensor_copy(z0b[:], V_["z0"])
        for g in ("ga", "gb"):
            e.tensor_scalar_max(V_["t0"], V_[g], 0.25)
            e.tensor_scalar_min(V_["t1"], V_[g], -0.25)
            e.tensor_single_scalar(V_["t2"], V_[g], 0.0, ALU.is_ge)
            selfw(e, "sv")
            e.tensor_sub(V_["t0"], V_["t0"], V_["t1"])
            selfw(e, "sv")
            e.tensor_mul(V_["t0"], V_["t0"], V_["t2"])
            selfw(e, "sv")
            e.tensor_add(V_[g], V_["t1"], V_["t0"])
            selfw(e, "sv")
        inc(e.drain(), "sv")
    block.vector(f)

    def f(e):
        W(e, "sv")
        for hp in range(CCH):
            sc = V_["lnwd"][:, hp:hp + 1]
            e.activation(wspow.ap()[:, hp, :], iotaCf[:], AF.Exp, scale=sc)
            e.activation(s0w.ap()[:, hp, :], expw_sb[:], AF.Exp, scale=sc)
        for j in range(CCH):
            e.activation(wbB.ap()[:, j * Q:(j + 1) * Q], tauQ[:], AF.Exp,
                         scale=V_["lnwd"][:, j:j + 1])
        for tb in range(2):
            for h in range(H):
                e.activation(wkBt.ap()[:, tb * C + h * N: tb * C + (h + 1) * N],
                             expP[:, tb * N:(tb + 1) * N], AF.Exp,
                             scale=lnwd_hb[:, h:h + 1])
        inc(e.drain(), "ss")
    block.scalar(f)

    for h in range(H):
        def f(e, h=h):
            W(e, "sv", "sg")
            e.activation(tauQ[:], dQ[:], AF.Exp, scale=lnwd_hb[:, h:h + 1])
            inc(e.drain(), "ss")
        block.scalar(f)

        def f(e, h=h):
            W(e, "ss")
            e.affine_select(out=tauQ[:], in_=tauQ[:], pattern=[[1, Q]], base=-1,
                            channel_multiplier=-1, compare_op=ALU.is_ge, fill=0.0)
            inc(e.drain(), "sg")
        block.gpsimd(f)

        def f(e, h=h):
            W(e, "sg")
            e.scalar_tensor_tensor(tauQ[:], I0[:], ufhb[:, h:h + 1], tauQ[:],
                                   ALU.mult, ALU.add)
            selfw(e, "sv")
            e.tensor_copy(masks0.ap()[:, h * Q:(h + 1) * Q], tauQ[:])
            e.tensor_copy(masks1.ap()[:, h * 128:(h + 1) * 128], tauQ[:, 0:128])
            inc(e.drain(), "sv")
        block.vector(f)

    dump("vecs", vecs.ap()[:, 0:16 * CCH])
    dump("masks0", masks0[:], BF16)
    dump("masks1", masks1[:], BF16)
    dump("wspow", wspow[:])
    dump("s0w", s0w[:])
    dump("wbB", wbB[:], BF16)
    dump("wkBt", wkBt[:], BF16)
    dump("mwb", mwb[:])

    done = stop_after == "p0"

    barrier()
    estP0.close()
    if not done:
        ps0.close()
        ps1 = contextlib.ExitStack()
        ps_s1 = PS(ps1, "ps_s1", [128, 512])[:, 0:Q + 1]
        ps_s2 = PS(ps1, "ps_s2", [128, 512])[:, 0:Q + 1]
        ps_p0 = PS(ps1, "ps_p0", [128, 512])
        ps_p1 = PS(ps1, "ps_p1", [128, 512])
        ps_a0 = PS(ps1, "ps_a0", [128, 512])[:, 0:Q]
        ps_a1 = PS(ps1, "ps_a1", [128, 512])[:, 0:128]
        ps_y = PS(ps1, "ps_y", [128, 512])[:, 0:Q]
        ps_d = PS(ps1, "ps_d", [128, 512])[:, 0:N]

        PCST = {}

        def p1_chunk(c):
            base = c * Q
            mixk = xsq.ap()[:, :, 0:Q]

            def f(e):
                if "u_done" in PCST:
                    e.wait_ge(sems["sv"], PCST["u_done"])
                inc(e.dma_start(out=xch.ap(),
                                in_=xs[:, base:base + Q + 1]
                                .rearrange("(k p) t -> p k t", p=128)), "dq", 16)
            block.sync(f)
            xload = cnt["dq"]

            def f(e):
                if "stats_done" in PCST:
                    e.wait_ge(sems["st"], PCST["stats_done"])
                inc(e.dma_start(out=xbf.ap(),
                                in_=xs[:, base:base + Q + 1]
                                .rearrange("(k p) t -> p k t", p=128)), "dq2", 16)
            block.gpsimd(f)
            xbload = cnt["dq2"]

            def f(e):
                e.wait_ge(sems["dq2"], xbload)
                if "mixk_done" in PCST:
                    e.wait_ge(sems["st"], PCST["mixk_done"])
                inc(e.tensor_mul(xsq[:], xbf[:], xbf[:]), "sv")
            block.vector(f)
            sq_done = cnt["sv"]

            def f(e):
                e.wait_ge(sems["sv"], sq_done)
                for k in range(CCH):
                    e.matmul(ps_s1[:], ones_b[:], xbf.ap()[:, k, :],
                             start=(k == 0), stop=(k == CCH - 1))
                for k in range(CCH):
                    last = e.matmul(ps_s2[:], ones_b[:], xsq.ap()[:, k, :],
                                    start=(k == 0), stop=(k == CCH - 1))
                inc(last, "st")
            block.tensor(f)
            PCST["stats_done"] = cnt["st"]

            def f(e):
                e.wait_ge(sems["st"], PCST["stats_done"])
                e.tensor_scalar_mul(statm[:], ps_s1[:], INV_C)
                e.tensor_mul(statv[:], statm[:], statm[:])
                e.tensor_scalar_sub(statv[:], statv[:], EPS)
                inc(e.scalar_tensor_tensor(statv[:], ps_s2[:], INV_C, statv[:],
                                           ALU.mult, ALU.subtract), "sv")
            block.vector(f)

            def f(e):
                e.wait_ge(sems["dq"], xload)
                bc_m = statm.ap()[:, None, :].broadcast_to([128, CCH, Q + 1])
                e.tensor_tensor(xch[:], xch[:], bc_m, ALU.subtract)
                e.tensor_single_scalar(statw.ap().bitcast(I32)[:, 0:Q + 1], statv.ap().bitcast(I32)[:, 0:Q + 1], 1,
                                       ALU.arith_shift_right)
                e.tensor_sub(statw.ap().bitcast(I32)[:, 0:Q + 1], magicT.ap()[:, 0:Q + 1], statw.ap().bitcast(I32)[:, 0:Q + 1])
                for _nit in range(2):
                    e.tensor_mul(statm[:], statv[:], statw[:])
                    e.tensor_mul(statm[:], statm[:], statw[:])
                    e.tensor_scalar(statm[:], statm[:], -0.5, 1.5, ALU.mult, ALU.add)
                    e.tensor_mul(statw[:], statw[:], statm[:])
                bc_v = statw.ap()[:, None, :].broadcast_to([128, CCH, Q + 1])
                inc(e.tensor_tensor(uch[:], xch[:], bc_v, ALU.mult), "sv")
            block.vector(f)
            PCST["u_done"] = cnt["sv"]

            if c == 0:
                def f(e):
                    W(e, "sv")
                    e.tensor_sub(mixb.ap()[:, 0, 0:CCH], z0b[:], uch.ap()[:, :, 0])
                    inc(e.scalar_tensor_tensor(uch.ap()[:, :, 0],
                                               mixb.ap()[:, 0, 0:CCH],
                                               c0fb.ap()[:, 0:1], uch.ap()[:, :, 0],
                                               ALU.mult, ALU.add), "sv")
                block.vector(f)
                PCST["u_done"] = cnt["sv"]
                dump("u0", uch[:], BF16)

            def mix(w, buf, free_st):
                def f(e):
                    e.wait_ge(sems["sv"], PCST["u_done"])
                    if free_st is not None:
                        e.wait_ge(sems["st"], free_st)
                    last = None
                    for j in range(CCH):
                        e.tensor_scalar(buf[:, j, :], uch.ap()[:, j, 1:Q + 1],
                                        V_["al" + w][:, j:j + 1],
                                        V_["sa"][:, j:j + 1], ALU.mult, ALU.add)
                        last = e.scalar_tensor_tensor(buf[:, j, :],
                                                      uch.ap()[:, j, 0:Q],
                                                      V_["be" + w][:, j:j + 1],
                                                      buf[:, j, :], ALU.mult, ALU.add)
                    inc(last, "sv")
                block.vector(f)
                return cnt["sv"]

            mr_done = mix("r", mixb.ap(), PCST.get("mixb_free"))
            ev = {}
            for m in range(CCH):
                pp = ps_p0 if m % 2 == 0 else ps_p1

                def f(e, m=m, pp=pp, mr_done=mr_done):
                    e.wait_ge(sems["sv"], mr_done)
                    if m >= 2:
                        e.wait_ge(sems["sv"], ev[m - 2])
                    e.matmul(pp[:, 0:Q], br_rb.ap()[0:1, m * 128:(m + 1) * 128],
                             ones_r.ap()[0:1, 0:Q], start=True, stop=False)
                    last = None
                    for k in range(CCH):
                        last = e.matmul(pp[:, 0:Q],
                                        wrT_sb.ap()[:, k, m * 128:(m + 1) * 128],
                                        mixb.ap()[:, k, :], start=False,
                                        stop=(k == CCH - 1))
                    inc(last, "st")
                block.tensor(f)
                stv = cnt["st"]

                def f(e, m=m, pp=pp, stv=stv, base=base):
                    e.wait_ge(sems["st"], stv)
                    e.tensor_copy(rfm.ap()[:, m, :], pp[:, 0:Q])
                    inc(e.tensor_tensor(rppv(m, base, base + Q), pp[:, 0:Q],
                                        wbB.ap()[:, m * Q:(m + 1) * Q], ALU.mult),
                        "sv")
                block.vector(f)
                ev[m] = cnt["sv"]

            mk_done = mix("k", mixk, PCST["stats_done"])
            ev = {}
            for m in range(CCH):
                pp = ps_p0 if m % 2 == 0 else ps_p1

                def f(e, m=m, pp=pp, mk_done=mk_done):
                    e.wait_ge(sems["sv"], mk_done)
                    if m >= 2:
                        e.wait_ge(sems["sv"], ev[m - 2])
                    e.matmul(pp[:, 0:Q], bk_rb.ap()[0:1, m * 128:(m + 1) * 128],
                             ones_r.ap()[0:1, 0:Q], start=True, stop=False)
                    last = None
                    for k in range(CCH):
                        last = e.matmul(pp[:, 0:Q],
                                        wkT_sb.ap()[:, k, m * 128:(m + 1) * 128],
                                        mixk[:, k, :], start=False,
                                        stop=(k == CCH - 1))
                    inc(last, "st")
                block.tensor(f)
                stv = cnt["st"]

                def f(e, m=m, pp=pp, stv=stv):
                    e.wait_ge(sems["st"], stv)
                    inc(e.tensor_copy(kfm.ap()[:, m, :], pp[:, 0:Q]), "sv")
                block.vector(f)
                ev[m] = cnt["sv"]
            PCST["mixk_done"] = cnt["st"]

            def f(e):
                W(e, "sv")
                for m in range(CCH):
                    for tb in range(2):
                        inc(e.dma_start(out=ktm.ap()[:, tb, m * 128:(m + 1) * 128],
                                        in_=kfm.ap()[:, m, tb * 128:(tb + 1) * 128],
                                        transpose=True), "dq3", 16)
            block.scalar(f)
            ktr_done = cnt["dq3"]

            def f(e):
                e.wait_ge(sems["dq3"], ktr_done)
                e.tensor_tensor(ktm.ap()[:, 0, :], ktm.ap()[:, 0, :],
                                wkBt.ap()[:, 0:C], ALU.mult)
                inc(e.tensor_tensor(ktm.ap()[:, 1, :], ktm.ap()[:, 1, :],
                                    wkBt.ap()[:, C:2 * C], ALU.mult), "sv")
            block.vector(f)

            mv_done = mix("v", mixb.ap(), cnt["st"])
            ev = {}
            for i4 in range(4):
                tb, nh = divmod(i4, 2)
                pp = ps_p0 if nh == 0 else ps_p1

                def f(e, i4=i4, tb=tb, nh=nh, pp=pp, mv_done=mv_done):
                    e.wait_ge(sems["sv"], mv_done)
                    if i4 >= 2:
                        e.wait_ge(sems["sv"], ev[i4 - 2])
                    e.matmul(pp[:], ones_r.ap()[0:1, 0:128],
                             bv_rb.ap()[:, nh * 512:(nh + 1) * 512],
                             start=True, stop=False)
                    last = None
                    for k in range(CCH):
                        last = e.matmul(pp[:], mixb.ap()[:, k, tb * 128:(tb + 1) * 128],
                                        wvT_sb.ap()[:, k, nh * 512:(nh + 1) * 512],
                                        start=False, stop=(k == CCH - 1))
                    inc(last, "st")
                block.tensor(f)
                stv = cnt["st"]

                def f(e, tb=tb, nh=nh, pp=pp, stv=stv):
                    e.wait_ge(sems["st"], stv)
                    inc(e.tensor_copy(vtm.ap()[:, tb, nh * 512:(nh + 1) * 512],
                                      pp[:]), "sv")
                block.vector(f)
                ev[i4] = cnt["sv"]
            vev_done = cnt["sv"]
            PCST["mixb_free"] = cnt["st"]

            if c == 0:
                dump("r0", rfm[:], BF16)
                dump("k0", kfm[:], BF16)
                dump("v0", vtm[:], BF16)
                dump("kw0", ktm[:], BF16)
                dump("rpp0", rppv(0, 0, Q), BF16)

            aev = {}
            yev = {}
            stY_prev = [0]
            lue = [0]
            for hp in range(CCH):
                hA, hB = 2 * hp, 2 * hp + 1
                pa0 = ps_a0 if hp % 2 == 0 else ps_s1.bitcast(F32)[:, 0:Q]
                pa1 = ps_a1 if hp % 2 == 0 else ps_s2.bitcast(F32)[:, 0:128]

                def f(e, hp=hp, pa0=pa0, pa1=pa1, vev_done=vev_done):
                    e.wait_ge(sems["sv"], vev_done)
                    if hp >= 2:
                        e.wait_ge(sems["sv"], aev[hp - 2])
                    e.matmul(pa0[:, 0:Q], kfm.ap()[0:64, hp, 0:128],
                             rfm.ap()[0:64, hp, :], start=True, stop=True)
                    inc(e.matmul(pa1[:, 0:128], kfm.ap()[0:64, hp, 128:Q],
                                 rfm.ap()[0:64, hp, 128:Q], start=True, stop=True),
                        "st")
                block.tensor(f)
                stA = cnt["st"]

                def f(e, hp=hp, hA=hA, pa0=pa0, pa1=pa1, stA=stA):
                    e.wait_ge(sems["st"], stA)
                    if hp >= 1:
                        e.wait_ge(sems["st"], stY_prev[0])
                    e.tensor_tensor(Asb.ap()[:, 0, 0:Q], pa0[:, 0:Q],
                                    masks0.ap()[:, hA * Q:(hA + 1) * Q], ALU.mult)
                    inc(e.tensor_tensor(Asb.ap()[:, 0, Q:Q + 128], pa1[:, 0:128],
                                        masks1.ap()[:, hA * 128:(hA + 1) * 128],
                                        ALU.mult), "sv")
                block.vector(f)
                evA = cnt["sv"]

                def f(e, hp=hp, pa0=pa0, pa1=pa1, evA=evA):
                    e.wait_ge(sems["sv"], evA)
                    e.matmul(pa0[:, 0:Q], kfm.ap()[64:128, hp, 0:128],
                             rfm.ap()[64:128, hp, :], start=True, stop=True)
                    inc(e.matmul(pa1[:, 0:128], kfm.ap()[64:128, hp, 128:Q],
                                 rfm.ap()[64:128, hp, 128:Q], start=True, stop=True),
                        "st")
                block.tensor(f)
                stB = cnt["st"]

                def f(e, hp=hp, hB=hB, pa0=pa0, pa1=pa1, stB=stB):
                    e.wait_ge(sems["st"], stB)
                    e.tensor_tensor(Asb.ap()[:, 1, 0:Q], pa0[:, 0:Q],
                                    masks0.ap()[:, hB * Q:(hB + 1) * Q], ALU.mult)
                    inc(e.tensor_tensor(Asb.ap()[:, 1, Q:Q + 128], pa1[:, 0:128],
                                        masks1.ap()[:, hB * 128:(hB + 1) * 128],
                                        ALU.mult), "sv")
                block.vector(f)
                aev[hp] = cnt["sv"]

                def f(e, hp=hp, hA=hA, hB=hB, base=base):
                    e.wait_ge(sems["sv"], aev[hp])
                    if hp >= 1:
                        e.wait_ge(sems["sv"], yev[hp - 1])
                    last = None
                    for hh, head in ((0, hA), (1, hB)):
                        rows = slice(hh * 64, hh * 64 + 64)
                        e.matmul(ps_y[rows, 0:Q], vtm.ap()[:, 0, head * N:(head + 1) * N],
                                 Asb.ap()[:, hh, 0:Q], start=True, stop=False)
                        e.matmul(ps_y[rows, 128:Q], vtm.ap()[:, 1, head * N:(head + 1) * N],
                                 Asb.ap()[:, hh, Q:Q + 128], start=False, stop=False)
                        last = e.matmul(ps_y[rows, 0:Q], Lb.ap()[rows, hp, :],
                                        rppv(hp, base, base + Q)[rows, :],
                                        start=False, stop=True)
                    inc(last, "st")
                block.tensor(f)
                stY = cnt["st"]
                stY_prev[0] = stY

                def f(e, hp=hp, base=base, stY=stY):
                    e.wait_ge(sems["st"], stY)
                    inc(e.tensor_copy(ypv(hp, base, base + Q), ps_y[:]), "sv")
                block.vector(f)
                yev[hp] = cnt["sv"]

                def f(e, hp=hp, hA=hA, hB=hB, stY=stY, lue_v=lue[0]):
                    e.wait_ge(sems["sv"], yev[hp])
                    if lue_v:
                        e.wait_ge(sems["sv"], lue_v)
                    last = None
                    for hh, head in ((0, hA), (1, hB)):
                        rows = slice(hh * 64, hh * 64 + 64)
                        for tb in range(2):
                            last = e.matmul(ps_d[rows, :],
                                            ktm.ap()[:, tb, head * N:(head + 1) * N],
                                            vtm.ap()[:, tb, head * N:(head + 1) * N],
                                            start=(tb == 0), stop=(tb == 1))
                    inc(last, "st")
                block.tensor(f)
                stD = cnt["st"]

                def f(e, hp=hp, c=c, stD=stD):
                    e.wait_ge(sems["st"], stD)
                    inc(e.scalar_tensor_tensor(lcv(c + 1, hp), lcv(c, hp),
                                               wspow.ap()[:, hp, 1:2], ps_d[:],
                                               ALU.mult, ALU.add), "sv")
                block.vector(f)
                lue[0] = cnt["sv"]

            def f(e, c=c):
                W(e, "sv")
                inc(e.tensor_copy(Lb[:],
                                  lcv(c + 1).rearrange("p (h n) -> p h n", h=CCH)),
                    "sv")
            block.vector(f)

            if c == 0:
                dump("yp0", ypv(0, 0, Q), BF16)
                dump("L1", lcv(1))

        last = NCH
        if stop_after and str(stop_after).startswith("p1c"):
            last = int(str(stop_after)[3:]) + 1
        for c in range(last):
            p1_chunk(c)

        dump("Lfin", lcv(NCH))
        dump("yp_full", yp_f[:], BF16)
        if stop_after and str(stop_after).startswith("p1"):
            done = True

        def f(e):
            W(e, "sv", "st")
            inc(e.dma_start(out=dsp_in, in_=lcv(last)), "dq", 16)
        block.sync(f)

        barrier()
        ps1.close()
    estA.close()

    if not done:
        estP2 = contextlib.ExitStack()
        Dg = SB(estP2, "Dg", [128, NCORES, CCH, N], F32)
        S0 = SB(estP2, "S0", [128, CCH, N], F32)
        S0cb = SB(estP2, "S0cb", [128, CCH, N], BF16)
        psP2 = contextlib.ExitStack()
        ps_c0 = PS(psP2, "ps_c0", [128, Q])
        ps_c1 = PS(psP2, "ps_c1", [128, Q])

        def f(e):
            inc(e.collective_compute("AllGather", ALU.bypass,
                                     replica_groups=[list(range(NCORES))],
                                     ins=[dsp_in], outs=[dsp_out]), "cc", 1)
            W(e, "cc")
            e.memset(S0[:], 0.0)
            inc(e.drain(), "sg")
        block.gpsimd(f)

        def f(e):
            W(e, "cc")
            for j in range(NCORES):
                inc(e.dma_start(out=Dg.ap()[:, j, :, :],
                                in_=dsp_out[128 * j:128 * (j + 1), :]
                                .rearrange("p (h n) -> p h n", h=CCH)), "dq", 16)
        block.sync(f)

        def f(e):
            W(e, "dq", "sg")
            last = None
            for j in range(NCORES):
                for hp in range(CCH):
                    last = e.scalar_tensor_tensor(S0.ap()[:, hp, :], Dg.ap()[:, j, hp, :],
                                                  s0w.ap()[:, hp, j:j + 1],
                                                  S0.ap()[:, hp, :], ALU.mult, ALU.add)
            inc(last, "sv")
        block.vector(f)

        dump("S0", S0[:])

        fixev = {}
        it2 = 0
        for c in range(NCH):
            def f(e, c=c):
                W(e, "st")
                if c == 0:
                    W(e, "sv")
                last = None
                for hp in range(CCH):
                    last = e.tensor_scalar_mul(S0cb.ap()[:, hp, :], S0.ap()[:, hp, :],
                                               wspow.ap()[:, hp, c:c + 1])
                inc(last, "sv")
            block.vector(f)
            scb_done = cnt["sv"]

            for hp in range(CCH):
                pc_ = ps_c0 if it2 % 2 == 0 else ps_c1

                def f(e, c=c, hp=hp, pc_=pc_, it2=it2, scb_done=scb_done):
                    e.wait_ge(sems["sv"], scb_done)
                    if it2 >= 2:
                        e.wait_ge(sems["sv"], fixev[it2 - 2])
                    e.matmul(pc_[0:64, :], S0cb.ap()[0:64, hp, :],
                             rppv(hp, c * Q, (c + 1) * Q)[0:64, :],
                             start=True, stop=True)
                    inc(e.matmul(pc_[64:128, :], S0cb.ap()[64:128, hp, :],
                                 rppv(hp, c * Q, (c + 1) * Q)[64:128, :],
                                 start=True, stop=True), "st")
                block.tensor(f)
                stv = cnt["st"]

                def f(e, c=c, hp=hp, pc_=pc_, stv=stv):
                    e.wait_ge(sems["st"], stv)
                    inc(e.tensor_tensor(ypv(hp, c * Q, (c + 1) * Q),
                                        ypv(hp, c * Q, (c + 1) * Q), pc_[:], ALU.add),
                        "sv")
                block.vector(f)
                fixev[it2] = cnt["sv"]
                it2 += 1

        dump("y_fix", yp_f[:], BF16)
        barrier()
        psP2.close()
        estP2.close()
        estB.close()
        if stop_after == "p2":
            done = True


    if not done:
        estP3 = contextlib.ExitStack()
        woT_sb = SB(estP3, "woT_sb", [128, CCH, C], BF16)
        x3 = SB(estP3, "x3", [128, CCH, 256], F32)
        x2t = SB(estP3, "x2t", [128, CCH, 256], F32)
        xb2 = SB(estP3, "xb2", [128, CCH, 256], BF16)
        xq2sq = SB(estP3, "xq2sq", [128, CCH, 256], BF16)
        h2b = SB(estP3, "h2b", [128, CCH, 256], F32)
        st2m = SB(estP3, "st2m", [128, 256], F32)
        st2v = SB(estP3, "st2v", [128, 256], F32)
        stw3 = SB(estP3, "stw3", [128, 256], F32)
        macc = SB(estP3, "macc", [128, 256], F32)
        sc_tm = SB(estP3, "sc_tm", [128, 4], F32)
        sbc = SB(estP3, "sbc", [128, 256], F32)
        cbc = SB(estP3, "cbc", [128, 256], F32)
        i32q = SB(estP3, "i32q", [128, 256], I32)
        fq = SB(estP3, "fq", [128, 256], F32)
        xqt3 = SB(estP3, "xqt3", [128, CCH, 256], BF16)
        wqs = SB(estP3, "wqs", [128, HID], F32)
        wqi = SB(estP3, "wqi", [128, HID], I32)
        wqo = SB(estP3, "wqo", [128, HID], BF16)

        wq_jobs = [(w1T, kk, HID, w1q_d, 0) for kk in range(CCH)] + \
                  [(w2T, kk, C, w2q_d, 1) for kk in range(HCH)]
        wq_state = {"i": 0, "g": 0}

        def emit_wq(n):
            for _ in range(n):
                if wq_state["i"] >= len(wq_jobs):
                    return
                srcw, kk, width, dst, rcol = wq_jobs[wq_state["i"]]
                wq_state["i"] += 1
                view = srcw.rearrange("(k p) m -> p k m", p=128)

                def f(e, kk=kk, view=view, width=width):
                    if wq_state["g"]:
                        e.wait_ge(sems["sv"], wq_state["g"])
                    inc(e.dma_start(out=wqs.ap()[:, 0:width], in_=view[:, kk, :]),
                        "dq2", 16)
                block.gpsimd(f)
                ld = cnt["dq2"]

                def f(e, ld=ld, width=width, rcol=rcol):
                    e.wait_ge(sems["dq2"], ld)
                    if wq_state.get("st3"):
                        e.wait_ge(sems["dq3"], wq_state["st3"])
                    e.tensor_scalar(wqs.ap()[:, 0:width], wqs.ap()[:, 0:width],
                                    rwb.ap()[:, rcol:rcol + 1], 1.49,
                                    ALU.mult, ALU.min)
                    e.tensor_scalar_max(wqs.ap()[:, 0:width], wqs.ap()[:, 0:width],
                                        -1.49)
                    e.tensor_copy(wqi.ap()[:, 0:width], wqs.ap()[:, 0:width])
                    inc(e.tensor_copy(wqo.ap()[:, 0:width], wqi.ap()[:, 0:width]),
                        "sv")
                block.vector(f)
                wq_state["g"] = cnt["sv"]

                def f(e, kk=kk, dst=dst, width=width):
                    W(e, "sv")
                    inc(e.dma_start(out=dst[kk, :, :], in_=wqo.ap()[:, 0:width]),
                        "dq3", 16)
                block.scalar(f)
                wq_state["st3"] = cnt["dq3"]

        psP3 = contextlib.ExitStack()
        ps_o0 = PS(psP3, "ps_o0", [128, 512])[:, 0:256]
        ps_o1 = PS(psP3, "ps_o1", [128, 512])[:, 0:256]
        ps_s1b = PS(psP3, "ps_s1b", [128, 512])[:, 0:256]
        ps_s2b = PS(psP3, "ps_s2b", [128, 512])[:, 0:256]
        ps_tr3 = PS(psP3, "ps_tr3", [128, 512])[:, 0:256]
        ps_bs = PS(psP3, "ps_bs", [128, 512])[:, 0:256]
        ps_bc1 = PS(psP3, "ps_bc1", [128, 512])[:, 0:256]

        def f(e):
            inc(e.dma_start(out=woT_sb.ap(),
                            in_=woT.rearrange("(k p) o -> p k o", p=128)), "dq2", 16)
        block.gpsimd(f)

        qdone3 = [0]

        def p3_tile(t):
            lo = t * 256
            emit_wq(5)

            def f(e):
                W(e, "sv", "ss")
                inc(e.dma_start(out=x3.ap(),
                                in_=xs[:, 1 + lo:1 + lo + 256]
                                .rearrange("(k p) t -> p k t", p=128)), "dq", 16)
            block.sync(f)

            ev = {}
            for m in range(CCH):
                pp = ps_o0 if m % 2 == 0 else ps_o1

                def f(e, m=m, pp=pp):
                    if m < 2:
                        W(e, "dq2", "sv")
                    else:
                        e.wait_ge(sems["sv"], ev[m - 2])
                    e.matmul(pp[:], bo_rb.ap()[0:1, m * 128:(m + 1) * 128],
                             ones_r.ap()[0:1, 0:256], start=True, stop=False)
                    last = None
                    for k in range(CCH):
                        last = e.matmul(pp[:], woT_sb.ap()[:, k, m * 128:(m + 1) * 128],
                                        ypv(k, lo, lo + 256), start=False,
                                        stop=(k == CCH - 1))
                    inc(last, "st")
                block.tensor(f)
                stv = cnt["st"]

                def f(e, m=m, pp=pp, stv=stv):
                    e.wait_ge(sems["st"], stv)
                    W(e, "dq")
                    inc(e.scalar_tensor_tensor(x2t.ap()[:, m, :], pp[:],
                                               V_["ga"][:, m:m + 1], x3.ap()[:, m, :],
                                               ALU.mult, ALU.add), "sv")
                block.vector(f)
                ev[m] = cnt["sv"]

            def f(e, t=t):
                W(e, "sv")
                inc(e.dma_start(out=x2_d[:, :, lo:lo + 256].rearrange("m p t -> p m t"),
                                in_=x2t.ap()), "dq", 16)
            block.sync(f)

            def f(e):
                W(e, "sv")
                inc(e.dma_start(out=xb2.ap(), in_=x2t.ap()), "dq2", 16)
            block.gpsimd(f)

            def f(e):
                W(e, "dq2")
                inc(e.tensor_mul(xq2sq[:], xb2[:], xb2[:]), "sv")
            block.vector(f)

            def f(e):
                W(e, "sv")
                for k in range(CCH):
                    e.matmul(ps_s1b[:], ones_b[:], xb2.ap()[:, k, :],
                             start=(k == 0), stop=(k == CCH - 1))
                last = None
                for k in range(CCH):
                    last = e.matmul(ps_s2b[:], ones_b[:], xq2sq.ap()[:, k, :],
                                    start=(k == 0), stop=(k == CCH - 1))
                inc(last, "st")
            block.tensor(f)

            def f(e):
                W(e, "st")
                e.tensor_scalar_mul(st2m[:], ps_s1b[:], INV_C)
                e.tensor_mul(st2v[:], st2m[:], st2m[:])
                e.tensor_scalar_sub(st2v[:], st2v[:], EPS)
                inc(e.scalar_tensor_tensor(st2v[:], ps_s2b[:], INV_C, st2v[:],
                                           ALU.mult, ALU.subtract), "sv")
            block.vector(f)

            def f(e):
                W(e, "sv")
                bc_m = st2m.ap()[:, None, :].broadcast_to([128, CCH, 256])
                e.tensor_tensor(x3[:], x2t[:], bc_m, ALU.subtract)
                e.tensor_single_scalar(stw3.ap().bitcast(I32)[:, 0:256], st2v.ap().bitcast(I32)[:, 0:256], 1,
                                       ALU.arith_shift_right)
                e.tensor_sub(stw3.ap().bitcast(I32)[:, 0:256], magicT.ap()[:, 0:256], stw3.ap().bitcast(I32)[:, 0:256])
                for _nit in range(2):
                    e.tensor_mul(st2m[:], st2v[:], stw3[:])
                    e.tensor_mul(st2m[:], st2m[:], stw3[:])
                    e.tensor_scalar(st2m[:], st2m[:], -0.5, 1.5, ALU.mult, ALU.add)
                    e.tensor_mul(stw3[:], stw3[:], st2m[:])
                bc_v = stw3.ap()[:, None, :].broadcast_to([128, CCH, 256])
                inc(e.tensor_tensor(x3[:], x3[:], bc_v, ALU.mult), "sv")
            block.vector(f)

            def f(e):
                W(e, "sv")
                last = None
                for j in range(CCH):
                    last = e.tensor_scalar(h2b.ap()[:, j, :], x3.ap()[:, j, :],
                                           V_["cb1"][:, j:j + 1],
                                           V_["sbv"][:, j:j + 1], ALU.mult, ALU.add)
                inc(last, "sv")
            block.vector(f)

            def f(e):
                W(e, "sv")
                inc(e.tensor_single_scalar(x3.ap().bitcast(I32)[:, :, :],
                                           h2b.ap().bitcast(I32)[:, :, :],
                                           0x7FFFFFFF, ALU.bitwise_and), "sv")
            block.vector(f)

            def f(e):
                W(e, "sv")
                e.tensor_copy(macc[:], x3.ap()[:, 0, :])
                last = None
                for j in range(1, CCH):
                    last = e.tensor_tensor(macc[:], macc[:], x3.ap()[:, j, :], ALU.max)
                inc(last, "sv")
            block.vector(f)

            def f(e):
                W(e, "sv")
                e.transpose(ps_tr3[:, 0:128].bitcast(F32), macc[:, 0:128], ident_f[:])
                inc(e.transpose(ps_tr3[:, 128:256].bitcast(F32), macc[:, 128:256],
                                ident_f[:]), "st")
            block.tensor(f)

            def f(e):
                W(e, "st")
                for hh in range(2):
                    e.reduce_max(sc_tm.ap()[:, hh:hh + 1],
                                 ps_tr3[:, hh * 128:(hh + 1) * 128], AX,
                                 apply_absolute_value=True)
                selfw(e, "sv")
                e.tensor_scalar_max(sc_tm.ap()[:, 0:2], sc_tm.ap()[:, 0:2], 1e-5)
                e.reciprocal(sc_tm.ap()[:, 2:4], sc_tm.ap()[:, 0:2])
                e.tensor_scalar(sc_tm.ap()[:, 0:2], sc_tm.ap()[:, 0:2],
                                mwb.ap()[:, 0:1], 1.0 / 127.0, ALU.mult, ALU.mult)
                inc(e.tensor_scalar_mul(sc_tm.ap()[:, 2:4], sc_tm.ap()[:, 2:4], 127.0),
                    "sv")
            block.vector(f)

            def f(e):
                W(e, "sv")
                last = None
                for hh in range(2):
                    e.tensor_copy(sbc[:, hh * 128:(hh + 1) * 128],
                                  sc_tm.ap()[:, 2 + hh:3 + hh].broadcast_to([128, 128]))
                    last = e.tensor_copy(cbc[:, hh * 128:(hh + 1) * 128],
                                         sc_tm.ap()[:, hh:hh + 1].broadcast_to([128, 128]))
                inc(last, "sv")
            block.vector(f)

            def f(e):
                W(e, "sv")
                e.wait_ge(sems["sv"], qdone3[0])
                last = None
                for hh in range(2):
                    e.matmul(ps_bs[:, hh * 128:(hh + 1) * 128],
                             sbc[:, hh * 128:(hh + 1) * 128], ident_f[:],
                             start=True, stop=True)
                    last = e.matmul(ps_bc1[:, hh * 128:(hh + 1) * 128],
                                    cbc[:, hh * 128:(hh + 1) * 128], ident_f[:],
                                    start=True, stop=True)
                inc(last, "st")
            block.tensor(f)

            if t == 7:
                dump("macc3", macc[:])
                dump("sctm3", sc_tm[:])
                dump("h2b3", h2b[:])

            def f(e, t=t):
                W(e, "st")
                last = None
                for j in range(CCH):
                    e.tensor_tensor(h2b.ap()[:, j, :], h2b.ap()[:, j, :], ps_bs[:],
                                    ALU.mult)
                    e.tensor_copy(i32q[:], h2b.ap()[:, j, :])
                    e.tensor_copy(fq[:], i32q[:])
                    last = e.tensor_tensor(xqt3.ap()[:, j, :], fq[:], ps_bc1[:],
                                           ALU.mult)
                inc(last, "sv")
            block.vector(f)
            qdone3[0] = cnt["sv"]

            def f(e, lo=lo):
                W(e, "sv")
                for j in range(CCH):
                    inc(e.dma_start(out=xqs_d[j, :, lo:lo + 256],
                                    in_=xqt3.ap()[:, j, :]), "dq", 16)
            block.sync(f)

        for t in range(TPC // 256):
            p3_tile(t)
        emit_wq(len(wq_jobs))

        dump("xqt3", xqt3[:], BF16)
        dump("x2t7", x2t[:])
        barrier()
        psP3.close()
        estP3.close()
        estC.close()
        if stop_after == "p3":
            done = True

    if not done:
        estP4 = contextlib.ExitStack()
        w1q_sb = SB(estP4, "w1q_sb", [128, CCH, HID], BF16)
        xq4 = SB(estP4, "xq4", [128, CCH, 258], BF16)
        xqcol = SB(estP4, "xqcol", [128, CCH, 2], BF16)
        e4 = SB(estP4, "e4", [128, HCH, 258], F32)
        z4 = SB(estP4, "z4", [128, HCH, 256], F32)
        xqh_t = SB(estP4, "xqh_t", [128, HCH, 256], BF16)
        halo_sb = SB(estP4, "halo_sb", [128, 2 * HCH], F32)
        haloG = SB(estP4, "haloG", [128, NCORES, HCH, 2], F32)
        i32h = SB(estP4, "i32h", [128, 256], I32)
        macc4 = SB(estP4, "macc4", [128, 256], F32)
        sc4 = SB(estP4, "sc4", [128, 4], F32)
        sbc4 = SB(estP4, "sbc4", [128, 256], F32)

        psP4 = contextlib.ExitStack()
        ps_w0 = PS(psP4, "ps_w0", [128, 512])[:, 0:258]
        ps_w1 = PS(psP4, "ps_w1", [128, 512])[:, 0:258]
        ps_tr4 = PS(psP4, "ps_tr4", [128, 512])[:, 0:256]
        ps_bs4 = PS(psP4, "ps_bs4", [128, 512])[:, 0:256]
        ps_e = PS(psP4, "ps_e", [128, 512])[:, 0:2 * HCH]

        def f(e):
            W(e, "dq3")
            for k in range(CCH):
                inc(e.dma_start(out=w1q_sb.ap()[:, k, :], in_=w1q_d[k, :, :]), "dq", 16)
                inc(e.dma_start(out=xqcol.ap()[:, k, :],
                                in_=xqs_d[k, :, 0:TPC:TPC - 1],
                                allow_slow_non_contiguous=True), "dq", 16)
        block.sync(f)

        def f(e):
            W(e, "dq", "sv")
            last = None
            for hc in range(HCH):
                for k in range(CCH):
                    last = e.matmul(ps_e[:, 2 * hc:2 * hc + 2],
                                    w1q_sb.ap()[:, k, hc * 128:(hc + 1) * 128],
                                    xqcol.ap()[:, k, :],
                                    start=(k == 0), stop=(k == CCH - 1))
            inc(last, "st")
        block.tensor(f)

        def f(e):
            W(e, "st")
            inc(e.tensor_copy(halo_sb[:], ps_e[:]), "sv")
        block.vector(f)

        def f(e):
            W(e, "sv")
            inc(e.dma_start(out=halo_in, in_=halo_sb[:]), "dq", 16)
        block.sync(f)

        def f(e):
            W(e, "dq")
            inc(e.collective_compute("AllGather", ALU.bypass,
                                     replica_groups=[list(range(NCORES))],
                                     ins=[halo_in], outs=[halo_out]), "cc", 1)
            W(e, "cc")
        block.gpsimd(f)

        def f(e):
            W(e, "cc")
            for j in range(NCORES):
                inc(e.dma_start(out=haloG.ap()[:, j, :, :],
                                in_=halo_out[128 * j:128 * (j + 1), :]
                                .rearrange("p (h w) -> p h w", w=2)), "dq", 16)
        block.sync(f)

        def f(e):
            W(e, "dq")
            e.tensor_scalar_mul(eLh[:], haloG.ap()[:, 0, :, 1], selLb.ap()[:, 0:1])
            e.tensor_scalar_mul(eRh[:], haloG.ap()[:, 0, :, 0], selRb.ap()[:, 0:1])
            last = None
            for j in range(1, NCORES):
                e.scalar_tensor_tensor(eLh[:], haloG.ap()[:, j, :, 1],
                                       selLb.ap()[:, j:j + 1], eLh[:],
                                       ALU.mult, ALU.add)
                last = e.scalar_tensor_tensor(eRh[:], haloG.ap()[:, j, :, 0],
                                              selRb.ap()[:, j:j + 1], eRh[:],
                                              ALU.mult, ALU.add)
            inc(last, "sv")
        block.vector(f)

        dump("eLh", eLh[:])

        NT4 = TPC // 256
        qdone4 = [0]

        def p4_tile(t):
            lo = t * 256
            wlo = lo - 1 if t > 0 else lo
            whi = lo + 257 if t < NT4 - 1 else lo + 256
            plo = 0 if t > 0 else 1

            def f(e, wlo=wlo, whi=whi):
                W(e, "sv", "ss")
                for k in range(CCH):
                    inc(e.dma_start(out=xq4.ap()[:, k, 0:whi - wlo],
                                    in_=xqs_d[k, :, wlo:whi]), "dq", 16)
            block.sync(f)

            ev = {}
            for hc in range(HCH):
                pp = ps_w0 if hc % 2 == 0 else ps_w1

                def f(e, hc=hc, pp=pp, wlo=wlo, whi=whi, plo=plo):
                    if hc < 2:
                        W(e, "dq", "sv", "ss")
                    else:
                        e.wait_ge(sems["sv"], ev[hc - 2])
                    last = None
                    for k in range(CCH):
                        last = e.matmul(pp[:, plo:plo + (whi - wlo)],
                                        w1q_sb.ap()[:, k, hc * 128:(hc + 1) * 128],
                                        xq4.ap()[:, k, 0:whi - wlo],
                                        start=(k == 0), stop=(k == CCH - 1))
                    inc(last, "st")
                block.tensor(f)
                stv = cnt["st"]

                def f(e, hc=hc, pp=pp, stv=stv, plo=plo, wlo=wlo, whi=whi):
                    e.wait_ge(sems["st"], stv)
                    inc(e.tensor_copy(e4.ap()[:, hc, plo:plo + (whi - wlo)],
                                      pp[:, plo:plo + (whi - wlo)]), "sv")
                block.vector(f)
                ev[hc] = cnt["sv"]

            if t == 0:
                def f(e):
                    W(e, "sv")
                    inc(e.tensor_copy(e4.ap()[:, :, 0], eLh[:]), "sv")
                block.vector(f)
            if t == NT4 - 1:
                def f(e):
                    W(e, "sv")
                    inc(e.tensor_copy(e4.ap()[:, :, 257], eRh[:]), "sv")
                block.vector(f)

            def f(e):
                W(e, "ss", "sv")
                for hc in range(HCH):
                    e.tensor_scalar(z4.ap()[:, hc, :], e4.ap()[:, hc, 1:257],
                                    CV["w1"][:, hc:hc + 1], CV["b"][:, hc:hc + 1],
                                    ALU.mult, ALU.add)
                last = None
                for hc in range(HCH):
                    e.scalar_tensor_tensor(z4.ap()[:, hc, :], e4.ap()[:, hc, 0:256],
                                           CV["w0"][:, hc:hc + 1], z4.ap()[:, hc, :],
                                           ALU.mult, ALU.add)
                    last = e.scalar_tensor_tensor(z4.ap()[:, hc, :],
                                                  e4.ap()[:, hc, 2:258],
                                                  CV["w2"][:, hc:hc + 1],
                                                  z4.ap()[:, hc, :], ALU.mult, ALU.add)
                inc(last, "sv")
            block.vector(f)

            def f(e):
                W(e, "sv")
                last = None
                for hc in range(HCH):
                    last = e.activation(e4.ap()[:, hc, 0:256], z4.ap()[:, hc, :],
                                        AF.Silu)
                inc(last, "ss")
            block.scalar(f)

            def f(e):
                W(e, "ss", "sv", "sg")
                inc(e.tensor_single_scalar(z4.ap().bitcast(I32)[:, :, :],
                                           e4.ap().bitcast(I32)[:, :, 0:256],
                                           0x7FFFFFFF, ALU.bitwise_and), "sv")
            block.vector(f)

            def f(e):
                W(e, "sv")
                e.tensor_copy(macc4[:], z4.ap()[:, 0, :])
                last = None
                for hc in range(1, HCH):
                    last = e.tensor_tensor(macc4[:], macc4[:], z4.ap()[:, hc, :],
                                           ALU.max)
                inc(last, "sv")
            block.vector(f)

            def f(e):
                W(e, "sv")
                e.transpose(ps_tr4[:, 0:128].bitcast(F32), macc4[:, 0:128], ident_f[:])
                inc(e.transpose(ps_tr4[:, 128:256].bitcast(F32), macc4[:, 128:256],
                                ident_f[:]), "st")
            block.tensor(f)

            def f(e, t=t):
                W(e, "st")
                for hh in range(2):
                    e.reduce_max(sc4.ap()[:, hh:hh + 1],
                                 ps_tr4[:, hh * 128:(hh + 1) * 128], AX,
                                 apply_absolute_value=True)
                selfw(e, "sv")
                e.tensor_scalar_max(sc4.ap()[:, 0:2], sc4.ap()[:, 0:2], 1e-5)
                e.reciprocal(sc4.ap()[:, 2:4], sc4.ap()[:, 0:2])
                e.tensor_scalar(c2row.ap()[:, 2 * t:2 * t + 2], sc4.ap()[:, 0:2],
                                mwb.ap()[:, 1:2], 1.0 / 127.0, ALU.mult, ALU.mult)
                inc(e.tensor_scalar_mul(sc4.ap()[:, 2:4], sc4.ap()[:, 2:4], 127.0),
                    "sv")
            block.vector(f)

            def f(e):
                W(e, "sv")
                e.tensor_copy(sbc4[:, 0:128], sc4.ap()[:, 2:3].broadcast_to([128, 128]))
                inc(e.tensor_copy(sbc4[:, 128:256],
                                  sc4.ap()[:, 3:4].broadcast_to([128, 128])), "sv")
            block.vector(f)

            def f(e):
                W(e, "sv")
                e.wait_ge(sems["sv"], qdone4[0])
                e.matmul(ps_bs4[:, 0:128], sbc4[:, 0:128], ident_f[:],
                         start=True, stop=True)
                inc(e.matmul(ps_bs4[:, 128:256], sbc4[:, 128:256], ident_f[:],
                             start=True, stop=True), "st")
            block.tensor(f)

            def f(e, t=t):
                W(e, "st")
                last = None
                for hc in range(HCH):
                    e.tensor_tensor(z4.ap()[:, hc, :], e4.ap()[:, hc, 0:256],
                                    ps_bs4[:], ALU.mult)
                    e.tensor_copy(i32h[:], z4.ap()[:, hc, :])
                    last = e.tensor_copy(xqh_t.ap()[:, hc, :], i32h[:])
                inc(last, "sv")
            block.vector(f)
            qdone4[0] = cnt["sv"]

            def f(e, t=t, lo=lo):
                W(e, "sv")
                inc(e.dma_start(out=xqh_d[:, :, lo:lo + 256].rearrange("h p t -> p h t"),
                                in_=xqh_t.ap()), "dq", 16)
            block.sync(f)

        for t in range(NT4):
            p4_tile(t)

        barrier()
        psP4.close()
        estP4.close()
        if stop_after == "p4":
            done = True

    if not done:
        estP5 = contextlib.ExitStack()
        w2q_sb = SB(estP5, "w2q_sb", [128, HCH, C], BF16)
        xqh5 = SB(estP5, "xqh5", [128, HCH, 256], BF16)
        x25 = SB(estP5, "x25", [128, CCH, 256], F32)
        out5 = SB(estP5, "out5", [128, CCH, 256], F32)
        c2sb = SB(estP5, "c2sb", [128, 256], F32)
        sbc5 = SB(estP5, "sbc5", [128, 256], F32)
        fq5 = SB(estP5, "fq5", [128, 256], F32)

        psP5 = contextlib.ExitStack()
        ps_f0 = PS(psP5, "ps_f0", [128, 512])[:, 0:256]
        ps_f1 = PS(psP5, "ps_f1", [128, 512])[:, 0:256]
        ps_bc2 = PS(psP5, "ps_bc2", [128, 512])[:, 0:256]

        def f(e):
            for hc in range(HCH):
                inc(e.dma_start(out=w2q_sb.ap()[:, hc, :], in_=w2q_d[hc, :, :]),
                    "dq2", 16)
        block.gpsimd(f)

        def p5_tile(t):
            lo = t * 256

            def f(e, t=t, lo=lo):
                W(e, "sv")
                inc(e.dma_start(out=xqh5.ap(),
                                in_=xqh_d[:, :, lo:lo + 256].rearrange("h p t -> p h t")),
                    "dq", 16)
                inc(e.dma_start(out=x25.ap(),
                                in_=x2_d[:, :, lo:lo + 256].rearrange("m p t -> p m t")),
                    "dq", 16)
            block.sync(f)

            def f(e, t=t):
                W(e, "sv")
                e.tensor_copy(sbc5[:, 0:128],
                              c2row.ap()[:, 2 * t:2 * t + 1].broadcast_to([128, 128]))
                inc(e.tensor_copy(sbc5[:, 128:256],
                                  c2row.ap()[:, 2 * t + 1:2 * t + 2]
                                  .broadcast_to([128, 128])), "sv")
            block.vector(f)

            def f(e):
                W(e, "sv")
                e.matmul(ps_bc2[:, 0:128], sbc5[:, 0:128], ident_f[:],
                         start=True, stop=True)
                inc(e.matmul(ps_bc2[:, 128:256], sbc5[:, 128:256], ident_f[:],
                             start=True, stop=True), "st")
            block.tensor(f)

            def f(e):
                W(e, "st")
                inc(e.tensor_copy(c2sb[:], ps_bc2[:]), "sv")
            block.vector(f)

            ev = {}
            for m in range(CCH):
                pp = ps_f0 if m % 2 == 0 else ps_f1

                def f(e, m=m, pp=pp):
                    if m < 2:
                        W(e, "dq", "sv")
                    else:
                        e.wait_ge(sems["sv"], ev[m - 2])
                    last = None
                    for hc in range(HCH):
                        last = e.matmul(pp[:], w2q_sb.ap()[:, hc, m * 128:(m + 1) * 128],
                                        xqh5.ap()[:, hc, :], start=(hc == 0),
                                        stop=(hc == HCH - 1))
                    inc(last, "st")
                block.tensor(f)
                stv = cnt["st"]

                def f(e, m=m, pp=pp, stv=stv):
                    e.wait_ge(sems["st"], stv)
                    e.tensor_tensor(fq5[:], pp[:], c2sb[:], ALU.mult)
                    inc(e.scalar_tensor_tensor(out5.ap()[:, m, :], fq5[:],
                                               V_["gb"][:, m:m + 1], x25.ap()[:, m, :],
                                               ALU.mult, ALU.add), "sv")
                block.vector(f)
                ev[m] = cnt["sv"]

            def f(e, lo=lo):
                W(e, "sv")
                for m in range(CCH):
                    inc(e.dma_start(out=outT[128 * m:128 * (m + 1), lo:lo + 256],
                                    in_=out5.ap()[:, m, :]), "so", 16)
            block.sync(f)

        for t in range(TPC // 256):
            p5_tile(t)

        barrier()
        psP5.close()
        estP5.close()

    barrier()
    blkg.__exit__(None, None, None)
    semstack.close()
    estB.close()
    estC.close()
    est0.close()
    return nc, dbg_map



def host_inputs(L0_attn_x, L0_attn_time_maa_k, L0_attn_time_maa_v, L0_attn_time_maa_r,
                L0_attn_time_decay, L0_attn_time_faaaa, L0_attn_Wr, L0_attn_br,
                L0_attn_Wk, L0_attn_bk, L0_attn_Wv, L0_attn_bv, L0_attn_Wo, L0_attn_bo,
                L0_ffn_w1, L0_ffn_conv_w, L0_ffn_conv_b, L0_ffn_w2, L0_c, L0_wmod):
    f32 = np.float32
    xT = np.ascontiguousarray(np.asarray(L0_attn_x, f32)[0].T)

    def fm(v, nch=CCH):
        return np.ascontiguousarray(np.asarray(v, f32).reshape(nch, 128).T)

    cw = np.asarray(L0_ffn_conv_w, f32)
    heads = (np.arange(C) // N).astype(np.int64)
    w1T = np.ascontiguousarray(np.asarray(L0_ffn_w1, f32).T)
    w2T = np.ascontiguousarray(np.asarray(L0_ffn_w2, f32).T)
    common = dict(
        wrT=np.ascontiguousarray(np.asarray(L0_attn_Wr, f32).T),
        wkT=np.ascontiguousarray(np.asarray(L0_attn_Wk, f32).T),
        wvT=np.ascontiguousarray(np.asarray(L0_attn_Wv, f32).T),
        woT=np.ascontiguousarray(np.asarray(L0_attn_Wo, f32).T),
        w1T=w1T, w2T=w2T,
        c_fm=fm(np.asarray(L0_c, f32)[0]),
        maa_k=fm(L0_attn_time_maa_k), maa_v=fm(L0_attn_time_maa_v),
        maa_r=fm(L0_attn_time_maa_r),
        bv_row=np.ascontiguousarray(np.asarray(L0_attn_bv, f32)[None, :]),
        bo_row=np.ascontiguousarray(np.asarray(L0_attn_bo, f32)[None, :]),
        br_row=np.ascontiguousarray(np.asarray(L0_attn_br, f32)[None, :]),
        bk_row=np.ascontiguousarray(np.asarray(L0_attn_bk, f32)[None, :]),
        td_fm=fm(np.asarray(L0_attn_time_decay, f32)[heads]),
        td_hb=np.ascontiguousarray(
            np.repeat(np.asarray(L0_attn_time_decay, f32)[None, :], 128, 0)),
        uf_hb=np.ascontiguousarray(
            np.repeat(np.asarray(L0_attn_time_faaaa, f32)[None, :], 128, 0)),
        cw0=fm(cw[:, 0, 0], HCH), cw1=fm(cw[:, 0, 1], HCH), cw2=fm(cw[:, 0, 2], HCH),
        cb_fm=fm(L0_ffn_conv_b, HCH),
    )
    wmod = np.asarray(L0_wmod, f32)
    in_maps = []
    for i in range(NCORES):
        s, e = i * TPC, (i + 1) * TPC
        xsl = np.empty((C, TPC + 1), f32)
        xsl[:, 1:] = xT[:, s:e]
        xsl[:, 0] = xT[:, s - 1] if i > 0 else xT[:, 0]
        ew = np.full((128, NCORES), 1e9, f32)
        for j in range(i):
            ew[:, j] = float(Q * NCH * (i - 1 - j))
        sl = np.zeros((128, NCORES), f32); sr = np.zeros((128, NCORES), f32)
        if i > 0:
            sl[:, i - 1] = 1.0
        if i < NCORES - 1:
            sr[:, i + 1] = 1.0
        m = dict(common)
        m.update(xs=np.ascontiguousarray(xsl),
                 wmod_s=np.ascontiguousarray(wmod[:, i * MODS:(i + 1) * MODS]),
                 w1s=np.ascontiguousarray(w1T[:, i * W1SH:(i + 1) * W1SH]),
                 w2s=np.ascontiguousarray(w2T[:, i * W2SH:(i + 1) * W2SH]),
                 expw=ew, selL=sl, selR=sr,
                 c0f=np.full((128, 1), 1.0 if i == 0 else 0.0, f32))
        in_maps.append(m)
    return in_maps


def kernel(**inp):
    in_maps = host_inputs(**inp)
    nc, _ = build()
    res = run_bass_kernel_spmd(nc, in_maps, core_ids=list(range(NCORES)))
    outs = [res.results[i]["outT"] for i in range(NCORES)]
    full = np.concatenate(outs, axis=1)
    return np.ascontiguousarray(full.T)[None].astype(np.float32)


# version 17
# speedup vs baseline: 1.1670x; 1.1670x over previous
import contextlib
import numpy as np
import concourse.bass as bass
import concourse.mybir as mybir
from concourse.bass_utils import run_bass_kernel_spmd

F32 = mybir.dt.float32
BF16 = mybir.dt.bfloat16
I32 = mybir.dt.int32
AF = mybir.ActivationFunctionType
ALU = mybir.AluOpType
AX = mybir.AxisListType.X

NCORES = 8
T = 16384
C = 1024
H = 16
N = 64
Q = 256
TPC = T // NCORES
NCH = TPC // Q
CCH = C // 128
HID = 4096
HCH = HID // 128
MODS = 6 * C // NCORES
W1SH = HID // NCORES
W2SH = C // NCORES
EPS = 1e-5
INV_C = 1.0 / C


def build(dbg=(), stop_after=None):
    nc = bass.Bass("TRN2", target_bir_lowering=False)
    dbg = set(dbg)
    dbg_map = {}

    def din(name, shape, dt=F32):
        return nc.dram_tensor(name, shape, dt, kind="ExternalInput").ap()

    xs = din("xs", [C, TPC + 1])
    wrT = din("wrT", [C, C]); wkT = din("wkT", [C, C])
    wvT = din("wvT", [C, C]); woT = din("woT", [C, C])
    w1T = din("w1T", [C, HID]); w2T = din("w2T", [HID, C])
    w1s = din("w1s", [C, W1SH]); w2s = din("w2s", [HID, W2SH])
    wmod_s = din("wmod_s", [C, MODS])
    c_fm = din("c_fm", [128, CCH])
    maa = {w: din(f"maa_{w}", [128, CCH]) for w in "kvr"}
    bv_row = din("bv_row", [1, C]); bo_row = din("bo_row", [1, C])
    br_row = din("br_row", [1, C]); bk_row = din("bk_row", [1, C])
    td_fm = din("td_fm", [128, CCH])
    td_hb = din("td_hb", [128, H]); uf_hb = din("uf_hb", [128, H])
    cwv = {k: din(k, [128, HCH]) for k in ("cw0", "cw1", "cw2", "cb_fm")}
    expw = din("expw", [128, NCORES])
    selL = din("selL", [128, NCORES]); selR = din("selR", [128, NCORES])
    c0f = din("c0f", [128, 1])

    outT = nc.dram_tensor("outT", [C, TPC], F32, kind="ExternalOutput").ap()

    mod_in = nc.dram_tensor("mod_in", [1, MODS + 2], F32).ap()
    mod_out = nc.dram_tensor("mod_out", [NCORES, MODS + 2], F32).ap()
    mod_flat = nc.dram_tensor("mod_flat", [NCORES, MODS], F32).ap()
    dsp_in = nc.dram_tensor("dsp_in", [128, NCH * N], F32).ap()
    dsp_out = nc.dram_tensor("dsp_out", [NCORES * 128, NCH * N], F32).ap()
    halo_in = nc.dram_tensor("halo_in", [128, HCH * 2], F32).ap()
    halo_out = nc.dram_tensor("halo_out", [NCORES * 128, HCH * 2], F32).ap()
    w1q_d = nc.dram_tensor("w1q_d", [CCH, 128, HID], BF16).ap()
    w2q_d = nc.dram_tensor("w2q_d", [HCH, 128, C], BF16).ap()
    x2_d = nc.dram_tensor("x2_d", [CCH, 128, TPC], F32).ap()
    xqs_d = nc.dram_tensor("xqs_d", [CCH, 128, TPC], BF16).ap()
    xqh_d = nc.dram_tensor("xqh_d", [HCH, 128, TPC], BF16).ap()

    def SB(stack, name, shape, dt=F32):
        return stack.enter_context(nc.sbuf_tensor(name, shape, dt))

    def PS(stack, name, shape, dt=F32):
        return stack.enter_context(nc.psum_tensor(name, shape, dt))

    est0 = contextlib.ExitStack()
    ones_b = SB(est0, "ones_b", [128, 128], BF16)
    ones_r = SB(est0, "ones_r", [1, 256], BF16)
    ones_fc = SB(est0, "ones_fc", [128, 1], F32)
    ones_fr = SB(est0, "ones_fr", [1, 128], F32)
    ident_f = SB(est0, "ident_f", [128, 128], F32)
    vecs = SB(est0, "vecs", [128, 18 * CCH], F32)
    VN = ["sa", "ca1", "alk", "alv", "alr", "bek", "bev", "ber", "ga", "gb",
          "cb1", "sbv", "lnwd", "z0", "t0", "t1", "t2", "t3"]
    V_ = {nm: vecs.ap()[:, i * CCH:(i + 1) * CCH] for i, nm in enumerate(VN)}
    z0b = SB(est0, "z0b", [128, CCH], BF16)
    lnwd_hb = SB(est0, "lnwd_hb", [128, H], F32)
    ufhb = SB(est0, "ufhb", [128, H], F32)
    expw_sb = SB(est0, "expw_sb", [128, NCORES], F32)
    convs = SB(est0, "convs", [128, 4 * HCH], F32)
    CV = {nm: convs.ap()[:, i * HCH:(i + 1) * HCH]
          for i, nm in enumerate(["w0", "w1", "w2", "b"])}
    wspow = SB(est0, "wspow", [128, CCH, NCH], F32)
    s0w = SB(est0, "s0w", [128, CCH, NCORES], F32)
    selLb = SB(est0, "selLb", [128, NCORES], F32)
    selRb = SB(est0, "selRb", [128, NCORES], F32)
    bv_rb = SB(est0, "bv_rb", [1, C], BF16)
    bo_rb = SB(est0, "bo_rb", [1, C], BF16)
    br_rb = SB(est0, "br_rb", [1, C], BF16)
    bk_rb = SB(est0, "bk_rb", [1, C], BF16)
    mwb = SB(est0, "mwb", [128, 2], F32)
    rwb = SB(est0, "rwb", [128, 2], F32)
    c2row = SB(est0, "c2row", [128, 16], F32)
    c0fb = SB(est0, "c0fb", [128, 1], F32)
    magicT = SB(est0, "magicT", [128, 264], I32)
    eLh = SB(est0, "eLh", [128, HCH], F32)
    eRh = SB(est0, "eRh", [128, HCH], F32)

    estC = contextlib.ExitStack()
    yp_f = SB(estC, "yp_f", [128, CCH * TPC], BF16)
    estB = contextlib.ExitStack()
    rpp_f = SB(estB, "rpp_f", [128, CCH * TPC], BF16)

    def rppv(j, lo, hi):
        return rpp_f.ap()[:, j * TPC + lo: j * TPC + hi]

    def ypv(j, lo, hi):
        return yp_f.ap()[:, j * TPC + lo: j * TPC + hi]

    estA = contextlib.ExitStack()
    wrT_sb = SB(estA, "wrT_sb", [128, CCH, C], BF16)
    wkT_sb = SB(estA, "wkT_sb", [128, CCH, C], BF16)
    wvT_sb = SB(estA, "wvT_sb", [128, CCH, C], BF16)
    masks0 = SB(estA, "masks0", [128, H * Q], BF16)
    masks1 = SB(estA, "masks1", [128, H * 128], BF16)
    wbB = SB(estA, "wbB", [128, CCH * Q], BF16)
    wkBt = SB(estA, "wkBt", [128, 2 * C], BF16)
    Lc = SB(estA, "Lc", [128, 2 * CCH * N], F32)
    Lb = SB(estA, "Lb", [128, CCH, N], BF16)
    xch = SB(estA, "xch", [128, CCH, Q + 1], F32)
    xbf = SB(estA, "xbf", [128, CCH, Q + 1], BF16)
    xsq = SB(estA, "xsq", [128, CCH, Q + 1], BF16)
    uch = SB(estA, "uch", [128, CCH, Q + 1], BF16)
    mixb = SB(estA, "mixb", [128, CCH, Q], BF16)
    statm = SB(estA, "statm", [128, Q + 1], F32)
    statv = SB(estA, "statv", [128, Q + 1], F32)
    statw = SB(estA, "statw", [128, Q + 1], F32)
    rfm = SB(estA, "rfm", [128, CCH, Q], BF16)
    kfm = SB(estA, "kfm", [128, CCH, Q], BF16)
    ktm = SB(estA, "ktm", [128, 2, C], BF16)
    vtm = SB(estA, "vtm", [128, 2, C], BF16)
    Asb = SB(estA, "Asb", [128, 2, Q + 128], BF16)

    estP0 = contextlib.ExitStack()
    siluc = SB(estP0, "siluc", [128, CCH], F32)
    mod_sb = SB(estP0, "mod_sb", [1, MODS + 2], F32)
    modg = SB(estP0, "modg", [NCORES, MODS + 2], F32)
    iotaQ = SB(estP0, "iotaQ", [128, Q], I32)
    dQ = SB(estP0, "dQ", [128, Q], F32)
    I0 = SB(estP0, "I0", [128, Q], F32)
    tauQ = SB(estP0, "tauQ", [128, Q], F32)
    expP = SB(estP0, "expP", [128, 2 * N], F32)
    iotaC = SB(estP0, "iotaC", [128, NCH], I32)
    iotaCf = SB(estP0, "iotaCf", [128, NCH], F32)

    def lcv(s, hp=None):
        b = (s % 2) * CCH * N
        if hp is None:
            return Lc.ap()[:, b:b + CCH * N]
        return Lc.ap()[:, b + hp * N: b + (hp + 1) * N]

    cnt = {}
    sems = {}
    semstack = contextlib.ExitStack()
    for s in ["dq", "dq2", "dq3", "sv", "ss", "sg", "st", "cc", "so"]:
        sems[s] = semstack.enter_context(nc.semaphore(s))
        cnt[s] = 0

    blkg = nc.Block()
    block = blkg.__enter__()

    def inc(instr, name, n=1):
        instr.then_inc(sems[name], n)
        cnt[name] += n
        return cnt[name]

    def W(e, *names):
        for name in names:
            e.wait_ge(sems[name], cnt[name])

    def selfw(e, name):
        inc(e.drain(), name)
        e.wait_ge(sems[name], cnt[name])

    DBGN = [0]

    def dump(key, ap, dt=F32):
        if key not in dbg:
            return
        i = DBGN[0]; DBGN[0] += 1
        d = nc.dram_tensor(f"dbg{i}", list(ap.shape), dt, kind="ExternalOutput").ap()
        dbg_map[key] = f"dbg{i}"

        def f(e):
            W(e, "sv", "ss", "sg", "st", "dq", "dq2", "dq3")
            inc(e.dma_start(out=d, in_=ap), "dq3", 16)
            W(e, "dq3")
        block.sync(f)

    def barrier():
        for eng in ("sync", "vector", "scalar", "gpsimd", "tensor"):
            def f(e):
                W(e, "sv", "ss", "sg", "st", "dq", "dq2", "dq3", "cc", "so")
            getattr(block, eng)(f)

    ps0 = contextlib.ExitStack()
    ps_red = PS(ps0, "ps_red", [1, 512])[:, 0:2]
    ps_bc = PS(ps0, "ps_bc", [128, 512])[:, 0:2]
    ps_mod1 = PS(ps0, "ps_mod1", [128, 512])[0:1, 0:256]
    ps_mod0 = PS(ps0, "ps_mod0", [128, 512])[0:1, :]

    def f(e):
        inc(e.dma_start(out=V_["t0"], in_=c_fm), "dq", 16)
        for w in "kvr":
            inc(e.dma_start(out=V_["al" + w], in_=maa[w]), "dq", 16)
        inc(e.dma_start(out=V_["t1"], in_=td_fm), "dq", 16)
        inc(e.dma_start(out=lnwd_hb[:], in_=td_hb), "dq", 16)
        inc(e.dma_start(out=ufhb[:], in_=uf_hb), "dq", 16)
        inc(e.dma_start(out=selLb[:], in_=selL), "dq", 16)
        inc(e.dma_start(out=selRb[:], in_=selR), "dq", 16)
        inc(e.dma_start(out=c0fb[:], in_=c0f), "dq", 16)
        inc(e.dma_start(out=expw_sb[:], in_=expw), "dq", 16)
        for k_, nmv in (("cw0", "w0"), ("cw1", "w1"), ("cw2", "w2"), ("cb_fm", "b")):
            inc(e.dma_start(out=CV[nmv], in_=cwv[k_]), "dq", 16)
    block.sync(f)

    def f(e):
        inc(e.dma_start(out=bv_rb[:], in_=bv_row), "dq2", 16)
        inc(e.dma_start(out=bo_rb[:], in_=bo_row), "dq2", 16)
        inc(e.dma_start(out=br_rb[:], in_=br_row), "dq2", 16)
        inc(e.dma_start(out=bk_rb[:], in_=bk_row), "dq2", 16)
        for wsb, wsrc in ((wrT_sb, wrT), (wkT_sb, wkT), (wvT_sb, wvT)):
            inc(e.dma_start(out=wsb.ap(),
                            in_=wsrc.rearrange("(k p) o -> p k o", p=128)), "dq2", 16)
    block.gpsimd(f)

    def f(e):
        e.memset(ones_b[:], 1.0)
        e.memset(ones_r[:], 1.0)
        e.memset(ones_fc[:], 1.0)
        e.memset(ones_fr[:], 1.0)
        e.memset(magicT[:], 0x5f3759df)
        e.memset(Lc[:], 0.0)
        e.memset(Lb[:], 0.0)
        e.iota(iotaQ[:], pattern=[[1, Q]], base=-1, channel_multiplier=-1)
        e.iota(iotaC[:], pattern=[[Q, NCH]], base=0, channel_multiplier=0)
        inc(e.drain(), "sg")
    block.gpsimd(f)

    def f(e):
        W(e, "sg")
        e.tensor_copy(dQ[:], iotaQ[:])
        e.tensor_copy(iotaCf[:], iotaC[:])
        selfw(e, "sv")
        e.tensor_single_scalar(I0[:], dQ[:], -1.0, ALU.is_equal)
        e.tensor_scalar_max(dQ[:], dQ[:], -1.0)
        selfw(e, "sv")
        e.tensor_copy(ident_f[:], I0[:, 0:128])
        inc(e.drain(), "sv")
    block.vector(f)

    def f(e):
        W(e, "sv")
        e.iota(iotaQ[:], pattern=[[1, Q]], base=0, channel_multiplier=0)
        selfw(e, "sg")
        inc(e.drain(), "sg")
    block.gpsimd(f)

    def f(e):
        W(e, "sg")
        e.tensor_copy(tauQ[:], iotaQ[:])
        inc(e.drain(), "sv")
    block.vector(f)

    def f(e):
        W(e, "sv")
        for tb in range(2):
            e.iota(iotaQ[:, tb * N:(tb + 1) * N], pattern=[[0, N]],
                   base=Q - 1 - 128 * tb, channel_multiplier=-1)
        inc(e.drain(), "sg")
    block.gpsimd(f)

    def f(e):
        W(e, "sg")
        e.tensor_copy(expP[:], iotaQ[:, 0:2 * N])
        inc(e.drain(), "sv")
    block.vector(f)

    def f(e):
        W(e, "dq")
        e.activation(siluc[:], V_["t0"], AF.Silu)
        e.activation(V_["lnwd"], V_["t1"], AF.Exp)
        e.activation(statw[:, 0:H], lnwd_hb[:], AF.Exp)
        inc(e.drain(), "ss")
    block.scalar(f)

    def f(e):
        W(e, "ss")
        e.tensor_scalar_mul(V_["lnwd"], V_["lnwd"], -1.0)
        e.tensor_scalar_mul(lnwd_hb[:], statw[:, 0:H], -1.0)
        inc(e.drain(), "sv")
    block.vector(f)
    siluc_done = cnt["ss"]
    negd_done = cnt["sv"]
    PCST = {}

    xstage = xch.ap().rearrange("p a b -> p (a b)")
    ustage = uch.ap().rearrange("p a b -> p (a b)")
    w1v = w1s.rearrange("(k p) m -> p k m", p=128)
    w2v = w2s.rearrange("(k p) m -> p k m", p=128)
    red_done = {}

    def w2slot(kk):
        return statm.ap()[:, (kk % 2) * W2SH:(kk % 2 + 1) * W2SH]

    def f(e):
        for kk in range(2):
            inc(e.dma_start(out=w2slot(kk), in_=w2v[:, kk, :]), "dq3", 16)
            red_done[("w2ld", kk)] = cnt["dq3"]
    block.scalar(f)

    def f(e):
        W(e, "sv")
        for hp in range(CCH):
            sc = V_["lnwd"][:, hp:hp + 1]
            e.activation(wspow.ap()[:, hp, :], iotaCf[:], AF.Exp, scale=sc)
            e.activation(s0w.ap()[:, hp, :], expw_sb[:], AF.Exp, scale=sc)
        for j in range(CCH):
            e.activation(wbB.ap()[:, j * Q:(j + 1) * Q], tauQ[:], AF.Exp,
                         scale=V_["lnwd"][:, j:j + 1])
        for tb in range(2):
            for h in range(H):
                e.activation(wkBt.ap()[:, tb * C + h * N: tb * C + (h + 1) * N],
                             expP[:, tb * N:(tb + 1) * N], AF.Exp,
                             scale=lnwd_hb[:, h:h + 1])
        last = None
        for h in range(H):
            last = e.activation(masks0.ap()[:, h * Q:(h + 1) * Q], dQ[:], AF.Exp,
                                scale=lnwd_hb[:, h:h + 1])
        inc(last, "ss")
    block.scalar(f)
    mexp_done = cnt["ss"]

    def f(e):
        e.wait_ge(sems["ss"], mexp_done)
        last = None
        for h in range(H):
            last = e.affine_select(out=masks0.ap()[:, h * Q:(h + 1) * Q],
                                   in_=masks0.ap()[:, h * Q:(h + 1) * Q],
                                   pattern=[[1, Q]], base=-1,
                                   channel_multiplier=-1, compare_op=ALU.is_ge,
                                   fill=0.0)
        inc(last, "sg")
    block.gpsimd(f)
    msel_done = cnt["sg"]

    def f(e):
        e.wait_ge(sems["sg"], msel_done)
        for h in range(H):
            e.scalar_tensor_tensor(masks0.ap()[:, h * Q:(h + 1) * Q], I0[:],
                                   ufhb[:, h:h + 1],
                                   masks0.ap()[:, h * Q:(h + 1) * Q],
                                   ALU.mult, ALU.add)
        inc(e.tensor_copy(
            masks1.ap().rearrange("p (h c) -> p h c", c=128),
            masks0.ap().rearrange("p (h q) -> p h q", q=Q)[:, :, 0:128]), "sv")
    block.vector(f)

    mm_done = {}
    acc_done = {}
    for k in range(CCH):
        slot = xstage[:, (k % 2) * MODS:(k % 2) * MODS + MODS]

        def f(e, k=k, slot=slot):
            if k >= 2:
                e.wait_ge(sems["st"], mm_done[k - 2])
            inc(e.dma_start(out=slot, in_=wmod_s[128 * k:128 * (k + 1), :]),
                "dq", 16)
        block.sync(f)
        ld_done = cnt["dq"]

        def f(e, k=k, slot=slot, ld_done=ld_done):
            e.wait_ge(sems["dq"], ld_done)
            e.wait_ge(sems["ss"], siluc_done)
            if k >= 1:
                e.wait_ge(sems["sv"], acc_done[k - 1])
            e.matmul(ps_mod0[:, 0:512], siluc.ap()[:, k:k + 1],
                     slot[:, 0:512], start=True, stop=True)
            inc(e.matmul(ps_mod1[:, 0:MODS - 512], siluc.ap()[:, k:k + 1],
                         slot[:, 512:MODS], start=True, stop=True), "st")
        block.tensor(f)
        mm_done[k] = cnt["st"]

        def f(e, k=k):
            e.wait_ge(sems["st"], mm_done[k])
            if k == 0:
                e.tensor_copy(mod_sb.ap()[:, 0:512], ps_mod0[:, 0:512])
                inc(e.tensor_copy(mod_sb.ap()[:, 512:MODS],
                                  ps_mod1[:, 0:MODS - 512]), "sv")
            else:
                e.tensor_add(mod_sb.ap()[:, 0:512], mod_sb.ap()[:, 0:512],
                             ps_mod0[:, 0:512])
                inc(e.tensor_add(mod_sb.ap()[:, 512:MODS],
                                 mod_sb.ap()[:, 512:MODS],
                                 ps_mod1[:, 0:MODS - 512]), "sv")
        block.vector(f)
        acc_done[k] = cnt["sv"]
    PCST["xch_free"] = cnt["st"]

    w1slot = xstage[:, 1536:1536 + W1SH]

    def f(e):
        inc(e.dma_start(out=w1slot, in_=w1v[:, 0, :]), "dq2", 16)
        red_done[("w1ld", 0)] = cnt["dq2"]
    block.gpsimd(f)

    for kk in range(CCH):
        def f(e, kk=kk):
            e.wait_ge(sems["dq2"], red_done[("w1ld", kk)])
            inc(e.reduce_sum(statv.ap()[:, kk:kk + 1], w1slot, AX,
                             apply_absolute_value=True), "sv")
        block.vector(f)
        red_done[("w1", kk)] = cnt["sv"]
        if kk + 1 < CCH:
            def f(e, kk=kk):
                e.wait_ge(sems["sv"], red_done[("w1", kk)])
                inc(e.dma_start(out=w1slot, in_=w1v[:, kk + 1, :]), "dq2", 16)
                red_done[("w1ld", kk + 1)] = cnt["dq2"]
            block.gpsimd(f)
    for kk in range(HCH):
        def f(e, kk=kk):
            e.wait_ge(sems["dq3"], red_done[("w2ld", kk)])
            inc(e.reduce_sum(statv.ap()[:, CCH + kk:CCH + kk + 1], w2slot(kk), AX,
                             apply_absolute_value=True), "sv")
        block.vector(f)
        red_done[("w2", kk)] = cnt["sv"]
        if kk + 2 < HCH:
            def f(e, kk=kk):
                e.wait_ge(sems["sv"], red_done[("w2", kk)])
                inc(e.dma_start(out=w2slot(kk + 2), in_=w2v[:, kk + 2, :]),
                    "dq3", 16)
                red_done[("w2ld", kk + 2)] = cnt["dq3"]
            block.scalar(f)

    def f(e):
        W(e, "sv")
        e.reduce_sum(V_["t2"][:, 0:1], statv.ap()[:, 0:CCH], AX)
        e.reduce_sum(V_["t2"][:, 1:2], statv.ap()[:, CCH:CCH + HCH], AX)
        selfw(e, "sv")
        inc(e.drain(), "sv")
    block.vector(f)
    PCST["xch_free_sv"] = cnt["sv"]

    def f(e):
        W(e, "sv")
        e.matmul(ps_red[:], ones_fc[:], V_["t2"][:, 0:2], start=True, stop=True)
        inc(e.drain(), "st")
    block.tensor(f)

    def f(e):
        W(e, "st")
        e.tensor_copy(mod_sb.ap()[:, MODS:MODS + 2], ps_red[:])
        inc(e.drain(), "sv")
    block.vector(f)

    dump("modsb", mod_sb[:])
    dump("siluc", siluc[:])

    def f(e):
        W(e, "sv")
        inc(e.dma_start(out=mod_in, in_=mod_sb[:]), "dq", 16)
    block.sync(f)

    def f(e):
        W(e, "dq")
        inc(e.collective_compute("AllGather", ALU.bypass,
                                 replica_groups=[list(range(NCORES))],
                                 ins=[mod_in], outs=[mod_out]), "cc", 1)
        W(e, "cc")
    block.gpsimd(f)

    def f(e):
        W(e, "cc")
        inc(e.dma_start(out=mod_flat, in_=mod_out[:, 0:MODS]), "dq", 16)
        inc(e.dma_start(out=modg[:], in_=mod_out), "dq", 16)
        W(e, "dq")
        flat = mod_flat.rearrange("a b -> (a b)")
        for i, nm in enumerate(["sa", "ca1", "ga", "sbv", "cb1", "gb"]):
            inc(e.dma_start(out=V_[nm],
                            in_=flat[i * C:(i + 1) * C].rearrange("(j p) -> p j", p=128),
                            allow_slow_non_contiguous=True), "dq", 16)
    block.sync(f)

    def f(e):
        W(e, "dq")
        e.matmul(ps_red[:], ones_fc[0:NCORES, :], modg.ap()[:, MODS:MODS + 2],
                 start=True, stop=True)
        inc(e.drain(), "st")
    block.tensor(f)

    def f(e):
        W(e, "st")
        e.tensor_scalar(V_["t3"][0:1, 0:2], ps_red[:], 1.0 / (C * HID), 1e-5,
                        ALU.mult, ALU.max)
        inc(e.drain(), "sv")
    block.vector(f)

    def f(e):
        W(e, "sv")
        e.matmul(ps_bc[:], ones_fr[:], V_["t3"][0:1, 0:2], start=True, stop=True)
        inc(e.drain(), "st")
    block.tensor(f)

    def f(e):
        W(e, "st")
        e.tensor_copy(mwb[:], ps_bc[:])
        selfw(e, "sv")
        e.reciprocal(rwb[:], mwb[:])
        inc(e.drain(), "sv")
    block.vector(f)

    def f(e):
        W(e, "sv", "dq")
        e.tensor_scalar_add(V_["ca1"], V_["ca1"], 1.0)
        e.tensor_scalar_add(V_["cb1"], V_["cb1"], 1.0)
        selfw(e, "sv")
        for w in "kvr":
            e.tensor_mul(V_["be" + w], V_["al" + w], V_["ca1"])
        selfw(e, "sv")
        for w in "kvr":
            e.tensor_sub(V_["al" + w], V_["ca1"], V_["be" + w])
        e.reciprocal(V_["t0"], V_["ca1"])
        selfw(e, "sv")
        e.tensor_mul(V_["z0"], V_["sa"], V_["t0"])
        selfw(e, "sv")
        e.tensor_scalar_mul(V_["z0"], V_["z0"], -1.0)
        selfw(e, "sv")
        e.tensor_copy(z0b[:], V_["z0"])
        for g in ("ga", "gb"):
            e.tensor_scalar_max(V_["t0"], V_[g], 0.25)
            e.tensor_scalar_min(V_["t1"], V_[g], -0.25)
            e.tensor_single_scalar(V_["t2"], V_[g], 0.0, ALU.is_ge)
            selfw(e, "sv")
            e.tensor_sub(V_["t0"], V_["t0"], V_["t1"])
            selfw(e, "sv")
            e.tensor_mul(V_["t0"], V_["t0"], V_["t2"])
            selfw(e, "sv")
            e.tensor_add(V_[g], V_["t1"], V_["t0"])
            selfw(e, "sv")
        inc(e.drain(), "sv")
    block.vector(f)

    dump("vecs", vecs.ap()[:, 0:16 * CCH])
    dump("masks0", masks0[:], BF16)
    dump("masks1", masks1[:], BF16)
    dump("wspow", wspow[:])
    dump("s0w", s0w[:])
    dump("wbB", wbB[:], BF16)
    dump("wkBt", wkBt[:], BF16)
    dump("mwb", mwb[:])

    done = stop_after == "p0"

    barrier()
    estP0.close()
    if not done:
        ps0.close()
        ps1 = contextlib.ExitStack()
        ps_s1 = PS(ps1, "ps_s1", [128, 512])[:, 0:Q + 1]
        ps_s2 = PS(ps1, "ps_s2", [128, 512])[:, 0:Q + 1]
        ps_p0 = PS(ps1, "ps_p0", [128, 512])
        ps_p1 = PS(ps1, "ps_p1", [128, 512])
        ps_a0 = PS(ps1, "ps_a0", [128, 512])[:, 0:Q]
        ps_a1 = PS(ps1, "ps_a1", [128, 512])[:, 0:128]
        ps_y = PS(ps1, "ps_y", [128, 512])[:, 0:Q]
        ps_d = PS(ps1, "ps_d", [128, 512])[:, 0:N]

        def p1_chunk(c):
            base = c * Q
            mixk = xsq.ap()[:, :, 0:Q]

            def f(e, c=c):
                if "u_done" in PCST:
                    e.wait_ge(sems["sv"], PCST["u_done"])
                if c == 0 and "xch_free" in PCST:
                    e.wait_ge(sems["st"], PCST["xch_free"])
                    e.wait_ge(sems["sv"], PCST["xch_free_sv"])
                inc(e.dma_start(out=xch.ap(),
                                in_=xs[:, base:base + Q + 1]
                                .rearrange("(k p) t -> p k t", p=128)), "dq", 16)
            block.sync(f)
            xload = cnt["dq"]

            def f(e):
                if "stats_done" in PCST:
                    e.wait_ge(sems["st"], PCST["stats_done"])
                inc(e.dma_start(out=xbf.ap(),
                                in_=xs[:, base:base + Q + 1]
                                .rearrange("(k p) t -> p k t", p=128)), "dq2", 16)
            block.gpsimd(f)
            xbload = cnt["dq2"]

            def f(e):
                e.wait_ge(sems["dq2"], xbload)
                if "mixk_done" in PCST:
                    e.wait_ge(sems["st"], PCST["mixk_done"])
                inc(e.tensor_mul(xsq[:], xbf[:], xbf[:]), "sv")
            block.vector(f)
            sq_done = cnt["sv"]

            def f(e):
                e.wait_ge(sems["sv"], sq_done)
                for k in range(CCH):
                    e.matmul(ps_s1[:], ones_b[:], xbf.ap()[:, k, :],
                             start=(k == 0), stop=(k == CCH - 1))
                for k in range(CCH):
                    last = e.matmul(ps_s2[:], ones_b[:], xsq.ap()[:, k, :],
                                    start=(k == 0), stop=(k == CCH - 1))
                inc(last, "st")
            block.tensor(f)
            PCST["stats_done"] = cnt["st"]

            def f(e):
                e.wait_ge(sems["st"], PCST["stats_done"])
                e.tensor_scalar_mul(statm[:], ps_s1[:], INV_C)
                e.tensor_mul(statv[:], statm[:], statm[:])
                e.tensor_scalar_sub(statv[:], statv[:], EPS)
                inc(e.scalar_tensor_tensor(statv[:], ps_s2[:], INV_C, statv[:],
                                           ALU.mult, ALU.subtract), "sv")
            block.vector(f)

            def f(e):
                e.wait_ge(sems["dq"], xload)
                bc_m = statm.ap()[:, None, :].broadcast_to([128, CCH, Q + 1])
                e.tensor_tensor(xch[:], xch[:], bc_m, ALU.subtract)
                e.tensor_single_scalar(statw.ap().bitcast(I32)[:, 0:Q + 1], statv.ap().bitcast(I32)[:, 0:Q + 1], 1,
                                       ALU.arith_shift_right)
                e.tensor_sub(statw.ap().bitcast(I32)[:, 0:Q + 1], magicT.ap()[:, 0:Q + 1], statw.ap().bitcast(I32)[:, 0:Q + 1])
                for _nit in range(2):
                    e.tensor_mul(statm[:], statv[:], statw[:])
                    e.tensor_mul(statm[:], statm[:], statw[:])
                    e.tensor_scalar(statm[:], statm[:], -0.5, 1.5, ALU.mult, ALU.add)
                    e.tensor_mul(statw[:], statw[:], statm[:])
                bc_v = statw.ap()[:, None, :].broadcast_to([128, CCH, Q + 1])
                inc(e.tensor_tensor(uch[:], xch[:], bc_v, ALU.mult), "sv")
            block.vector(f)
            PCST["u_done"] = cnt["sv"]

            if c == 0:
                def f(e):
                    W(e, "sv")
                    e.tensor_sub(mixb.ap()[:, 0, 0:CCH], z0b[:], uch.ap()[:, :, 0])
                    inc(e.scalar_tensor_tensor(uch.ap()[:, :, 0],
                                               mixb.ap()[:, 0, 0:CCH],
                                               c0fb.ap()[:, 0:1], uch.ap()[:, :, 0],
                                               ALU.mult, ALU.add), "sv")
                block.vector(f)
                PCST["u_done"] = cnt["sv"]
                dump("u0", uch[:], BF16)

            def mix(w, buf, free_st):
                def f(e):
                    e.wait_ge(sems["sv"], PCST["u_done"])
                    if free_st is not None:
                        e.wait_ge(sems["st"], free_st)
                    last = None
                    for j in range(CCH):
                        e.tensor_scalar(buf[:, j, :], uch.ap()[:, j, 1:Q + 1],
                                        V_["al" + w][:, j:j + 1],
                                        V_["sa"][:, j:j + 1], ALU.mult, ALU.add)
                        last = e.scalar_tensor_tensor(buf[:, j, :],
                                                      uch.ap()[:, j, 0:Q],
                                                      V_["be" + w][:, j:j + 1],
                                                      buf[:, j, :], ALU.mult, ALU.add)
                    inc(last, "sv")
                block.vector(f)
                return cnt["sv"]

            mr_done = mix("r", mixb.ap(), PCST.get("mixb_free"))
            ev = {}
            for m in range(CCH):
                pp = ps_p0 if m % 2 == 0 else ps_p1

                def f(e, m=m, pp=pp, mr_done=mr_done):
                    e.wait_ge(sems["sv"], mr_done)
                    if m >= 2:
                        e.wait_ge(sems["sv"], ev[m - 2])
                    e.matmul(pp[:, 0:Q], br_rb.ap()[0:1, m * 128:(m + 1) * 128],
                             ones_r.ap()[0:1, 0:Q], start=True, stop=False)
                    last = None
                    for k in range(CCH):
                        last = e.matmul(pp[:, 0:Q],
                                        wrT_sb.ap()[:, k, m * 128:(m + 1) * 128],
                                        mixb.ap()[:, k, :], start=False,
                                        stop=(k == CCH - 1))
                    inc(last, "st")
                block.tensor(f)
                stv = cnt["st"]

                def f(e, m=m, pp=pp, stv=stv, base=base):
                    e.wait_ge(sems["st"], stv)
                    e.tensor_copy(rfm.ap()[:, m, :], pp[:, 0:Q])
                    inc(e.tensor_tensor(rppv(m, base, base + Q), pp[:, 0:Q],
                                        wbB.ap()[:, m * Q:(m + 1) * Q], ALU.mult),
                        "sv")
                block.vector(f)
                ev[m] = cnt["sv"]

            mk_done = mix("k", mixk, PCST["stats_done"])
            ev = {}
            for m in range(CCH):
                pp = ps_p0 if m % 2 == 0 else ps_p1

                def f(e, m=m, pp=pp, mk_done=mk_done):
                    e.wait_ge(sems["sv"], mk_done)
                    if m >= 2:
                        e.wait_ge(sems["sv"], ev[m - 2])
                    e.matmul(pp[:, 0:Q], bk_rb.ap()[0:1, m * 128:(m + 1) * 128],
                             ones_r.ap()[0:1, 0:Q], start=True, stop=False)
                    last = None
                    for k in range(CCH):
                        last = e.matmul(pp[:, 0:Q],
                                        wkT_sb.ap()[:, k, m * 128:(m + 1) * 128],
                                        mixk[:, k, :], start=False,
                                        stop=(k == CCH - 1))
                    inc(last, "st")
                block.tensor(f)
                stv = cnt["st"]

                def f(e, m=m, pp=pp, stv=stv):
                    e.wait_ge(sems["st"], stv)
                    inc(e.tensor_copy(kfm.ap()[:, m, :], pp[:, 0:Q]), "sv")
                block.vector(f)
                ev[m] = cnt["sv"]
            PCST["mixk_done"] = cnt["st"]

            def f(e):
                W(e, "sv")
                for m in range(CCH):
                    for tb in range(2):
                        inc(e.dma_start(out=ktm.ap()[:, tb, m * 128:(m + 1) * 128],
                                        in_=kfm.ap()[:, m, tb * 128:(tb + 1) * 128],
                                        transpose=True), "dq3", 16)
            block.sync(f)
            ktr_done = cnt["dq3"]

            def f(e):
                e.wait_ge(sems["dq3"], ktr_done)
                e.tensor_tensor(ktm.ap()[:, 0, :], ktm.ap()[:, 0, :],
                                wkBt.ap()[:, 0:C], ALU.mult)
                inc(e.tensor_tensor(ktm.ap()[:, 1, :], ktm.ap()[:, 1, :],
                                    wkBt.ap()[:, C:2 * C], ALU.mult), "sv")
            block.vector(f)

            mv_done = mix("v", mixb.ap(), cnt["st"])
            ev = {}
            for i4 in range(4):
                tb, nh = divmod(i4, 2)
                pp = ps_p0 if nh == 0 else ps_p1

                def f(e, i4=i4, tb=tb, nh=nh, pp=pp, mv_done=mv_done):
                    e.wait_ge(sems["sv"], mv_done)
                    if i4 >= 2:
                        e.wait_ge(sems["sv"], ev[i4 - 2])
                    e.matmul(pp[:], ones_r.ap()[0:1, 0:128],
                             bv_rb.ap()[:, nh * 512:(nh + 1) * 512],
                             start=True, stop=False)
                    last = None
                    for k in range(CCH):
                        last = e.matmul(pp[:], mixb.ap()[:, k, tb * 128:(tb + 1) * 128],
                                        wvT_sb.ap()[:, k, nh * 512:(nh + 1) * 512],
                                        start=False, stop=(k == CCH - 1))
                    inc(last, "st")
                block.tensor(f)
                stv = cnt["st"]

                def f(e, tb=tb, nh=nh, pp=pp, stv=stv):
                    e.wait_ge(sems["st"], stv)
                    inc(e.tensor_copy(vtm.ap()[:, tb, nh * 512:(nh + 1) * 512],
                                      pp[:]), "sv")
                block.vector(f)
                ev[i4] = cnt["sv"]
            vev_done = cnt["sv"]
            PCST["mixb_free"] = cnt["st"]

            if c == 0:
                dump("r0", rfm[:], BF16)
                dump("k0", kfm[:], BF16)
                dump("v0", vtm[:], BF16)
                dump("kw0", ktm[:], BF16)
                dump("rpp0", rppv(0, 0, Q), BF16)

            aev = {}
            yev = {}
            stY_prev = [0]
            lue = [0]
            for hp in range(CCH):
                hA, hB = 2 * hp, 2 * hp + 1
                pa0 = ps_a0 if hp % 2 == 0 else ps_s1.bitcast(F32)[:, 0:Q]
                pa1 = ps_a1 if hp % 2 == 0 else ps_s2.bitcast(F32)[:, 0:128]

                def f(e, hp=hp, pa0=pa0, pa1=pa1, vev_done=vev_done):
                    e.wait_ge(sems["sv"], vev_done)
                    if hp >= 2:
                        e.wait_ge(sems["sv"], aev[hp - 2])
                    e.matmul(pa0[:, 0:Q], kfm.ap()[0:64, hp, 0:128],
                             rfm.ap()[0:64, hp, :], start=True, stop=True)
                    inc(e.matmul(pa1[:, 0:128], kfm.ap()[0:64, hp, 128:Q],
                                 rfm.ap()[0:64, hp, 128:Q], start=True, stop=True),
                        "st")
                block.tensor(f)
                stA = cnt["st"]

                def f(e, hp=hp, hA=hA, pa0=pa0, pa1=pa1, stA=stA):
                    e.wait_ge(sems["st"], stA)
                    if hp >= 1:
                        e.wait_ge(sems["st"], stY_prev[0])
                    e.tensor_tensor(Asb.ap()[:, 0, 0:Q], pa0[:, 0:Q],
                                    masks0.ap()[:, hA * Q:(hA + 1) * Q], ALU.mult)
                    inc(e.tensor_tensor(Asb.ap()[:, 0, Q:Q + 128], pa1[:, 0:128],
                                        masks1.ap()[:, hA * 128:(hA + 1) * 128],
                                        ALU.mult), "sv")
                block.vector(f)
                evA = cnt["sv"]

                def f(e, hp=hp, pa0=pa0, pa1=pa1, evA=evA):
                    e.wait_ge(sems["sv"], evA)
                    e.matmul(pa0[:, 0:Q], kfm.ap()[64:128, hp, 0:128],
                             rfm.ap()[64:128, hp, :], start=True, stop=True)
                    inc(e.matmul(pa1[:, 0:128], kfm.ap()[64:128, hp, 128:Q],
                                 rfm.ap()[64:128, hp, 128:Q], start=True, stop=True),
                        "st")
                block.tensor(f)
                stB = cnt["st"]

                def f(e, hp=hp, hB=hB, pa0=pa0, pa1=pa1, stB=stB):
                    e.wait_ge(sems["st"], stB)
                    e.tensor_tensor(Asb.ap()[:, 1, 0:Q], pa0[:, 0:Q],
                                    masks0.ap()[:, hB * Q:(hB + 1) * Q], ALU.mult)
                    inc(e.tensor_tensor(Asb.ap()[:, 1, Q:Q + 128], pa1[:, 0:128],
                                        masks1.ap()[:, hB * 128:(hB + 1) * 128],
                                        ALU.mult), "sv")
                block.vector(f)
                aev[hp] = cnt["sv"]

                def f(e, hp=hp, hA=hA, hB=hB, base=base):
                    e.wait_ge(sems["sv"], aev[hp])
                    if hp >= 1:
                        e.wait_ge(sems["sv"], yev[hp - 1])
                    last = None
                    for hh, head in ((0, hA), (1, hB)):
                        rows = slice(hh * 64, hh * 64 + 64)
                        e.matmul(ps_y[rows, 0:Q], vtm.ap()[:, 0, head * N:(head + 1) * N],
                                 Asb.ap()[:, hh, 0:Q], start=True, stop=False)
                        e.matmul(ps_y[rows, 128:Q], vtm.ap()[:, 1, head * N:(head + 1) * N],
                                 Asb.ap()[:, hh, Q:Q + 128], start=False, stop=False)
                        last = e.matmul(ps_y[rows, 0:Q], Lb.ap()[rows, hp, :],
                                        rppv(hp, base, base + Q)[rows, :],
                                        start=False, stop=True)
                    inc(last, "st")
                block.tensor(f)
                stY = cnt["st"]
                stY_prev[0] = stY

                def f(e, hp=hp, base=base, stY=stY):
                    e.wait_ge(sems["st"], stY)
                    inc(e.tensor_copy(ypv(hp, base, base + Q), ps_y[:]), "sv")
                block.vector(f)
                yev[hp] = cnt["sv"]

                def f(e, hp=hp, hA=hA, hB=hB, stY=stY, lue_v=lue[0]):
                    e.wait_ge(sems["sv"], yev[hp])
                    if lue_v:
                        e.wait_ge(sems["sv"], lue_v)
                    last = None
                    for hh, head in ((0, hA), (1, hB)):
                        rows = slice(hh * 64, hh * 64 + 64)
                        for tb in range(2):
                            last = e.matmul(ps_d[rows, :],
                                            ktm.ap()[:, tb, head * N:(head + 1) * N],
                                            vtm.ap()[:, tb, head * N:(head + 1) * N],
                                            start=(tb == 0), stop=(tb == 1))
                    inc(last, "st")
                block.tensor(f)
                stD = cnt["st"]

                def f(e, hp=hp, c=c, stD=stD):
                    e.wait_ge(sems["st"], stD)
                    inc(e.scalar_tensor_tensor(lcv(c + 1, hp), lcv(c, hp),
                                               wspow.ap()[:, hp, 1:2], ps_d[:],
                                               ALU.mult, ALU.add), "sv")
                block.vector(f)
                lue[0] = cnt["sv"]

            def f(e, c=c):
                W(e, "sv")
                inc(e.tensor_copy(Lb[:],
                                  lcv(c + 1).rearrange("p (h n) -> p h n", h=CCH)),
                    "sv")
            block.vector(f)

            if c == 0:
                dump("yp0", ypv(0, 0, Q), BF16)
                dump("L1", lcv(1))

        last = NCH
        if stop_after and str(stop_after).startswith("p1c"):
            last = int(str(stop_after)[3:]) + 1
        for c in range(last):
            p1_chunk(c)

        dump("Lfin", lcv(NCH))
        dump("yp_full", yp_f[:], BF16)
        if stop_after and str(stop_after).startswith("p1"):
            done = True

        def f(e):
            W(e, "sv", "st")
            inc(e.dma_start(out=dsp_in, in_=lcv(last)), "dq", 16)
        block.sync(f)

        barrier()
        ps1.close()
    estA.close()

    if not done:
        estP2 = contextlib.ExitStack()
        Dg = SB(estP2, "Dg", [128, NCORES, CCH, N], F32)
        S0 = SB(estP2, "S0", [128, CCH, N], F32)
        S0cb = SB(estP2, "S0cb", [128, CCH, N], BF16)
        psP2 = contextlib.ExitStack()
        ps_c0 = PS(psP2, "ps_c0", [128, Q])
        ps_c1 = PS(psP2, "ps_c1", [128, Q])

        def f(e):
            inc(e.collective_compute("AllGather", ALU.bypass,
                                     replica_groups=[list(range(NCORES))],
                                     ins=[dsp_in], outs=[dsp_out]), "cc", 1)
            W(e, "cc")
            e.memset(S0[:], 0.0)
            inc(e.drain(), "sg")
        block.gpsimd(f)

        def f(e):
            W(e, "cc")
            for j in range(NCORES):
                inc(e.dma_start(out=Dg.ap()[:, j, :, :],
                                in_=dsp_out[128 * j:128 * (j + 1), :]
                                .rearrange("p (h n) -> p h n", h=CCH)), "dq", 16)
        block.sync(f)

        def f(e):
            W(e, "dq", "sg")
            last = None
            for j in range(NCORES):
                for hp in range(CCH):
                    last = e.scalar_tensor_tensor(S0.ap()[:, hp, :], Dg.ap()[:, j, hp, :],
                                                  s0w.ap()[:, hp, j:j + 1],
                                                  S0.ap()[:, hp, :], ALU.mult, ALU.add)
            inc(last, "sv")
        block.vector(f)

        dump("S0", S0[:])

        fixev = {}
        it2 = 0
        for c in range(NCH):
            def f(e, c=c):
                W(e, "st")
                if c == 0:
                    W(e, "sv")
                last = None
                for hp in range(CCH):
                    last = e.tensor_scalar_mul(S0cb.ap()[:, hp, :], S0.ap()[:, hp, :],
                                               wspow.ap()[:, hp, c:c + 1])
                inc(last, "sv")
            block.vector(f)
            scb_done = cnt["sv"]

            for hp in range(CCH):
                pc_ = ps_c0 if it2 % 2 == 0 else ps_c1

                def f(e, c=c, hp=hp, pc_=pc_, it2=it2, scb_done=scb_done):
                    e.wait_ge(sems["sv"], scb_done)
                    if it2 >= 2:
                        e.wait_ge(sems["sv"], fixev[it2 - 2])
                    e.matmul(pc_[0:64, :], S0cb.ap()[0:64, hp, :],
                             rppv(hp, c * Q, (c + 1) * Q)[0:64, :],
                             start=True, stop=True)
                    inc(e.matmul(pc_[64:128, :], S0cb.ap()[64:128, hp, :],
                                 rppv(hp, c * Q, (c + 1) * Q)[64:128, :],
                                 start=True, stop=True), "st")
                block.tensor(f)
                stv = cnt["st"]

                def f(e, c=c, hp=hp, pc_=pc_, stv=stv):
                    e.wait_ge(sems["st"], stv)
                    inc(e.tensor_tensor(ypv(hp, c * Q, (c + 1) * Q),
                                        ypv(hp, c * Q, (c + 1) * Q), pc_[:], ALU.add),
                        "sv")
                block.vector(f)
                fixev[it2] = cnt["sv"]
                it2 += 1

        dump("y_fix", yp_f[:], BF16)
        barrier()
        psP2.close()
        estP2.close()
        estB.close()
        if stop_after == "p2":
            done = True


    if not done:
        estP3 = contextlib.ExitStack()
        woT_sb = SB(estP3, "woT_sb", [128, CCH, C], BF16)
        x3 = SB(estP3, "x3", [128, CCH, 256], F32)
        x2t = SB(estP3, "x2t", [128, CCH, 256], F32)
        xb2 = SB(estP3, "xb2", [128, CCH, 256], BF16)
        xq2sq = SB(estP3, "xq2sq", [128, CCH, 256], BF16)
        h2b = SB(estP3, "h2b", [128, CCH, 256], F32)
        st2m = SB(estP3, "st2m", [128, 256], F32)
        st2v = SB(estP3, "st2v", [128, 256], F32)
        stw3 = SB(estP3, "stw3", [128, 256], F32)
        macc = SB(estP3, "macc", [128, 256], F32)
        sc_tm = SB(estP3, "sc_tm", [128, 4], F32)
        sbc = SB(estP3, "sbc", [128, 256], F32)
        cbc = SB(estP3, "cbc", [128, 256], F32)
        i32q = SB(estP3, "i32q", [128, 256], I32)
        fq = SB(estP3, "fq", [128, 256], F32)
        xqt3 = SB(estP3, "xqt3", [128, CCH, 256], BF16)
        wqs = SB(estP3, "wqs", [128, HID], F32)
        wqi = SB(estP3, "wqi", [128, HID], I32)
        wqo = SB(estP3, "wqo", [128, HID], BF16)

        wq_jobs = [(w1T, kk, HID, w1q_d, 0) for kk in range(CCH)] + \
                  [(w2T, kk, C, w2q_d, 1) for kk in range(HCH)]
        wq_state = {"i": 0, "g": 0}

        def emit_wq(n):
            for _ in range(n):
                if wq_state["i"] >= len(wq_jobs):
                    return
                srcw, kk, width, dst, rcol = wq_jobs[wq_state["i"]]
                wq_state["i"] += 1
                view = srcw.rearrange("(k p) m -> p k m", p=128)

                def f(e, kk=kk, view=view, width=width):
                    if wq_state["g"]:
                        e.wait_ge(sems["sv"], wq_state["g"])
                    inc(e.dma_start(out=wqs.ap()[:, 0:width], in_=view[:, kk, :]),
                        "dq2", 16)
                block.gpsimd(f)
                ld = cnt["dq2"]

                def f(e, ld=ld, width=width, rcol=rcol):
                    e.wait_ge(sems["dq2"], ld)
                    if wq_state.get("st3"):
                        e.wait_ge(sems["dq3"], wq_state["st3"])
                    e.tensor_scalar(wqs.ap()[:, 0:width], wqs.ap()[:, 0:width],
                                    rwb.ap()[:, rcol:rcol + 1], 1.49,
                                    ALU.mult, ALU.min)
                    e.tensor_scalar_max(wqs.ap()[:, 0:width], wqs.ap()[:, 0:width],
                                        -1.49)
                    e.tensor_copy(wqi.ap()[:, 0:width], wqs.ap()[:, 0:width])
                    inc(e.tensor_copy(wqo.ap()[:, 0:width], wqi.ap()[:, 0:width]),
                        "sv")
                block.vector(f)
                wq_state["g"] = cnt["sv"]

                def f(e, kk=kk, dst=dst, width=width):
                    W(e, "sv")
                    inc(e.dma_start(out=dst[kk, :, :], in_=wqo.ap()[:, 0:width]),
                        "dq3", 16)
                block.scalar(f)
                wq_state["st3"] = cnt["dq3"]

        psP3 = contextlib.ExitStack()
        ps_o0 = PS(psP3, "ps_o0", [128, 512])[:, 0:256]
        ps_o1 = PS(psP3, "ps_o1", [128, 512])[:, 0:256]
        ps_s1b = PS(psP3, "ps_s1b", [128, 512])[:, 0:256]
        ps_s2b = PS(psP3, "ps_s2b", [128, 512])[:, 0:256]
        ps_tr3 = PS(psP3, "ps_tr3", [128, 512])[:, 0:256]
        ps_bs = PS(psP3, "ps_bs", [128, 512])[:, 0:256]
        ps_bc1 = PS(psP3, "ps_bc1", [128, 512])[:, 0:256]

        def f(e):
            inc(e.dma_start(out=woT_sb.ap(),
                            in_=woT.rearrange("(k p) o -> p k o", p=128)), "dq2", 16)
        block.gpsimd(f)

        qdone3 = [0]

        def p3_tile(t):
            lo = t * 256
            emit_wq(5)

            def f(e):
                W(e, "sv", "ss")
                inc(e.dma_start(out=x3.ap(),
                                in_=xs[:, 1 + lo:1 + lo + 256]
                                .rearrange("(k p) t -> p k t", p=128)), "dq", 16)
            block.sync(f)

            ev = {}
            for m in range(CCH):
                pp = ps_o0 if m % 2 == 0 else ps_o1

                def f(e, m=m, pp=pp):
                    if m < 2:
                        W(e, "dq2", "sv")
                    else:
                        e.wait_ge(sems["sv"], ev[m - 2])
                    e.matmul(pp[:], bo_rb.ap()[0:1, m * 128:(m + 1) * 128],
                             ones_r.ap()[0:1, 0:256], start=True, stop=False)
                    last = None
                    for k in range(CCH):
                        last = e.matmul(pp[:], woT_sb.ap()[:, k, m * 128:(m + 1) * 128],
                                        ypv(k, lo, lo + 256), start=False,
                                        stop=(k == CCH - 1))
                    inc(last, "st")
                block.tensor(f)
                stv = cnt["st"]

                def f(e, m=m, pp=pp, stv=stv):
                    e.wait_ge(sems["st"], stv)
                    W(e, "dq")
                    inc(e.scalar_tensor_tensor(x2t.ap()[:, m, :], pp[:],
                                               V_["ga"][:, m:m + 1], x3.ap()[:, m, :],
                                               ALU.mult, ALU.add), "sv")
                block.vector(f)
                ev[m] = cnt["sv"]

            def f(e, t=t):
                W(e, "sv")
                inc(e.dma_start(out=x2_d[:, :, lo:lo + 256].rearrange("m p t -> p m t"),
                                in_=x2t.ap()), "dq", 16)
            block.sync(f)

            def f(e):
                W(e, "sv")
                inc(e.dma_start(out=xb2.ap(), in_=x2t.ap()), "dq2", 16)
            block.gpsimd(f)

            def f(e):
                W(e, "dq2")
                inc(e.tensor_mul(xq2sq[:], xb2[:], xb2[:]), "sv")
            block.vector(f)

            def f(e):
                W(e, "sv")
                for k in range(CCH):
                    e.matmul(ps_s1b[:], ones_b[:], xb2.ap()[:, k, :],
                             start=(k == 0), stop=(k == CCH - 1))
                last = None
                for k in range(CCH):
                    last = e.matmul(ps_s2b[:], ones_b[:], xq2sq.ap()[:, k, :],
                                    start=(k == 0), stop=(k == CCH - 1))
                inc(last, "st")
            block.tensor(f)

            def f(e):
                W(e, "st")
                e.tensor_scalar_mul(st2m[:], ps_s1b[:], INV_C)
                e.tensor_mul(st2v[:], st2m[:], st2m[:])
                e.tensor_scalar_sub(st2v[:], st2v[:], EPS)
                inc(e.scalar_tensor_tensor(st2v[:], ps_s2b[:], INV_C, st2v[:],
                                           ALU.mult, ALU.subtract), "sv")
            block.vector(f)

            def f(e):
                W(e, "sv")
                bc_m = st2m.ap()[:, None, :].broadcast_to([128, CCH, 256])
                e.tensor_tensor(x3[:], x2t[:], bc_m, ALU.subtract)
                e.tensor_single_scalar(stw3.ap().bitcast(I32)[:, 0:256], st2v.ap().bitcast(I32)[:, 0:256], 1,
                                       ALU.arith_shift_right)
                e.tensor_sub(stw3.ap().bitcast(I32)[:, 0:256], magicT.ap()[:, 0:256], stw3.ap().bitcast(I32)[:, 0:256])
                for _nit in range(2):
                    e.tensor_mul(st2m[:], st2v[:], stw3[:])
                    e.tensor_mul(st2m[:], st2m[:], stw3[:])
                    e.tensor_scalar(st2m[:], st2m[:], -0.5, 1.5, ALU.mult, ALU.add)
                    e.tensor_mul(stw3[:], stw3[:], st2m[:])
                bc_v = stw3.ap()[:, None, :].broadcast_to([128, CCH, 256])
                inc(e.tensor_tensor(x3[:], x3[:], bc_v, ALU.mult), "sv")
            block.vector(f)

            def f(e):
                W(e, "sv")
                last = None
                for j in range(CCH):
                    last = e.tensor_scalar(h2b.ap()[:, j, :], x3.ap()[:, j, :],
                                           V_["cb1"][:, j:j + 1],
                                           V_["sbv"][:, j:j + 1], ALU.mult, ALU.add)
                inc(last, "sv")
            block.vector(f)

            def f(e):
                W(e, "sv")
                inc(e.tensor_single_scalar(x3.ap().bitcast(I32)[:, :, :],
                                           h2b.ap().bitcast(I32)[:, :, :],
                                           0x7FFFFFFF, ALU.bitwise_and), "sv")
            block.vector(f)

            def f(e):
                W(e, "sv")
                e.tensor_copy(macc[:], x3.ap()[:, 0, :])
                last = None
                for j in range(1, CCH):
                    last = e.tensor_tensor(macc[:], macc[:], x3.ap()[:, j, :], ALU.max)
                inc(last, "sv")
            block.vector(f)

            def f(e):
                W(e, "sv")
                e.transpose(ps_tr3[:, 0:128].bitcast(F32), macc[:, 0:128], ident_f[:])
                inc(e.transpose(ps_tr3[:, 128:256].bitcast(F32), macc[:, 128:256],
                                ident_f[:]), "st")
            block.tensor(f)

            def f(e):
                W(e, "st")
                for hh in range(2):
                    e.reduce_max(sc_tm.ap()[:, hh:hh + 1],
                                 ps_tr3[:, hh * 128:(hh + 1) * 128], AX,
                                 apply_absolute_value=True)
                selfw(e, "sv")
                e.tensor_scalar_max(sc_tm.ap()[:, 0:2], sc_tm.ap()[:, 0:2], 1e-5)
                e.reciprocal(sc_tm.ap()[:, 2:4], sc_tm.ap()[:, 0:2])
                e.tensor_scalar(sc_tm.ap()[:, 0:2], sc_tm.ap()[:, 0:2],
                                mwb.ap()[:, 0:1], 1.0 / 127.0, ALU.mult, ALU.mult)
                inc(e.tensor_scalar_mul(sc_tm.ap()[:, 2:4], sc_tm.ap()[:, 2:4], 127.0),
                    "sv")
            block.vector(f)

            def f(e):
                W(e, "sv")
                last = None
                for hh in range(2):
                    e.tensor_copy(sbc[:, hh * 128:(hh + 1) * 128],
                                  sc_tm.ap()[:, 2 + hh:3 + hh].broadcast_to([128, 128]))
                    last = e.tensor_copy(cbc[:, hh * 128:(hh + 1) * 128],
                                         sc_tm.ap()[:, hh:hh + 1].broadcast_to([128, 128]))
                inc(last, "sv")
            block.vector(f)

            def f(e):
                W(e, "sv")
                e.wait_ge(sems["sv"], qdone3[0])
                last = None
                for hh in range(2):
                    e.matmul(ps_bs[:, hh * 128:(hh + 1) * 128],
                             sbc[:, hh * 128:(hh + 1) * 128], ident_f[:],
                             start=True, stop=True)
                    last = e.matmul(ps_bc1[:, hh * 128:(hh + 1) * 128],
                                    cbc[:, hh * 128:(hh + 1) * 128], ident_f[:],
                                    start=True, stop=True)
                inc(last, "st")
            block.tensor(f)

            if t == 7:
                dump("macc3", macc[:])
                dump("sctm3", sc_tm[:])
                dump("h2b3", h2b[:])

            def f(e, t=t):
                W(e, "st")
                last = None
                for j in range(CCH):
                    e.tensor_tensor(h2b.ap()[:, j, :], h2b.ap()[:, j, :], ps_bs[:],
                                    ALU.mult)
                    e.tensor_copy(i32q[:], h2b.ap()[:, j, :])
                    e.tensor_copy(fq[:], i32q[:])
                    last = e.tensor_tensor(xqt3.ap()[:, j, :], fq[:], ps_bc1[:],
                                           ALU.mult)
                inc(last, "sv")
            block.vector(f)
            qdone3[0] = cnt["sv"]

            def f(e, lo=lo):
                W(e, "sv")
                for j in range(CCH):
                    inc(e.dma_start(out=xqs_d[j, :, lo:lo + 256],
                                    in_=xqt3.ap()[:, j, :]), "dq", 16)
            block.sync(f)

        for t in range(TPC // 256):
            p3_tile(t)
        emit_wq(len(wq_jobs))

        dump("xqt3", xqt3[:], BF16)
        dump("x2t7", x2t[:])
        barrier()
        psP3.close()
        estP3.close()
        estC.close()
        if stop_after == "p3":
            done = True

    if not done:
        estP4 = contextlib.ExitStack()
        w1q_sb = SB(estP4, "w1q_sb", [128, CCH, HID], BF16)
        xq4 = SB(estP4, "xq4", [128, CCH, 258], BF16)
        xqcol = SB(estP4, "xqcol", [128, CCH, 2], BF16)
        e4 = SB(estP4, "e4", [128, HCH, 258], F32)
        z4 = SB(estP4, "z4", [128, HCH, 256], F32)
        xqh_t = SB(estP4, "xqh_t", [128, HCH, 256], BF16)
        halo_sb = SB(estP4, "halo_sb", [128, 2 * HCH], F32)
        haloG = SB(estP4, "haloG", [128, NCORES, HCH, 2], F32)
        i32h = SB(estP4, "i32h", [128, 256], I32)
        macc4 = SB(estP4, "macc4", [128, 256], F32)
        sc4 = SB(estP4, "sc4", [128, 4], F32)
        sbc4 = SB(estP4, "sbc4", [128, 256], F32)

        psP4 = contextlib.ExitStack()
        ps_w0 = PS(psP4, "ps_w0", [128, 512])[:, 0:258]
        ps_w1 = PS(psP4, "ps_w1", [128, 512])[:, 0:258]
        ps_tr4 = PS(psP4, "ps_tr4", [128, 512])[:, 0:256]
        ps_bs4 = PS(psP4, "ps_bs4", [128, 512])[:, 0:256]
        ps_e = PS(psP4, "ps_e", [128, 512])[:, 0:2 * HCH]

        def f(e):
            W(e, "dq3")
            for k in range(CCH):
                inc(e.dma_start(out=w1q_sb.ap()[:, k, :], in_=w1q_d[k, :, :]), "dq", 16)
                inc(e.dma_start(out=xqcol.ap()[:, k, :],
                                in_=xqs_d[k, :, 0:TPC:TPC - 1],
                                allow_slow_non_contiguous=True), "dq", 16)
        block.sync(f)

        def f(e):
            W(e, "dq", "sv")
            last = None
            for hc in range(HCH):
                for k in range(CCH):
                    last = e.matmul(ps_e[:, 2 * hc:2 * hc + 2],
                                    w1q_sb.ap()[:, k, hc * 128:(hc + 1) * 128],
                                    xqcol.ap()[:, k, :],
                                    start=(k == 0), stop=(k == CCH - 1))
            inc(last, "st")
        block.tensor(f)

        def f(e):
            W(e, "st")
            inc(e.tensor_copy(halo_sb[:], ps_e[:]), "sv")
        block.vector(f)

        def f(e):
            W(e, "sv")
            inc(e.dma_start(out=halo_in, in_=halo_sb[:]), "dq", 16)
        block.sync(f)

        def f(e):
            W(e, "dq")
            inc(e.collective_compute("AllGather", ALU.bypass,
                                     replica_groups=[list(range(NCORES))],
                                     ins=[halo_in], outs=[halo_out]), "cc", 1)
            W(e, "cc")
        block.gpsimd(f)

        def f(e):
            W(e, "cc")
            for j in range(NCORES):
                inc(e.dma_start(out=haloG.ap()[:, j, :, :],
                                in_=halo_out[128 * j:128 * (j + 1), :]
                                .rearrange("p (h w) -> p h w", w=2)), "dq", 16)
        block.sync(f)

        def f(e):
            W(e, "dq")
            e.tensor_scalar_mul(eLh[:], haloG.ap()[:, 0, :, 1], selLb.ap()[:, 0:1])
            e.tensor_scalar_mul(eRh[:], haloG.ap()[:, 0, :, 0], selRb.ap()[:, 0:1])
            last = None
            for j in range(1, NCORES):
                e.scalar_tensor_tensor(eLh[:], haloG.ap()[:, j, :, 1],
                                       selLb.ap()[:, j:j + 1], eLh[:],
                                       ALU.mult, ALU.add)
                last = e.scalar_tensor_tensor(eRh[:], haloG.ap()[:, j, :, 0],
                                              selRb.ap()[:, j:j + 1], eRh[:],
                                              ALU.mult, ALU.add)
            inc(last, "sv")
        block.vector(f)

        dump("eLh", eLh[:])

        NT4 = TPC // 256
        qdone4 = [0]

        def p4_tile(t):
            lo = t * 256
            wlo = lo - 1 if t > 0 else lo
            whi = lo + 257 if t < NT4 - 1 else lo + 256
            plo = 0 if t > 0 else 1

            def f(e, wlo=wlo, whi=whi):
                W(e, "sv", "ss")
                for k in range(CCH):
                    inc(e.dma_start(out=xq4.ap()[:, k, 0:whi - wlo],
                                    in_=xqs_d[k, :, wlo:whi]), "dq", 16)
            block.sync(f)

            ev = {}
            for hc in range(HCH):
                pp = ps_w0 if hc % 2 == 0 else ps_w1

                def f(e, hc=hc, pp=pp, wlo=wlo, whi=whi, plo=plo):
                    if hc < 2:
                        W(e, "dq", "sv", "ss")
                    else:
                        e.wait_ge(sems["sv"], ev[hc - 2])
                    last = None
                    for k in range(CCH):
                        last = e.matmul(pp[:, plo:plo + (whi - wlo)],
                                        w1q_sb.ap()[:, k, hc * 128:(hc + 1) * 128],
                                        xq4.ap()[:, k, 0:whi - wlo],
                                        start=(k == 0), stop=(k == CCH - 1))
                    inc(last, "st")
                block.tensor(f)
                stv = cnt["st"]

                def f(e, hc=hc, pp=pp, stv=stv, plo=plo, wlo=wlo, whi=whi):
                    e.wait_ge(sems["st"], stv)
                    inc(e.tensor_copy(e4.ap()[:, hc, plo:plo + (whi - wlo)],
                                      pp[:, plo:plo + (whi - wlo)]), "sv")
                block.vector(f)
                ev[hc] = cnt["sv"]

            if t == 0:
                def f(e):
                    W(e, "sv")
                    inc(e.tensor_copy(e4.ap()[:, :, 0], eLh[:]), "sv")
                block.vector(f)
            if t == NT4 - 1:
                def f(e):
                    W(e, "sv")
                    inc(e.tensor_copy(e4.ap()[:, :, 257], eRh[:]), "sv")
                block.vector(f)

            def f(e):
                W(e, "ss", "sv")
                for hc in range(HCH):
                    e.tensor_scalar(z4.ap()[:, hc, :], e4.ap()[:, hc, 1:257],
                                    CV["w1"][:, hc:hc + 1], CV["b"][:, hc:hc + 1],
                                    ALU.mult, ALU.add)
                last = None
                for hc in range(HCH):
                    e.scalar_tensor_tensor(z4.ap()[:, hc, :], e4.ap()[:, hc, 0:256],
                                           CV["w0"][:, hc:hc + 1], z4.ap()[:, hc, :],
                                           ALU.mult, ALU.add)
                    last = e.scalar_tensor_tensor(z4.ap()[:, hc, :],
                                                  e4.ap()[:, hc, 2:258],
                                                  CV["w2"][:, hc:hc + 1],
                                                  z4.ap()[:, hc, :], ALU.mult, ALU.add)
                inc(last, "sv")
            block.vector(f)

            def f(e):
                W(e, "sv")
                last = None
                for hc in range(HCH):
                    last = e.activation(e4.ap()[:, hc, 0:256], z4.ap()[:, hc, :],
                                        AF.Silu)
                inc(last, "ss")
            block.scalar(f)

            def f(e):
                W(e, "ss", "sv", "sg")
                inc(e.tensor_single_scalar(z4.ap().bitcast(I32)[:, :, :],
                                           e4.ap().bitcast(I32)[:, :, 0:256],
                                           0x7FFFFFFF, ALU.bitwise_and), "sv")
            block.vector(f)

            def f(e):
                W(e, "sv")
                e.tensor_copy(macc4[:], z4.ap()[:, 0, :])
                last = None
                for hc in range(1, HCH):
                    last = e.tensor_tensor(macc4[:], macc4[:], z4.ap()[:, hc, :],
                                           ALU.max)
                inc(last, "sv")
            block.vector(f)

            def f(e):
                W(e, "sv")
                e.transpose(ps_tr4[:, 0:128].bitcast(F32), macc4[:, 0:128], ident_f[:])
                inc(e.transpose(ps_tr4[:, 128:256].bitcast(F32), macc4[:, 128:256],
                                ident_f[:]), "st")
            block.tensor(f)

            def f(e, t=t):
                W(e, "st")
                for hh in range(2):
                    e.reduce_max(sc4.ap()[:, hh:hh + 1],
                                 ps_tr4[:, hh * 128:(hh + 1) * 128], AX,
                                 apply_absolute_value=True)
                selfw(e, "sv")
                e.tensor_scalar_max(sc4.ap()[:, 0:2], sc4.ap()[:, 0:2], 1e-5)
                e.reciprocal(sc4.ap()[:, 2:4], sc4.ap()[:, 0:2])
                e.tensor_scalar(c2row.ap()[:, 2 * t:2 * t + 2], sc4.ap()[:, 0:2],
                                mwb.ap()[:, 1:2], 1.0 / 127.0, ALU.mult, ALU.mult)
                inc(e.tensor_scalar_mul(sc4.ap()[:, 2:4], sc4.ap()[:, 2:4], 127.0),
                    "sv")
            block.vector(f)

            def f(e):
                W(e, "sv")
                e.tensor_copy(sbc4[:, 0:128], sc4.ap()[:, 2:3].broadcast_to([128, 128]))
                inc(e.tensor_copy(sbc4[:, 128:256],
                                  sc4.ap()[:, 3:4].broadcast_to([128, 128])), "sv")
            block.vector(f)

            def f(e):
                W(e, "sv")
                e.wait_ge(sems["sv"], qdone4[0])
                e.matmul(ps_bs4[:, 0:128], sbc4[:, 0:128], ident_f[:],
                         start=True, stop=True)
                inc(e.matmul(ps_bs4[:, 128:256], sbc4[:, 128:256], ident_f[:],
                             start=True, stop=True), "st")
            block.tensor(f)

            def f(e, t=t):
                W(e, "st")
                last = None
                for hc in range(HCH):
                    e.tensor_tensor(z4.ap()[:, hc, :], e4.ap()[:, hc, 0:256],
                                    ps_bs4[:], ALU.mult)
                    e.tensor_copy(i32h[:], z4.ap()[:, hc, :])
                    last = e.tensor_copy(xqh_t.ap()[:, hc, :], i32h[:])
                inc(last, "sv")
            block.vector(f)
            qdone4[0] = cnt["sv"]

            def f(e, t=t, lo=lo):
                W(e, "sv")
                inc(e.dma_start(out=xqh_d[:, :, lo:lo + 256].rearrange("h p t -> p h t"),
                                in_=xqh_t.ap()), "dq", 16)
            block.sync(f)

        for t in range(NT4):
            p4_tile(t)

        barrier()
        psP4.close()
        estP4.close()
        if stop_after == "p4":
            done = True

    if not done:
        estP5 = contextlib.ExitStack()
        w2q_sb = SB(estP5, "w2q_sb", [128, HCH, C], BF16)
        xqh5 = SB(estP5, "xqh5", [128, HCH, 256], BF16)
        x25 = SB(estP5, "x25", [128, CCH, 256], F32)
        out5 = SB(estP5, "out5", [128, CCH, 256], F32)
        c2sb = SB(estP5, "c2sb", [128, 256], F32)
        sbc5 = SB(estP5, "sbc5", [128, 256], F32)
        fq5 = SB(estP5, "fq5", [128, 256], F32)

        psP5 = contextlib.ExitStack()
        ps_f0 = PS(psP5, "ps_f0", [128, 512])[:, 0:256]
        ps_f1 = PS(psP5, "ps_f1", [128, 512])[:, 0:256]
        ps_bc2 = PS(psP5, "ps_bc2", [128, 512])[:, 0:256]

        def f(e):
            for hc in range(HCH):
                inc(e.dma_start(out=w2q_sb.ap()[:, hc, :], in_=w2q_d[hc, :, :]),
                    "dq2", 16)
        block.gpsimd(f)

        def p5_tile(t):
            lo = t * 256

            def f(e, t=t, lo=lo):
                W(e, "sv")
                inc(e.dma_start(out=xqh5.ap(),
                                in_=xqh_d[:, :, lo:lo + 256].rearrange("h p t -> p h t")),
                    "dq", 16)
                inc(e.dma_start(out=x25.ap(),
                                in_=x2_d[:, :, lo:lo + 256].rearrange("m p t -> p m t")),
                    "dq", 16)
            block.sync(f)

            def f(e, t=t):
                W(e, "sv")
                e.tensor_copy(sbc5[:, 0:128],
                              c2row.ap()[:, 2 * t:2 * t + 1].broadcast_to([128, 128]))
                inc(e.tensor_copy(sbc5[:, 128:256],
                                  c2row.ap()[:, 2 * t + 1:2 * t + 2]
                                  .broadcast_to([128, 128])), "sv")
            block.vector(f)

            def f(e):
                W(e, "sv")
                e.matmul(ps_bc2[:, 0:128], sbc5[:, 0:128], ident_f[:],
                         start=True, stop=True)
                inc(e.matmul(ps_bc2[:, 128:256], sbc5[:, 128:256], ident_f[:],
                             start=True, stop=True), "st")
            block.tensor(f)

            def f(e):
                W(e, "st")
                inc(e.tensor_copy(c2sb[:], ps_bc2[:]), "sv")
            block.vector(f)

            ev = {}
            for m in range(CCH):
                pp = ps_f0 if m % 2 == 0 else ps_f1

                def f(e, m=m, pp=pp):
                    if m < 2:
                        W(e, "dq", "sv")
                    else:
                        e.wait_ge(sems["sv"], ev[m - 2])
                    last = None
                    for hc in range(HCH):
                        last = e.matmul(pp[:], w2q_sb.ap()[:, hc, m * 128:(m + 1) * 128],
                                        xqh5.ap()[:, hc, :], start=(hc == 0),
                                        stop=(hc == HCH - 1))
                    inc(last, "st")
                block.tensor(f)
                stv = cnt["st"]

                def f(e, m=m, pp=pp, stv=stv):
                    e.wait_ge(sems["st"], stv)
                    e.tensor_tensor(fq5[:], pp[:], c2sb[:], ALU.mult)
                    inc(e.scalar_tensor_tensor(out5.ap()[:, m, :], fq5[:],
                                               V_["gb"][:, m:m + 1], x25.ap()[:, m, :],
                                               ALU.mult, ALU.add), "sv")
                block.vector(f)
                ev[m] = cnt["sv"]

            def f(e, lo=lo):
                W(e, "sv")
                for m in range(CCH):
                    inc(e.dma_start(out=outT[128 * m:128 * (m + 1), lo:lo + 256],
                                    in_=out5.ap()[:, m, :]), "so", 16)
            block.sync(f)

        for t in range(TPC // 256):
            p5_tile(t)

        barrier()
        psP5.close()
        estP5.close()

    barrier()
    blkg.__exit__(None, None, None)
    semstack.close()
    estB.close()
    estC.close()
    est0.close()
    return nc, dbg_map



def host_inputs(L0_attn_x, L0_attn_time_maa_k, L0_attn_time_maa_v, L0_attn_time_maa_r,
                L0_attn_time_decay, L0_attn_time_faaaa, L0_attn_Wr, L0_attn_br,
                L0_attn_Wk, L0_attn_bk, L0_attn_Wv, L0_attn_bv, L0_attn_Wo, L0_attn_bo,
                L0_ffn_w1, L0_ffn_conv_w, L0_ffn_conv_b, L0_ffn_w2, L0_c, L0_wmod):
    f32 = np.float32
    xT = np.ascontiguousarray(np.asarray(L0_attn_x, f32)[0].T)

    def fm(v, nch=CCH):
        return np.ascontiguousarray(np.asarray(v, f32).reshape(nch, 128).T)

    cw = np.asarray(L0_ffn_conv_w, f32)
    heads = (np.arange(C) // N).astype(np.int64)
    w1T = np.ascontiguousarray(np.asarray(L0_ffn_w1, f32).T)
    w2T = np.ascontiguousarray(np.asarray(L0_ffn_w2, f32).T)
    common = dict(
        wrT=np.ascontiguousarray(np.asarray(L0_attn_Wr, f32).T),
        wkT=np.ascontiguousarray(np.asarray(L0_attn_Wk, f32).T),
        wvT=np.ascontiguousarray(np.asarray(L0_attn_Wv, f32).T),
        woT=np.ascontiguousarray(np.asarray(L0_attn_Wo, f32).T),
        w1T=w1T, w2T=w2T,
        c_fm=fm(np.asarray(L0_c, f32)[0]),
        maa_k=fm(L0_attn_time_maa_k), maa_v=fm(L0_attn_time_maa_v),
        maa_r=fm(L0_attn_time_maa_r),
        bv_row=np.ascontiguousarray(np.asarray(L0_attn_bv, f32)[None, :]),
        bo_row=np.ascontiguousarray(np.asarray(L0_attn_bo, f32)[None, :]),
        br_row=np.ascontiguousarray(np.asarray(L0_attn_br, f32)[None, :]),
        bk_row=np.ascontiguousarray(np.asarray(L0_attn_bk, f32)[None, :]),
        td_fm=fm(np.asarray(L0_attn_time_decay, f32)[heads]),
        td_hb=np.ascontiguousarray(
            np.repeat(np.asarray(L0_attn_time_decay, f32)[None, :], 128, 0)),
        uf_hb=np.ascontiguousarray(
            np.repeat(np.asarray(L0_attn_time_faaaa, f32)[None, :], 128, 0)),
        cw0=fm(cw[:, 0, 0], HCH), cw1=fm(cw[:, 0, 1], HCH), cw2=fm(cw[:, 0, 2], HCH),
        cb_fm=fm(L0_ffn_conv_b, HCH),
    )
    wmod = np.asarray(L0_wmod, f32)
    in_maps = []
    for i in range(NCORES):
        s, e = i * TPC, (i + 1) * TPC
        xsl = np.empty((C, TPC + 1), f32)
        xsl[:, 1:] = xT[:, s:e]
        xsl[:, 0] = xT[:, s - 1] if i > 0 else xT[:, 0]
        ew = np.full((128, NCORES), 1e9, f32)
        for j in range(i):
            ew[:, j] = float(Q * NCH * (i - 1 - j))
        sl = np.zeros((128, NCORES), f32); sr = np.zeros((128, NCORES), f32)
        if i > 0:
            sl[:, i - 1] = 1.0
        if i < NCORES - 1:
            sr[:, i + 1] = 1.0
        m = dict(common)
        m.update(xs=np.ascontiguousarray(xsl),
                 wmod_s=np.ascontiguousarray(wmod[:, i * MODS:(i + 1) * MODS]),
                 w1s=np.ascontiguousarray(w1T[:, i * W1SH:(i + 1) * W1SH]),
                 w2s=np.ascontiguousarray(w2T[:, i * W2SH:(i + 1) * W2SH]),
                 expw=ew, selL=sl, selR=sr,
                 c0f=np.full((128, 1), 1.0 if i == 0 else 0.0, f32))
        in_maps.append(m)
    return in_maps


def kernel(**inp):
    in_maps = host_inputs(**inp)
    nc, _ = build()
    res = run_bass_kernel_spmd(nc, in_maps, core_ids=list(range(NCORES)))
    outs = [res.results[i]["outT"] for i in range(NCORES)]
    full = np.concatenate(outs, axis=1)
    return np.ascontiguousarray(full.T)[None].astype(np.float32)


# version 18
# speedup vs baseline: 1.1991x; 1.0275x over previous
import contextlib
import numpy as np
import concourse.bass as bass
import concourse.mybir as mybir
from concourse.bass_utils import run_bass_kernel_spmd

F32 = mybir.dt.float32
BF16 = mybir.dt.bfloat16
I32 = mybir.dt.int32
AF = mybir.ActivationFunctionType
ALU = mybir.AluOpType
AX = mybir.AxisListType.X

NCORES = 8
T = 16384
C = 1024
H = 16
N = 64
Q = 256
TPC = T // NCORES
NCH = TPC // Q
CCH = C // 128
HID = 4096
HCH = HID // 128
MODS = 6 * C // NCORES
W1SH = HID // NCORES
W2SH = C // NCORES
EPS = 1e-5
INV_C = 1.0 / C


def build(dbg=(), stop_after=None):
    nc = bass.Bass("TRN2", target_bir_lowering=False)
    dbg = set(dbg)
    dbg_map = {}

    def din(name, shape, dt=F32):
        return nc.dram_tensor(name, shape, dt, kind="ExternalInput").ap()

    xs = din("xs", [C, TPC + 1])
    wrT = din("wrT", [C, C]); wkT = din("wkT", [C, C])
    wvT = din("wvT", [C, C]); woT = din("woT", [C, C])
    w1T = din("w1T", [C, HID]); w2T = din("w2T", [HID, C])
    w1s = din("w1s", [C, W1SH]); w2s = din("w2s", [HID, W2SH])
    wmod_s = din("wmod_s", [C, MODS])
    c_fm = din("c_fm", [128, CCH])
    maa = {w: din(f"maa_{w}", [128, CCH]) for w in "kvr"}
    bv_row = din("bv_row", [1, C]); bo_row = din("bo_row", [1, C])
    br_row = din("br_row", [1, C]); bk_row = din("bk_row", [1, C])
    td_fm = din("td_fm", [128, CCH])
    td_hb = din("td_hb", [128, H]); uf_hb = din("uf_hb", [128, H])
    cwv = {k: din(k, [128, HCH]) for k in ("cw0", "cw1", "cw2", "cb_fm")}
    expw = din("expw", [128, NCORES])
    selL = din("selL", [128, NCORES]); selR = din("selR", [128, NCORES])
    c0f = din("c0f", [128, 1])

    outT = nc.dram_tensor("outT", [C, TPC], F32, kind="ExternalOutput").ap()

    mod_in = nc.dram_tensor("mod_in", [1, MODS + 2], F32).ap()
    mod_out = nc.dram_tensor("mod_out", [NCORES, MODS + 2], F32).ap()
    mod_flat = nc.dram_tensor("mod_flat", [NCORES, MODS], F32).ap()
    dsp_in = nc.dram_tensor("dsp_in", [128, NCH * N], F32).ap()
    dsp_out = nc.dram_tensor("dsp_out", [NCORES * 128, NCH * N], F32).ap()
    halo_in = nc.dram_tensor("halo_in", [128, HCH * 2], F32).ap()
    halo_out = nc.dram_tensor("halo_out", [NCORES * 128, HCH * 2], F32).ap()
    w1q_d = nc.dram_tensor("w1q_d", [CCH, 128, HID], BF16).ap()
    w2q_d = nc.dram_tensor("w2q_d", [HCH, 128, C], BF16).ap()
    x2_d = nc.dram_tensor("x2_d", [CCH, 128, TPC], F32).ap()
    xqs_d = nc.dram_tensor("xqs_d", [CCH, 128, TPC], BF16).ap()
    xqh_d = nc.dram_tensor("xqh_d", [HCH, 128, TPC], BF16).ap()

    def SB(stack, name, shape, dt=F32):
        return stack.enter_context(nc.sbuf_tensor(name, shape, dt))

    def PS(stack, name, shape, dt=F32):
        return stack.enter_context(nc.psum_tensor(name, shape, dt))

    est0 = contextlib.ExitStack()
    ones_b = SB(est0, "ones_b", [128, 128], BF16)
    ones_r = SB(est0, "ones_r", [1, 256], BF16)
    ones_fc = SB(est0, "ones_fc", [128, 1], F32)
    ones_fr = SB(est0, "ones_fr", [1, 128], F32)
    ident_f = SB(est0, "ident_f", [128, 128], F32)
    vecs = SB(est0, "vecs", [128, 18 * CCH], F32)
    VN = ["sa", "ca1", "alk", "alv", "alr", "bek", "bev", "ber", "ga", "gb",
          "cb1", "sbv", "lnwd", "z0", "t0", "t1", "t2", "t3"]
    V_ = {nm: vecs.ap()[:, i * CCH:(i + 1) * CCH] for i, nm in enumerate(VN)}
    z0b = SB(est0, "z0b", [128, CCH], BF16)
    lnwd_hb = SB(est0, "lnwd_hb", [128, H], F32)
    ufhb = SB(est0, "ufhb", [128, H], F32)
    expw_sb = SB(est0, "expw_sb", [128, NCORES], F32)
    convs = SB(est0, "convs", [128, 4 * HCH], F32)
    CV = {nm: convs.ap()[:, i * HCH:(i + 1) * HCH]
          for i, nm in enumerate(["w0", "w1", "w2", "b"])}
    wspow = SB(est0, "wspow", [128, CCH, NCH], F32)
    s0w = SB(est0, "s0w", [128, CCH, NCORES], F32)
    selLb = SB(est0, "selLb", [128, NCORES], F32)
    selRb = SB(est0, "selRb", [128, NCORES], F32)
    bv_rb = SB(est0, "bv_rb", [1, C], BF16)
    bo_rb = SB(est0, "bo_rb", [1, C], BF16)
    br_rb = SB(est0, "br_rb", [1, C], BF16)
    bk_rb = SB(est0, "bk_rb", [1, C], BF16)
    mwb = SB(est0, "mwb", [128, 2], F32)
    rwb = SB(est0, "rwb", [128, 2], F32)
    c2row = SB(est0, "c2row", [128, 16], F32)
    c0fb = SB(est0, "c0fb", [128, 1], F32)
    magicT = SB(est0, "magicT", [128, 264], I32)
    eLh = SB(est0, "eLh", [128, HCH], F32)
    eRh = SB(est0, "eRh", [128, HCH], F32)

    estC = contextlib.ExitStack()
    yp_f = SB(estC, "yp_f", [128, CCH * TPC], BF16)
    estB = contextlib.ExitStack()
    rpp_f = SB(estB, "rpp_f", [128, CCH * TPC], BF16)

    def rppv(j, lo, hi):
        return rpp_f.ap()[:, j * TPC + lo: j * TPC + hi]

    def ypv(j, lo, hi):
        return yp_f.ap()[:, j * TPC + lo: j * TPC + hi]

    estA = contextlib.ExitStack()
    wrT_sb = SB(estA, "wrT_sb", [128, CCH, C], BF16)
    wkT_sb = SB(estA, "wkT_sb", [128, CCH, C], BF16)
    wvT_sb = SB(estA, "wvT_sb", [128, CCH, C], BF16)
    masks0 = SB(estA, "masks0", [128, H * Q], BF16)
    masks1 = SB(estA, "masks1", [128, H * 128], BF16)
    wbB = SB(estA, "wbB", [128, CCH * Q], BF16)
    wkBt = SB(estA, "wkBt", [128, 2 * C], BF16)
    Lc = SB(estA, "Lc", [128, 2 * CCH * N], F32)
    Lb = SB(estA, "Lb", [128, CCH, N], BF16)
    xch = SB(estA, "xch", [128, CCH, Q + 1], F32)
    xbf = SB(estA, "xbf", [128, CCH, Q + 1], BF16)
    xsq = SB(estA, "xsq", [128, CCH, Q + 1], BF16)
    uch = SB(estA, "uch", [128, CCH, Q + 1], BF16)
    mixb = SB(estA, "mixb", [128, CCH, Q], BF16)
    statm = SB(estA, "statm", [128, Q + 1], F32)
    statv = SB(estA, "statv", [128, Q + 1], F32)
    statw = SB(estA, "statw", [128, Q + 1], F32)
    rfm = SB(estA, "rfm", [128, CCH, Q], BF16)
    kfm = SB(estA, "kfm", [128, CCH, Q], BF16)
    ktm = SB(estA, "ktm", [128, 2, C], BF16)
    vtm = SB(estA, "vtm", [128, 2, C], BF16)
    Asb = SB(estA, "Asb", [128, 2, Q + 128], BF16)

    estP0 = contextlib.ExitStack()
    siluc = SB(estP0, "siluc", [128, CCH], F32)
    mod_sb = SB(estP0, "mod_sb", [1, MODS + 2], F32)
    modg = SB(estP0, "modg", [NCORES, MODS + 2], F32)
    iotaQ = SB(estP0, "iotaQ", [128, Q], I32)
    dQ = SB(estP0, "dQ", [128, Q], F32)
    I0 = SB(estP0, "I0", [128, Q], F32)
    tauQ = SB(estP0, "tauQ", [128, Q], F32)
    expP = SB(estP0, "expP", [128, 2 * N], F32)
    iotaC = SB(estP0, "iotaC", [128, NCH], I32)
    iotaCf = SB(estP0, "iotaCf", [128, NCH], F32)

    def lcv(s, hp=None):
        b = (s % 2) * CCH * N
        if hp is None:
            return Lc.ap()[:, b:b + CCH * N]
        return Lc.ap()[:, b + hp * N: b + (hp + 1) * N]

    cnt = {}
    sems = {}
    semstack = contextlib.ExitStack()
    for s in ["dq", "dq2", "dq3", "sv", "ss", "sg", "st", "cc", "so"]:
        sems[s] = semstack.enter_context(nc.semaphore(s))
        cnt[s] = 0

    blkg = nc.Block()
    block = blkg.__enter__()

    def inc(instr, name, n=1):
        instr.then_inc(sems[name], n)
        cnt[name] += n
        return cnt[name]

    def W(e, *names):
        for name in names:
            e.wait_ge(sems[name], cnt[name])

    def selfw(e, name):
        inc(e.drain(), name)
        e.wait_ge(sems[name], cnt[name])

    DBGN = [0]

    def dump(key, ap, dt=F32):
        if key not in dbg:
            return
        i = DBGN[0]; DBGN[0] += 1
        d = nc.dram_tensor(f"dbg{i}", list(ap.shape), dt, kind="ExternalOutput").ap()
        dbg_map[key] = f"dbg{i}"

        def f(e):
            W(e, "sv", "ss", "sg", "st", "dq", "dq2", "dq3")
            inc(e.dma_start(out=d, in_=ap), "dq3", 16)
            W(e, "dq3")
        block.sync(f)

    def barrier():
        for eng in ("sync", "vector", "scalar", "gpsimd", "tensor"):
            def f(e):
                W(e, "sv", "ss", "sg", "st", "dq", "dq2", "dq3", "cc", "so")
            getattr(block, eng)(f)

    ps0 = contextlib.ExitStack()
    ps_red = PS(ps0, "ps_red", [1, 512])[:, 0:2]
    ps_bc = PS(ps0, "ps_bc", [128, 512])[:, 0:2]
    ps_mod1 = PS(ps0, "ps_mod1", [128, 512])[0:1, 0:256]
    ps_mod0 = PS(ps0, "ps_mod0", [128, 512])[0:1, :]

    def f(e):
        inc(e.dma_start(out=V_["t0"], in_=c_fm), "dq", 16)
        for w in "kvr":
            inc(e.dma_start(out=V_["al" + w], in_=maa[w]), "dq", 16)
        inc(e.dma_start(out=V_["t1"], in_=td_fm), "dq", 16)
        inc(e.dma_start(out=lnwd_hb[:], in_=td_hb), "dq", 16)
        inc(e.dma_start(out=ufhb[:], in_=uf_hb), "dq", 16)
        inc(e.dma_start(out=selLb[:], in_=selL), "dq", 16)
        inc(e.dma_start(out=selRb[:], in_=selR), "dq", 16)
        inc(e.dma_start(out=c0fb[:], in_=c0f), "dq", 16)
        inc(e.dma_start(out=expw_sb[:], in_=expw), "dq", 16)
        for k_, nmv in (("cw0", "w0"), ("cw1", "w1"), ("cw2", "w2"), ("cb_fm", "b")):
            inc(e.dma_start(out=CV[nmv], in_=cwv[k_]), "dq", 16)
    block.sync(f)

    def f(e):
        inc(e.dma_start(out=bv_rb[:], in_=bv_row), "dq2", 16)
        inc(e.dma_start(out=bo_rb[:], in_=bo_row), "dq2", 16)
        inc(e.dma_start(out=br_rb[:], in_=br_row), "dq2", 16)
        inc(e.dma_start(out=bk_rb[:], in_=bk_row), "dq2", 16)
        for wsb, wsrc in ((wrT_sb, wrT), (wkT_sb, wkT), (wvT_sb, wvT)):
            inc(e.dma_start(out=wsb.ap(),
                            in_=wsrc.rearrange("(k p) o -> p k o", p=128)), "dq2", 16)
    block.gpsimd(f)

    def f(e):
        e.memset(ones_b[:], 1.0)
        e.memset(ones_r[:], 1.0)
        e.memset(ones_fc[:], 1.0)
        e.memset(ones_fr[:], 1.0)
        e.memset(magicT[:], 0x5f3759df)
        e.memset(Lc[:], 0.0)
        e.memset(Lb[:], 0.0)
        e.iota(iotaQ[:], pattern=[[1, Q]], base=-1, channel_multiplier=-1)
        e.iota(iotaC[:], pattern=[[Q, NCH]], base=0, channel_multiplier=0)
        inc(e.drain(), "sg")
    block.gpsimd(f)

    def f(e):
        W(e, "sg")
        e.tensor_copy(dQ[:], iotaQ[:])
        e.tensor_copy(iotaCf[:], iotaC[:])
        selfw(e, "sv")
        e.tensor_single_scalar(I0[:], dQ[:], -1.0, ALU.is_equal)
        e.tensor_scalar_max(dQ[:], dQ[:], -1.0)
        selfw(e, "sv")
        e.tensor_copy(ident_f[:], I0[:, 0:128])
        inc(e.drain(), "sv")
    block.vector(f)

    def f(e):
        W(e, "sv")
        e.iota(iotaQ[:], pattern=[[1, Q]], base=0, channel_multiplier=0)
        selfw(e, "sg")
        inc(e.drain(), "sg")
    block.gpsimd(f)

    def f(e):
        W(e, "sg")
        e.tensor_copy(tauQ[:], iotaQ[:])
        inc(e.drain(), "sv")
    block.vector(f)

    def f(e):
        W(e, "sv")
        for tb in range(2):
            e.iota(iotaQ[:, tb * N:(tb + 1) * N], pattern=[[0, N]],
                   base=Q - 1 - 128 * tb, channel_multiplier=-1)
        inc(e.drain(), "sg")
    block.gpsimd(f)

    def f(e):
        W(e, "sg")
        e.tensor_copy(expP[:], iotaQ[:, 0:2 * N])
        inc(e.drain(), "sv")
    block.vector(f)

    def f(e):
        W(e, "dq")
        e.activation(siluc[:], V_["t0"], AF.Silu)
        e.activation(V_["lnwd"], V_["t1"], AF.Exp)
        e.activation(statw[:, 0:H], lnwd_hb[:], AF.Exp)
        inc(e.drain(), "ss")
    block.scalar(f)

    def f(e):
        W(e, "ss")
        e.tensor_scalar_mul(V_["lnwd"], V_["lnwd"], -1.0)
        e.tensor_scalar_mul(lnwd_hb[:], statw[:, 0:H], -1.0)
        inc(e.drain(), "sv")
    block.vector(f)
    siluc_done = cnt["ss"]
    negd_done = cnt["sv"]
    PCST = {}

    xstage = xch.ap().rearrange("p a b -> p (a b)")
    ustage = uch.ap().rearrange("p a b -> p (a b)")
    w1v = w1s.rearrange("(k p) m -> p k m", p=128)
    w2v = w2s.rearrange("(k p) m -> p k m", p=128)
    red_done = {}

    def w2slot(kk):
        return statm.ap()[:, (kk % 2) * W2SH:(kk % 2 + 1) * W2SH]

    def f(e):
        for kk in range(2):
            inc(e.dma_start(out=w2slot(kk), in_=w2v[:, kk, :]), "dq3", 16)
            red_done[("w2ld", kk)] = cnt["dq3"]
    block.scalar(f)

    def f(e):
        W(e, "sv")
        for hp in range(CCH):
            sc = V_["lnwd"][:, hp:hp + 1]
            e.activation(wspow.ap()[:, hp, :], iotaCf[:], AF.Exp, scale=sc)
            e.activation(s0w.ap()[:, hp, :], expw_sb[:], AF.Exp, scale=sc)
        for j in range(CCH):
            e.activation(wbB.ap()[:, j * Q:(j + 1) * Q], tauQ[:], AF.Exp,
                         scale=V_["lnwd"][:, j:j + 1])
        for tb in range(2):
            for h in range(H):
                e.activation(wkBt.ap()[:, tb * C + h * N: tb * C + (h + 1) * N],
                             expP[:, tb * N:(tb + 1) * N], AF.Exp,
                             scale=lnwd_hb[:, h:h + 1])
        last = None
        for h in range(H):
            last = e.activation(masks0.ap()[:, h * Q:(h + 1) * Q], dQ[:], AF.Exp,
                                scale=lnwd_hb[:, h:h + 1])
        inc(last, "ss")
    block.scalar(f)
    mexp_done = cnt["ss"]

    def f(e):
        e.wait_ge(sems["ss"], mexp_done)
        last = None
        for h in range(H):
            last = e.affine_select(out=masks0.ap()[:, h * Q:(h + 1) * Q],
                                   in_=masks0.ap()[:, h * Q:(h + 1) * Q],
                                   pattern=[[1, Q]], base=-1,
                                   channel_multiplier=-1, compare_op=ALU.is_ge,
                                   fill=0.0)
        inc(last, "sg")
    block.gpsimd(f)
    msel_done = cnt["sg"]

    def f(e):
        e.wait_ge(sems["sg"], msel_done)
        for h in range(H):
            e.scalar_tensor_tensor(masks0.ap()[:, h * Q:(h + 1) * Q], I0[:],
                                   ufhb[:, h:h + 1],
                                   masks0.ap()[:, h * Q:(h + 1) * Q],
                                   ALU.mult, ALU.add)
        inc(e.tensor_copy(
            masks1.ap().rearrange("p (h c) -> p h c", c=128),
            masks0.ap().rearrange("p (h q) -> p h q", q=Q)[:, :, 0:128]), "sv")
    block.vector(f)

    mm_done = {}
    acc_done = {}
    for k in range(CCH):
        slot = xstage[:, (k % 2) * MODS:(k % 2) * MODS + MODS]

        def f(e, k=k, slot=slot):
            if k >= 2:
                e.wait_ge(sems["st"], mm_done[k - 2])
            inc(e.dma_start(out=slot, in_=wmod_s[128 * k:128 * (k + 1), :]),
                "dq", 16)
        block.sync(f)
        ld_done = cnt["dq"]

        def f(e, k=k, slot=slot, ld_done=ld_done):
            e.wait_ge(sems["dq"], ld_done)
            e.wait_ge(sems["ss"], siluc_done)
            if k >= 1:
                e.wait_ge(sems["sv"], acc_done[k - 1])
            e.matmul(ps_mod0[:, 0:512], siluc.ap()[:, k:k + 1],
                     slot[:, 0:512], start=True, stop=True)
            inc(e.matmul(ps_mod1[:, 0:MODS - 512], siluc.ap()[:, k:k + 1],
                         slot[:, 512:MODS], start=True, stop=True), "st")
        block.tensor(f)
        mm_done[k] = cnt["st"]

        def f(e, k=k):
            e.wait_ge(sems["st"], mm_done[k])
            if k == 0:
                e.tensor_copy(mod_sb.ap()[:, 0:512], ps_mod0[:, 0:512])
                inc(e.tensor_copy(mod_sb.ap()[:, 512:MODS],
                                  ps_mod1[:, 0:MODS - 512]), "sv")
            else:
                e.tensor_add(mod_sb.ap()[:, 0:512], mod_sb.ap()[:, 0:512],
                             ps_mod0[:, 0:512])
                inc(e.tensor_add(mod_sb.ap()[:, 512:MODS],
                                 mod_sb.ap()[:, 512:MODS],
                                 ps_mod1[:, 0:MODS - 512]), "sv")
        block.vector(f)
        acc_done[k] = cnt["sv"]
    PCST["xch_free"] = cnt["st"]

    w1slot = xstage[:, 1536:1536 + W1SH]

    def f(e):
        inc(e.dma_start(out=w1slot, in_=w1v[:, 0, :]), "dq2", 16)
        red_done[("w1ld", 0)] = cnt["dq2"]
    block.gpsimd(f)

    for kk in range(CCH):
        def f(e, kk=kk):
            e.wait_ge(sems["dq2"], red_done[("w1ld", kk)])
            inc(e.reduce_sum(statv.ap()[:, kk:kk + 1], w1slot, AX,
                             apply_absolute_value=True), "sv")
        block.vector(f)
        red_done[("w1", kk)] = cnt["sv"]
        if kk + 1 < CCH:
            def f(e, kk=kk):
                e.wait_ge(sems["sv"], red_done[("w1", kk)])
                inc(e.dma_start(out=w1slot, in_=w1v[:, kk + 1, :]), "dq2", 16)
                red_done[("w1ld", kk + 1)] = cnt["dq2"]
            block.gpsimd(f)
    for kk in range(HCH):
        def f(e, kk=kk):
            e.wait_ge(sems["dq3"], red_done[("w2ld", kk)])
            inc(e.reduce_sum(statv.ap()[:, CCH + kk:CCH + kk + 1], w2slot(kk), AX,
                             apply_absolute_value=True), "sv")
        block.vector(f)
        red_done[("w2", kk)] = cnt["sv"]
        if kk + 2 < HCH:
            def f(e, kk=kk):
                e.wait_ge(sems["sv"], red_done[("w2", kk)])
                inc(e.dma_start(out=w2slot(kk + 2), in_=w2v[:, kk + 2, :]),
                    "dq3", 16)
                red_done[("w2ld", kk + 2)] = cnt["dq3"]
            block.scalar(f)

    def f(e):
        W(e, "sv")
        e.reduce_sum(V_["t2"][:, 0:1], statv.ap()[:, 0:CCH], AX)
        e.reduce_sum(V_["t2"][:, 1:2], statv.ap()[:, CCH:CCH + HCH], AX)
        selfw(e, "sv")
        inc(e.drain(), "sv")
    block.vector(f)
    PCST["xch_free_sv"] = cnt["sv"]

    def f(e):
        W(e, "sv")
        e.matmul(ps_red[:], ones_fc[:], V_["t2"][:, 0:2], start=True, stop=True)
        inc(e.drain(), "st")
    block.tensor(f)

    def f(e):
        W(e, "st")
        e.tensor_copy(mod_sb.ap()[:, MODS:MODS + 2], ps_red[:])
        inc(e.drain(), "sv")
    block.vector(f)

    dump("modsb", mod_sb[:])
    dump("siluc", siluc[:])

    def f(e):
        W(e, "sv")
        inc(e.dma_start(out=mod_in, in_=mod_sb[:]), "dq", 16)
    block.sync(f)

    def f(e):
        W(e, "dq")
        inc(e.collective_compute("AllGather", ALU.bypass,
                                 replica_groups=[list(range(NCORES))],
                                 ins=[mod_in], outs=[mod_out]), "cc", 1)
        W(e, "cc")
    block.gpsimd(f)

    def f(e):
        W(e, "cc")
        inc(e.dma_start(out=mod_flat, in_=mod_out[:, 0:MODS]), "dq", 16)
        inc(e.dma_start(out=modg[:], in_=mod_out), "dq", 16)
        W(e, "dq")
        flat = mod_flat.rearrange("a b -> (a b)")
        for i, nm in enumerate(["sa", "ca1", "ga", "sbv", "cb1", "gb"]):
            inc(e.dma_start(out=V_[nm],
                            in_=flat[i * C:(i + 1) * C].rearrange("(j p) -> p j", p=128),
                            allow_slow_non_contiguous=True), "dq", 16)
    block.sync(f)

    def f(e):
        W(e, "dq")
        e.matmul(ps_red[:], ones_fc[0:NCORES, :], modg.ap()[:, MODS:MODS + 2],
                 start=True, stop=True)
        inc(e.drain(), "st")
    block.tensor(f)

    def f(e):
        W(e, "st")
        e.tensor_scalar(V_["t3"][0:1, 0:2], ps_red[:], 1.0 / (C * HID), 1e-5,
                        ALU.mult, ALU.max)
        inc(e.drain(), "sv")
    block.vector(f)

    def f(e):
        W(e, "sv")
        e.matmul(ps_bc[:], ones_fr[:], V_["t3"][0:1, 0:2], start=True, stop=True)
        inc(e.drain(), "st")
    block.tensor(f)

    def f(e):
        W(e, "st")
        e.tensor_copy(mwb[:], ps_bc[:])
        selfw(e, "sv")
        e.reciprocal(rwb[:], mwb[:])
        inc(e.drain(), "sv")
    block.vector(f)

    def f(e):
        W(e, "sv", "dq")
        e.tensor_scalar_add(V_["ca1"], V_["ca1"], 1.0)
        e.tensor_scalar_add(V_["cb1"], V_["cb1"], 1.0)
        selfw(e, "sv")
        for w in "kvr":
            e.tensor_mul(V_["be" + w], V_["al" + w], V_["ca1"])
        selfw(e, "sv")
        for w in "kvr":
            e.tensor_sub(V_["al" + w], V_["ca1"], V_["be" + w])
        e.reciprocal(V_["t0"], V_["ca1"])
        selfw(e, "sv")
        e.tensor_mul(V_["z0"], V_["sa"], V_["t0"])
        selfw(e, "sv")
        e.tensor_scalar_mul(V_["z0"], V_["z0"], -1.0)
        selfw(e, "sv")
        e.tensor_copy(z0b[:], V_["z0"])
        for g in ("ga", "gb"):
            e.tensor_scalar_max(V_["t0"], V_[g], 0.25)
            e.tensor_scalar_min(V_["t1"], V_[g], -0.25)
            e.tensor_single_scalar(V_["t2"], V_[g], 0.0, ALU.is_ge)
            selfw(e, "sv")
            e.tensor_sub(V_["t0"], V_["t0"], V_["t1"])
            selfw(e, "sv")
            e.tensor_mul(V_["t0"], V_["t0"], V_["t2"])
            selfw(e, "sv")
            e.tensor_add(V_[g], V_["t1"], V_["t0"])
            selfw(e, "sv")
        inc(e.drain(), "sv")
    block.vector(f)

    dump("vecs", vecs.ap()[:, 0:16 * CCH])
    dump("masks0", masks0[:], BF16)
    dump("masks1", masks1[:], BF16)
    dump("wspow", wspow[:])
    dump("s0w", s0w[:])
    dump("wbB", wbB[:], BF16)
    dump("wkBt", wkBt[:], BF16)
    dump("mwb", mwb[:])

    done = stop_after == "p0"

    barrier()
    estP0.close()
    if not done:
        ps0.close()
        ps1 = contextlib.ExitStack()
        ps_s1 = PS(ps1, "ps_s1", [128, 512])[:, 0:Q + 1]
        ps_s2 = PS(ps1, "ps_s2", [128, 512])[:, 0:Q + 1]
        ps_p0 = PS(ps1, "ps_p0", [128, 512])
        ps_p1 = PS(ps1, "ps_p1", [128, 512])
        ps_a0 = PS(ps1, "ps_a0", [128, 512])[:, 0:Q]
        ps_a1 = PS(ps1, "ps_a1", [128, 512])[:, 0:128]
        ps_y = PS(ps1, "ps_y", [128, 512])[:, 0:Q]
        ps_d = PS(ps1, "ps_d", [128, 512])[:, 0:N]

        def p1_chunk(c):
            base = c * Q
            mixk = xsq.ap()[:, :, 0:Q]

            def f(e, c=c):
                if "u_done" in PCST:
                    e.wait_ge(sems["sv"], PCST["u_done"])
                if c == 0 and "xch_free" in PCST:
                    e.wait_ge(sems["st"], PCST["xch_free"])
                    e.wait_ge(sems["sv"], PCST["xch_free_sv"])
                inc(e.dma_start(out=xch.ap(),
                                in_=xs[:, base:base + Q + 1]
                                .rearrange("(k p) t -> p k t", p=128)), "dq", 16)
            block.sync(f)
            xload = cnt["dq"]

            def f(e):
                if "stats_done" in PCST:
                    e.wait_ge(sems["st"], PCST["stats_done"])
                inc(e.dma_start(out=xbf.ap(),
                                in_=xs[:, base:base + Q + 1]
                                .rearrange("(k p) t -> p k t", p=128)), "dq2", 16)
            block.gpsimd(f)
            xbload = cnt["dq2"]

            def f(e):
                e.wait_ge(sems["dq2"], xbload)
                if "mixk_done" in PCST:
                    e.wait_ge(sems["st"], PCST["mixk_done"])
                inc(e.tensor_mul(xsq[:], xbf[:], xbf[:]), "sv")
            block.vector(f)
            sq_done = cnt["sv"]

            def f(e):
                e.wait_ge(sems["sv"], sq_done)
                for k in range(CCH):
                    e.matmul(ps_s1[:], ones_b[:], xbf.ap()[:, k, :],
                             start=(k == 0), stop=(k == CCH - 1))
                for k in range(CCH):
                    last = e.matmul(ps_s2[:], ones_b[:], xsq.ap()[:, k, :],
                                    start=(k == 0), stop=(k == CCH - 1))
                inc(last, "st")
            block.tensor(f)
            PCST["stats_done"] = cnt["st"]

            def f(e):
                e.wait_ge(sems["st"], PCST["stats_done"])
                e.tensor_scalar_mul(statm[:], ps_s1[:], INV_C)
                e.tensor_mul(statv[:], statm[:], statm[:])
                e.tensor_scalar_sub(statv[:], statv[:], EPS)
                inc(e.scalar_tensor_tensor(statv[:], ps_s2[:], INV_C, statv[:],
                                           ALU.mult, ALU.subtract), "sv")
            block.vector(f)

            def f(e):
                e.wait_ge(sems["dq"], xload)
                bc_m = statm.ap()[:, None, :].broadcast_to([128, CCH, Q + 1])
                e.tensor_tensor(xch[:], xch[:], bc_m, ALU.subtract)
                e.tensor_single_scalar(statw.ap().bitcast(I32)[:, 0:Q + 1], statv.ap().bitcast(I32)[:, 0:Q + 1], 1,
                                       ALU.arith_shift_right)
                e.tensor_sub(statw.ap().bitcast(I32)[:, 0:Q + 1], magicT.ap()[:, 0:Q + 1], statw.ap().bitcast(I32)[:, 0:Q + 1])
                for _nit in range(2):
                    e.tensor_mul(statm[:], statv[:], statw[:])
                    e.tensor_mul(statm[:], statm[:], statw[:])
                    e.tensor_scalar(statm[:], statm[:], -0.5, 1.5, ALU.mult, ALU.add)
                    e.tensor_mul(statw[:], statw[:], statm[:])
                bc_v = statw.ap()[:, None, :].broadcast_to([128, CCH, Q + 1])
                inc(e.tensor_tensor(uch[:], xch[:], bc_v, ALU.mult), "sv")
            block.vector(f)
            PCST["u_done"] = cnt["sv"]

            if c == 0:
                def f(e):
                    W(e, "sv")
                    e.tensor_sub(mixb.ap()[:, 0, 0:CCH], z0b[:], uch.ap()[:, :, 0])
                    inc(e.scalar_tensor_tensor(uch.ap()[:, :, 0],
                                               mixb.ap()[:, 0, 0:CCH],
                                               c0fb.ap()[:, 0:1], uch.ap()[:, :, 0],
                                               ALU.mult, ALU.add), "sv")
                block.vector(f)
                PCST["u_done"] = cnt["sv"]
                dump("u0", uch[:], BF16)

            def mix(w, buf, free_st):
                def f(e):
                    e.wait_ge(sems["sv"], PCST["u_done"])
                    if free_st is not None:
                        e.wait_ge(sems["st"], free_st)
                    last = None
                    for j in range(CCH):
                        e.tensor_scalar(buf[:, j, :], uch.ap()[:, j, 1:Q + 1],
                                        V_["al" + w][:, j:j + 1],
                                        V_["sa"][:, j:j + 1], ALU.mult, ALU.add)
                        last = e.scalar_tensor_tensor(buf[:, j, :],
                                                      uch.ap()[:, j, 0:Q],
                                                      V_["be" + w][:, j:j + 1],
                                                      buf[:, j, :], ALU.mult, ALU.add)
                    inc(last, "sv")
                block.vector(f)
                return cnt["sv"]

            mr_done = mix("r", mixb.ap(), PCST.get("mixb_free"))
            ev = {}
            for m in range(CCH):
                pp = ps_p0 if m % 2 == 0 else ps_p1

                def f(e, m=m, pp=pp, mr_done=mr_done):
                    e.wait_ge(sems["sv"], mr_done)
                    if m >= 2:
                        e.wait_ge(sems["sv"], ev[m - 2])
                    e.matmul(pp[:, 0:Q], br_rb.ap()[0:1, m * 128:(m + 1) * 128],
                             ones_r.ap()[0:1, 0:Q], start=True, stop=False)
                    last = None
                    for k in range(CCH):
                        last = e.matmul(pp[:, 0:Q],
                                        wrT_sb.ap()[:, k, m * 128:(m + 1) * 128],
                                        mixb.ap()[:, k, :], start=False,
                                        stop=(k == CCH - 1))
                    inc(last, "st")
                block.tensor(f)
                stv = cnt["st"]

                def f(e, m=m, pp=pp, stv=stv, base=base):
                    e.wait_ge(sems["st"], stv)
                    e.tensor_copy(rfm.ap()[:, m, :], pp[:, 0:Q])
                    inc(e.tensor_tensor(rppv(m, base, base + Q), pp[:, 0:Q],
                                        wbB.ap()[:, m * Q:(m + 1) * Q], ALU.mult),
                        "sv")
                block.vector(f)
                ev[m] = cnt["sv"]

            mk_done = mix("k", mixk, PCST["stats_done"])
            ev = {}
            for m in range(CCH):
                pp = ps_p0 if m % 2 == 0 else ps_p1

                def f(e, m=m, pp=pp, mk_done=mk_done):
                    e.wait_ge(sems["sv"], mk_done)
                    if m >= 2:
                        e.wait_ge(sems["sv"], ev[m - 2])
                    e.matmul(pp[:, 0:Q], bk_rb.ap()[0:1, m * 128:(m + 1) * 128],
                             ones_r.ap()[0:1, 0:Q], start=True, stop=False)
                    last = None
                    for k in range(CCH):
                        last = e.matmul(pp[:, 0:Q],
                                        wkT_sb.ap()[:, k, m * 128:(m + 1) * 128],
                                        mixk[:, k, :], start=False,
                                        stop=(k == CCH - 1))
                    inc(last, "st")
                block.tensor(f)
                stv = cnt["st"]

                def f(e, m=m, pp=pp, stv=stv):
                    e.wait_ge(sems["st"], stv)
                    inc(e.tensor_copy(kfm.ap()[:, m, :], pp[:, 0:Q]), "sv")
                block.vector(f)
                ev[m] = cnt["sv"]
            PCST["mixk_done"] = cnt["st"]

            def f(e):
                W(e, "sv")
                for m in range(CCH):
                    for tb in range(2):
                        inc(e.dma_start(out=ktm.ap()[:, tb, m * 128:(m + 1) * 128],
                                        in_=kfm.ap()[:, m, tb * 128:(tb + 1) * 128],
                                        transpose=True), "dq3", 16)
            block.sync(f)
            ktr_done = cnt["dq3"]

            def f(e):
                e.wait_ge(sems["dq3"], ktr_done)
                e.tensor_tensor(ktm.ap()[:, 0, :], ktm.ap()[:, 0, :],
                                wkBt.ap()[:, 0:C], ALU.mult)
                inc(e.tensor_tensor(ktm.ap()[:, 1, :], ktm.ap()[:, 1, :],
                                    wkBt.ap()[:, C:2 * C], ALU.mult), "sv")
            block.vector(f)

            mv_done = mix("v", mixb.ap(), cnt["st"])
            ev = {}
            for i4 in range(4):
                tb, nh = divmod(i4, 2)
                pp = ps_p0 if nh == 0 else ps_p1

                def f(e, i4=i4, tb=tb, nh=nh, pp=pp, mv_done=mv_done):
                    e.wait_ge(sems["sv"], mv_done)
                    if i4 >= 2:
                        e.wait_ge(sems["sv"], ev[i4 - 2])
                    e.matmul(pp[:], ones_r.ap()[0:1, 0:128],
                             bv_rb.ap()[:, nh * 512:(nh + 1) * 512],
                             start=True, stop=False)
                    last = None
                    for k in range(CCH):
                        last = e.matmul(pp[:], mixb.ap()[:, k, tb * 128:(tb + 1) * 128],
                                        wvT_sb.ap()[:, k, nh * 512:(nh + 1) * 512],
                                        start=False, stop=(k == CCH - 1))
                    inc(last, "st")
                block.tensor(f)
                stv = cnt["st"]

                def f(e, tb=tb, nh=nh, pp=pp, stv=stv):
                    e.wait_ge(sems["st"], stv)
                    inc(e.tensor_copy(vtm.ap()[:, tb, nh * 512:(nh + 1) * 512],
                                      pp[:]), "sv")
                block.vector(f)
                ev[i4] = cnt["sv"]
            vev_done = cnt["sv"]
            PCST["mixb_free"] = cnt["st"]

            if c == 0:
                dump("r0", rfm[:], BF16)
                dump("k0", kfm[:], BF16)
                dump("v0", vtm[:], BF16)
                dump("kw0", ktm[:], BF16)
                dump("rpp0", rppv(0, 0, Q), BF16)

            aev = {}
            yev = {}
            stY_prev = [0]
            lue = [0]
            for hp in range(CCH):
                hA, hB = 2 * hp, 2 * hp + 1
                pa0 = ps_a0 if hp % 2 == 0 else ps_s1.bitcast(F32)[:, 0:Q]
                pa1 = ps_a1 if hp % 2 == 0 else ps_s2.bitcast(F32)[:, 0:128]

                def f(e, hp=hp, pa0=pa0, pa1=pa1, vev_done=vev_done):
                    e.wait_ge(sems["sv"], vev_done)
                    if hp >= 2:
                        e.wait_ge(sems["sv"], aev[hp - 2])
                    e.matmul(pa0[:, 0:Q], kfm.ap()[0:64, hp, 0:128],
                             rfm.ap()[0:64, hp, :], start=True, stop=True)
                    inc(e.matmul(pa1[:, 0:128], kfm.ap()[0:64, hp, 128:Q],
                                 rfm.ap()[0:64, hp, 128:Q], start=True, stop=True),
                        "st")
                block.tensor(f)
                stA = cnt["st"]

                def f(e, hp=hp, hA=hA, pa0=pa0, pa1=pa1, stA=stA):
                    e.wait_ge(sems["st"], stA)
                    if hp >= 1:
                        e.wait_ge(sems["st"], stY_prev[0])
                    e.tensor_tensor(Asb.ap()[:, 0, 0:Q], pa0[:, 0:Q],
                                    masks0.ap()[:, hA * Q:(hA + 1) * Q], ALU.mult)
                    inc(e.tensor_tensor(Asb.ap()[:, 0, Q:Q + 128], pa1[:, 0:128],
                                        masks1.ap()[:, hA * 128:(hA + 1) * 128],
                                        ALU.mult), "sv")
                block.vector(f)
                evA = cnt["sv"]

                def f(e, hp=hp, pa0=pa0, pa1=pa1, evA=evA):
                    e.wait_ge(sems["sv"], evA)
                    e.matmul(pa0[:, 0:Q], kfm.ap()[64:128, hp, 0:128],
                             rfm.ap()[64:128, hp, :], start=True, stop=True)
                    inc(e.matmul(pa1[:, 0:128], kfm.ap()[64:128, hp, 128:Q],
                                 rfm.ap()[64:128, hp, 128:Q], start=True, stop=True),
                        "st")
                block.tensor(f)
                stB = cnt["st"]

                def f(e, hp=hp, hB=hB, pa0=pa0, pa1=pa1, stB=stB):
                    e.wait_ge(sems["st"], stB)
                    e.tensor_tensor(Asb.ap()[:, 1, 0:Q], pa0[:, 0:Q],
                                    masks0.ap()[:, hB * Q:(hB + 1) * Q], ALU.mult)
                    inc(e.tensor_tensor(Asb.ap()[:, 1, Q:Q + 128], pa1[:, 0:128],
                                        masks1.ap()[:, hB * 128:(hB + 1) * 128],
                                        ALU.mult), "sv")
                block.vector(f)
                aev[hp] = cnt["sv"]

                def f(e, hp=hp, hA=hA, hB=hB, base=base):
                    e.wait_ge(sems["sv"], aev[hp])
                    if hp >= 1:
                        e.wait_ge(sems["sv"], yev[hp - 1])
                    last = None
                    for hh, head in ((0, hA), (1, hB)):
                        rows = slice(hh * 64, hh * 64 + 64)
                        e.matmul(ps_y[rows, 0:Q], vtm.ap()[:, 0, head * N:(head + 1) * N],
                                 Asb.ap()[:, hh, 0:Q], start=True, stop=False)
                        e.matmul(ps_y[rows, 128:Q], vtm.ap()[:, 1, head * N:(head + 1) * N],
                                 Asb.ap()[:, hh, Q:Q + 128], start=False, stop=False)
                        last = e.matmul(ps_y[rows, 0:Q], Lb.ap()[rows, hp, :],
                                        rppv(hp, base, base + Q)[rows, :],
                                        start=False, stop=True)
                    inc(last, "st")
                block.tensor(f)
                stY = cnt["st"]
                stY_prev[0] = stY

                def f(e, hp=hp, base=base, stY=stY):
                    e.wait_ge(sems["st"], stY)
                    inc(e.tensor_copy(ypv(hp, base, base + Q), ps_y[:]), "sv")
                block.vector(f)
                yev[hp] = cnt["sv"]

                def f(e, hp=hp, hA=hA, hB=hB, stY=stY, lue_v=lue[0]):
                    e.wait_ge(sems["sv"], yev[hp])
                    if lue_v:
                        e.wait_ge(sems["sv"], lue_v)
                    last = None
                    for hh, head in ((0, hA), (1, hB)):
                        rows = slice(hh * 64, hh * 64 + 64)
                        for tb in range(2):
                            last = e.matmul(ps_d[rows, :],
                                            ktm.ap()[:, tb, head * N:(head + 1) * N],
                                            vtm.ap()[:, tb, head * N:(head + 1) * N],
                                            start=(tb == 0), stop=(tb == 1))
                    inc(last, "st")
                block.tensor(f)
                stD = cnt["st"]

                def f(e, hp=hp, c=c, stD=stD):
                    e.wait_ge(sems["st"], stD)
                    inc(e.scalar_tensor_tensor(lcv(c + 1, hp), lcv(c, hp),
                                               wspow.ap()[:, hp, 1:2], ps_d[:],
                                               ALU.mult, ALU.add), "sv")
                block.vector(f)
                lue[0] = cnt["sv"]

            def f(e, c=c):
                W(e, "sv")
                inc(e.tensor_copy(Lb[:],
                                  lcv(c + 1).rearrange("p (h n) -> p h n", h=CCH)),
                    "sv")
            block.vector(f)

            if c == 0:
                dump("yp0", ypv(0, 0, Q), BF16)
                dump("L1", lcv(1))

        last = NCH
        if stop_after and str(stop_after).startswith("p1c"):
            last = int(str(stop_after)[3:]) + 1
        for c in range(last):
            p1_chunk(c)

        dump("Lfin", lcv(NCH))
        dump("yp_full", yp_f[:], BF16)
        if stop_after and str(stop_after).startswith("p1"):
            done = True

        def f(e):
            W(e, "sv", "st")
            inc(e.dma_start(out=dsp_in, in_=lcv(last)), "dq", 16)
        block.sync(f)

        barrier()
        ps1.close()
    estA.close()

    if not done:
        estP2 = contextlib.ExitStack()
        Dg = SB(estP2, "Dg", [128, NCORES, CCH, N], F32)
        S0 = SB(estP2, "S0", [128, CCH, N], F32)
        S0cb = SB(estP2, "S0cb", [128, CCH, N], BF16)
        psP2 = contextlib.ExitStack()
        ps_c0 = PS(psP2, "ps_c0", [128, Q])
        ps_c1 = PS(psP2, "ps_c1", [128, Q])

        def f(e):
            inc(e.collective_compute("AllGather", ALU.bypass,
                                     replica_groups=[list(range(NCORES))],
                                     ins=[dsp_in], outs=[dsp_out]), "cc", 1)
            W(e, "cc")
            e.memset(S0[:], 0.0)
            inc(e.drain(), "sg")
        block.gpsimd(f)

        def f(e):
            W(e, "cc")
            for j in range(NCORES):
                inc(e.dma_start(out=Dg.ap()[:, j, :, :],
                                in_=dsp_out[128 * j:128 * (j + 1), :]
                                .rearrange("p (h n) -> p h n", h=CCH)), "dq", 16)
        block.sync(f)

        def f(e):
            W(e, "dq", "sg")
            last = None
            for j in range(NCORES):
                for hp in range(CCH):
                    last = e.scalar_tensor_tensor(S0.ap()[:, hp, :], Dg.ap()[:, j, hp, :],
                                                  s0w.ap()[:, hp, j:j + 1],
                                                  S0.ap()[:, hp, :], ALU.mult, ALU.add)
            inc(last, "sv")
        block.vector(f)

        dump("S0", S0[:])

        fixev = {}
        it2 = 0
        for c in range(NCH):
            def f(e, c=c):
                W(e, "st")
                if c == 0:
                    W(e, "sv")
                last = None
                for hp in range(CCH):
                    last = e.tensor_scalar_mul(S0cb.ap()[:, hp, :], S0.ap()[:, hp, :],
                                               wspow.ap()[:, hp, c:c + 1])
                inc(last, "sv")
            block.vector(f)
            scb_done = cnt["sv"]

            for hp in range(CCH):
                pc_ = ps_c0 if it2 % 2 == 0 else ps_c1

                def f(e, c=c, hp=hp, pc_=pc_, it2=it2, scb_done=scb_done):
                    e.wait_ge(sems["sv"], scb_done)
                    if it2 >= 2:
                        e.wait_ge(sems["sv"], fixev[it2 - 2])
                    e.matmul(pc_[0:64, :], S0cb.ap()[0:64, hp, :],
                             rppv(hp, c * Q, (c + 1) * Q)[0:64, :],
                             start=True, stop=True)
                    inc(e.matmul(pc_[64:128, :], S0cb.ap()[64:128, hp, :],
                                 rppv(hp, c * Q, (c + 1) * Q)[64:128, :],
                                 start=True, stop=True), "st")
                block.tensor(f)
                stv = cnt["st"]

                def f(e, c=c, hp=hp, pc_=pc_, stv=stv):
                    e.wait_ge(sems["st"], stv)
                    inc(e.tensor_tensor(ypv(hp, c * Q, (c + 1) * Q),
                                        ypv(hp, c * Q, (c + 1) * Q), pc_[:], ALU.add),
                        "sv")
                block.vector(f)
                fixev[it2] = cnt["sv"]
                it2 += 1

        dump("y_fix", yp_f[:], BF16)
        barrier()
        psP2.close()
        estP2.close()
        estB.close()
        if stop_after == "p2":
            done = True


    if not done:
        estP3 = contextlib.ExitStack()
        woT_sb = SB(estP3, "woT_sb", [128, CCH, C], BF16)
        x3 = SB(estP3, "x3", [128, CCH, 256], F32)
        x3b = SB(estP3, "x3b", [128, CCH, 256], F32)
        x2t = SB(estP3, "x2t", [128, CCH, 256], F32)
        xb2 = SB(estP3, "xb2", [128, CCH, 256], BF16)
        xq2sq = SB(estP3, "xq2sq", [128, CCH, 256], BF16)
        h2b = SB(estP3, "h2b", [128, CCH, 256], F32)
        st2m = SB(estP3, "st2m", [128, 256], F32)
        st2v = SB(estP3, "st2v", [128, 256], F32)
        stw3 = SB(estP3, "stw3", [128, 256], F32)
        macc = SB(estP3, "macc", [128, 256], F32)
        sc_tm = SB(estP3, "sc_tm", [128, 4], F32)
        sbc = SB(estP3, "sbc", [128, 256], F32)
        cbc = SB(estP3, "cbc", [128, 256], F32)
        i32q = SB(estP3, "i32q", [128, 256], I32)
        fq = SB(estP3, "fq", [128, 256], F32)
        xqt3 = SB(estP3, "xqt3", [128, CCH, 256], BF16)
        wqs = SB(estP3, "wqs", [128, HID], F32)
        wqi = SB(estP3, "wqi", [128, HID], I32)
        wqo = SB(estP3, "wqo", [128, HID], BF16)

        wq_jobs = [(w1T, kk, HID, w1q_d, 0) for kk in range(CCH)] + \
                  [(w2T, kk, C, w2q_d, 1) for kk in range(HCH)]
        wq_state = {"i": 0, "g": 0}

        def emit_wq(n):
            for _ in range(n):
                if wq_state["i"] >= len(wq_jobs):
                    return
                srcw, kk, width, dst, rcol = wq_jobs[wq_state["i"]]
                wq_state["i"] += 1
                view = srcw.rearrange("(k p) m -> p k m", p=128)

                def f(e, kk=kk, view=view, width=width):
                    if wq_state["g"]:
                        e.wait_ge(sems["sv"], wq_state["g"])
                    inc(e.dma_start(out=wqs.ap()[:, 0:width], in_=view[:, kk, :]),
                        "dq2", 16)
                block.gpsimd(f)
                ld = cnt["dq2"]

                def f(e, ld=ld, width=width, rcol=rcol):
                    e.wait_ge(sems["dq2"], ld)
                    if wq_state.get("st3"):
                        e.wait_ge(sems["dq3"], wq_state["st3"])
                    e.tensor_scalar(wqs.ap()[:, 0:width], wqs.ap()[:, 0:width],
                                    rwb.ap()[:, rcol:rcol + 1], 1.49,
                                    ALU.mult, ALU.min)
                    e.tensor_scalar_max(wqs.ap()[:, 0:width], wqs.ap()[:, 0:width],
                                        -1.49)
                    e.tensor_copy(wqi.ap()[:, 0:width], wqs.ap()[:, 0:width])
                    inc(e.tensor_copy(wqo.ap()[:, 0:width], wqi.ap()[:, 0:width]),
                        "sv")
                block.vector(f)
                wq_state["g"] = cnt["sv"]

                def f(e, kk=kk, dst=dst, width=width):
                    W(e, "sv")
                    inc(e.dma_start(out=dst[kk, :, :], in_=wqo.ap()[:, 0:width]),
                        "dq3", 16)
                block.scalar(f)
                wq_state["st3"] = cnt["dq3"]

        psP3 = contextlib.ExitStack()
        ps_o0 = PS(psP3, "ps_o0", [128, 512])[:, 0:256]
        ps_o1 = PS(psP3, "ps_o1", [128, 512])[:, 0:256]
        ps_s1b = PS(psP3, "ps_s1b", [128, 512])[:, 0:256]
        ps_s2b = PS(psP3, "ps_s2b", [128, 512])[:, 0:256]
        ps_tr3 = PS(psP3, "ps_tr3", [128, 512])[:, 0:256]
        ps_bs = PS(psP3, "ps_bs", [128, 512])[:, 0:256]
        ps_bc1 = PS(psP3, "ps_bc1", [128, 512])[:, 0:256]

        def f(e):
            inc(e.dma_start(out=woT_sb.ap(),
                            in_=woT.rearrange("(k p) o -> p k o", p=128)), "dq2", 16)
        block.gpsimd(f)

        qdone3 = [0]
        p3st = {"ev": {}}

        def p3_tile(t):
            lo = t * 256
            x3c = x3 if t % 2 == 0 else x3b
            emit_wq(5)

            def f(e, t=t):
                if ("x3fr", t - 2) in p3st:
                    e.wait_ge(sems["sv"], p3st[("x3fr", t - 2)])
                inc(e.dma_start(out=x3c.ap(),
                                in_=xs[:, 1 + lo:1 + lo + 256]
                                .rearrange("(k p) t -> p k t", p=128)), "dq", 16)
            block.sync(f)

            ev = p3st["ev"]
            for m in range(CCH):
                pp = ps_o0 if m % 2 == 0 else ps_o1
                gm = t * CCH + m

                def f(e, m=m, pp=pp, gm=gm, t=t):
                    if t == 0 and m < 2:
                        W(e, "dq2", "sv")
                    elif gm - 2 in ev:
                        e.wait_ge(sems["sv"], ev[gm - 2])
                    e.matmul(pp[:], bo_rb.ap()[0:1, m * 128:(m + 1) * 128],
                             ones_r.ap()[0:1, 0:256], start=True, stop=False)
                    last = None
                    for k in range(CCH):
                        last = e.matmul(pp[:], woT_sb.ap()[:, k, m * 128:(m + 1) * 128],
                                        ypv(k, lo, lo + 256), start=False,
                                        stop=(k == CCH - 1))
                    inc(last, "st")
                block.tensor(f)
                stv = cnt["st"]

                def f(e, m=m, pp=pp, stv=stv, x3c=x3c):
                    e.wait_ge(sems["st"], stv)
                    W(e, "dq")
                    inc(e.scalar_tensor_tensor(x2t.ap()[:, m, :], pp[:],
                                               V_["ga"][:, m:m + 1], x3c.ap()[:, m, :],
                                               ALU.mult, ALU.add), "sv")
                block.vector(f)
                ev[gm] = cnt["sv"]

            def f(e, t=t):
                W(e, "sv")
                inc(e.dma_start(out=x2_d[:, :, lo:lo + 256].rearrange("m p t -> p m t"),
                                in_=x2t.ap()), "dq", 16)
            block.sync(f)

            def f(e, t=t):
                W(e, "sv")
                if ("stats", t - 1) in p3st:
                    e.wait_ge(sems["st"], p3st[("stats", t - 1)])
                inc(e.dma_start(out=xb2.ap(), in_=x2t.ap()), "dq2", 16)
            block.gpsimd(f)

            def f(e):
                W(e, "dq2")
                inc(e.tensor_mul(xq2sq[:], xb2[:], xb2[:]), "sv")
            block.vector(f)

            def f(e):
                W(e, "sv")
                for k in range(CCH):
                    e.matmul(ps_s1b[:], ones_b[:], xb2.ap()[:, k, :],
                             start=(k == 0), stop=(k == CCH - 1))
                last = None
                for k in range(CCH):
                    last = e.matmul(ps_s2b[:], ones_b[:], xq2sq.ap()[:, k, :],
                                    start=(k == 0), stop=(k == CCH - 1))
                inc(last, "st")
            block.tensor(f)
            p3st[("stats", t)] = cnt["st"]

            def f(e):
                W(e, "st")
                e.tensor_scalar_mul(st2m[:], ps_s1b[:], INV_C)
                e.tensor_mul(st2v[:], st2m[:], st2m[:])
                e.tensor_scalar_sub(st2v[:], st2v[:], EPS)
                inc(e.scalar_tensor_tensor(st2v[:], ps_s2b[:], INV_C, st2v[:],
                                           ALU.mult, ALU.subtract), "sv")
            block.vector(f)

            def f(e):
                W(e, "sv")
                bc_m = st2m.ap()[:, None, :].broadcast_to([128, CCH, 256])
                e.tensor_tensor(x3c[:], x2t[:], bc_m, ALU.subtract)
                e.tensor_single_scalar(stw3.ap().bitcast(I32)[:, 0:256], st2v.ap().bitcast(I32)[:, 0:256], 1,
                                       ALU.arith_shift_right)
                e.tensor_sub(stw3.ap().bitcast(I32)[:, 0:256], magicT.ap()[:, 0:256], stw3.ap().bitcast(I32)[:, 0:256])
                for _nit in range(2):
                    e.tensor_mul(st2m[:], st2v[:], stw3[:])
                    e.tensor_mul(st2m[:], st2m[:], stw3[:])
                    e.tensor_scalar(st2m[:], st2m[:], -0.5, 1.5, ALU.mult, ALU.add)
                    e.tensor_mul(stw3[:], stw3[:], st2m[:])
                bc_v = stw3.ap()[:, None, :].broadcast_to([128, CCH, 256])
                inc(e.tensor_tensor(x3c[:], x3c[:], bc_v, ALU.mult), "sv")
            block.vector(f)

            def f(e):
                W(e, "sv")
                last = None
                for j in range(CCH):
                    last = e.tensor_scalar(h2b.ap()[:, j, :], x3c.ap()[:, j, :],
                                           V_["cb1"][:, j:j + 1],
                                           V_["sbv"][:, j:j + 1], ALU.mult, ALU.add)
                inc(last, "sv")
            block.vector(f)

            def f(e):
                W(e, "sv")
                inc(e.tensor_single_scalar(x3c.ap().bitcast(I32)[:, :, :],
                                           h2b.ap().bitcast(I32)[:, :, :],
                                           0x7FFFFFFF, ALU.bitwise_and), "sv")
            block.vector(f)

            def f(e):
                W(e, "sv")
                e.tensor_copy(macc[:], x3c.ap()[:, 0, :])
                last = None
                for j in range(1, CCH):
                    last = e.tensor_tensor(macc[:], macc[:], x3c.ap()[:, j, :],
                                           ALU.max)
                inc(last, "sv")
            block.vector(f)
            p3st[("x3fr", t)] = cnt["sv"]

            def f(e):
                W(e, "sv")
                e.transpose(ps_tr3[:, 0:128].bitcast(F32), macc[:, 0:128], ident_f[:])
                inc(e.transpose(ps_tr3[:, 128:256].bitcast(F32), macc[:, 128:256],
                                ident_f[:]), "st")
            block.tensor(f)

            def f(e):
                W(e, "st")
                for hh in range(2):
                    e.reduce_max(sc_tm.ap()[:, hh:hh + 1],
                                 ps_tr3[:, hh * 128:(hh + 1) * 128], AX,
                                 apply_absolute_value=True)
                selfw(e, "sv")
                e.tensor_scalar_max(sc_tm.ap()[:, 0:2], sc_tm.ap()[:, 0:2], 1e-5)
                e.reciprocal(sc_tm.ap()[:, 2:4], sc_tm.ap()[:, 0:2])
                e.tensor_scalar(sc_tm.ap()[:, 0:2], sc_tm.ap()[:, 0:2],
                                mwb.ap()[:, 0:1], 1.0 / 127.0, ALU.mult, ALU.mult)
                inc(e.tensor_scalar_mul(sc_tm.ap()[:, 2:4], sc_tm.ap()[:, 2:4], 127.0),
                    "sv")
            block.vector(f)

            def f(e):
                W(e, "sv")
                last = None
                for hh in range(2):
                    e.tensor_copy(sbc[:, hh * 128:(hh + 1) * 128],
                                  sc_tm.ap()[:, 2 + hh:3 + hh].broadcast_to([128, 128]))
                    last = e.tensor_copy(cbc[:, hh * 128:(hh + 1) * 128],
                                         sc_tm.ap()[:, hh:hh + 1].broadcast_to([128, 128]))
                inc(last, "sv")
            block.vector(f)

            def f(e):
                W(e, "sv")
                e.wait_ge(sems["sv"], qdone3[0])
                last = None
                for hh in range(2):
                    e.matmul(ps_bs[:, hh * 128:(hh + 1) * 128],
                             sbc[:, hh * 128:(hh + 1) * 128], ident_f[:],
                             start=True, stop=True)
                    last = e.matmul(ps_bc1[:, hh * 128:(hh + 1) * 128],
                                    cbc[:, hh * 128:(hh + 1) * 128], ident_f[:],
                                    start=True, stop=True)
                inc(last, "st")
            block.tensor(f)

            if t == 7:
                dump("macc3", macc[:])
                dump("sctm3", sc_tm[:])
                dump("h2b3", h2b[:])

            def f(e, t=t):
                W(e, "st")
                last = None
                for j in range(CCH):
                    e.tensor_tensor(h2b.ap()[:, j, :], h2b.ap()[:, j, :], ps_bs[:],
                                    ALU.mult)
                    e.tensor_copy(i32q[:], h2b.ap()[:, j, :])
                    e.tensor_copy(fq[:], i32q[:])
                    last = e.tensor_tensor(xqt3.ap()[:, j, :], fq[:], ps_bc1[:],
                                           ALU.mult)
                inc(last, "sv")
            block.vector(f)
            qdone3[0] = cnt["sv"]

            def f(e, lo=lo):
                W(e, "sv")
                for j in range(CCH):
                    inc(e.dma_start(out=xqs_d[j, :, lo:lo + 256],
                                    in_=xqt3.ap()[:, j, :]), "dq", 16)
            block.sync(f)

        for t in range(TPC // 256):
            p3_tile(t)
        emit_wq(len(wq_jobs))

        dump("xqt3", xqt3[:], BF16)
        dump("x2t7", x2t[:])
        barrier()
        psP3.close()
        estP3.close()
        estC.close()
        if stop_after == "p3":
            done = True

    if not done:
        estP4 = contextlib.ExitStack()
        w1q_sb = SB(estP4, "w1q_sb", [128, CCH, HID], BF16)
        xq4 = SB(estP4, "xq4", [128, CCH, 258], BF16)
        xqcol = SB(estP4, "xqcol", [128, CCH, 2], BF16)
        e4 = SB(estP4, "e4", [128, HCH, 258], F32)
        z4 = SB(estP4, "z4", [128, HCH, 256], F32)
        xqh_t = SB(estP4, "xqh_t", [128, HCH, 256], BF16)
        halo_sb = SB(estP4, "halo_sb", [128, 2 * HCH], F32)
        haloG = SB(estP4, "haloG", [128, NCORES, HCH, 2], F32)
        i32h = SB(estP4, "i32h", [128, 256], I32)
        macc4 = SB(estP4, "macc4", [128, 256], F32)
        sc4 = SB(estP4, "sc4", [128, 4], F32)
        sbc4 = SB(estP4, "sbc4", [128, 256], F32)

        psP4 = contextlib.ExitStack()
        ps_w0 = PS(psP4, "ps_w0", [128, 512])[:, 0:258]
        ps_w1 = PS(psP4, "ps_w1", [128, 512])[:, 0:258]
        ps_tr4 = PS(psP4, "ps_tr4", [128, 512])[:, 0:256]
        ps_bs4 = PS(psP4, "ps_bs4", [128, 512])[:, 0:256]
        ps_e = PS(psP4, "ps_e", [128, 512])[:, 0:2 * HCH]

        def f(e):
            W(e, "dq3")
            for k in range(CCH):
                inc(e.dma_start(out=w1q_sb.ap()[:, k, :], in_=w1q_d[k, :, :]), "dq", 16)
                inc(e.dma_start(out=xqcol.ap()[:, k, :],
                                in_=xqs_d[k, :, 0:TPC:TPC - 1],
                                allow_slow_non_contiguous=True), "dq", 16)
        block.sync(f)

        def f(e):
            W(e, "dq", "sv")
            last = None
            for hc in range(HCH):
                for k in range(CCH):
                    last = e.matmul(ps_e[:, 2 * hc:2 * hc + 2],
                                    w1q_sb.ap()[:, k, hc * 128:(hc + 1) * 128],
                                    xqcol.ap()[:, k, :],
                                    start=(k == 0), stop=(k == CCH - 1))
            inc(last, "st")
        block.tensor(f)

        def f(e):
            W(e, "st")
            inc(e.tensor_copy(halo_sb[:], ps_e[:]), "sv")
        block.vector(f)

        def f(e):
            W(e, "sv")
            inc(e.dma_start(out=halo_in, in_=halo_sb[:]), "dq", 16)
        block.sync(f)

        def f(e):
            W(e, "dq")
            inc(e.collective_compute("AllGather", ALU.bypass,
                                     replica_groups=[list(range(NCORES))],
                                     ins=[halo_in], outs=[halo_out]), "cc", 1)
            W(e, "cc")
        block.gpsimd(f)

        def f(e):
            W(e, "cc")
            for j in range(NCORES):
                inc(e.dma_start(out=haloG.ap()[:, j, :, :],
                                in_=halo_out[128 * j:128 * (j + 1), :]
                                .rearrange("p (h w) -> p h w", w=2)), "dq", 16)
        block.sync(f)

        def f(e):
            W(e, "dq")
            e.tensor_scalar_mul(eLh[:], haloG.ap()[:, 0, :, 1], selLb.ap()[:, 0:1])
            e.tensor_scalar_mul(eRh[:], haloG.ap()[:, 0, :, 0], selRb.ap()[:, 0:1])
            last = None
            for j in range(1, NCORES):
                e.scalar_tensor_tensor(eLh[:], haloG.ap()[:, j, :, 1],
                                       selLb.ap()[:, j:j + 1], eLh[:],
                                       ALU.mult, ALU.add)
                last = e.scalar_tensor_tensor(eRh[:], haloG.ap()[:, j, :, 0],
                                              selRb.ap()[:, j:j + 1], eRh[:],
                                              ALU.mult, ALU.add)
            inc(last, "sv")
        block.vector(f)

        dump("eLh", eLh[:])

        NT4 = TPC // 256
        qdone4 = [0]

        def p4_tile(t):
            lo = t * 256
            wlo = lo - 1 if t > 0 else lo
            whi = lo + 257 if t < NT4 - 1 else lo + 256
            plo = 0 if t > 0 else 1

            def f(e, wlo=wlo, whi=whi):
                W(e, "sv", "ss")
                for k in range(CCH):
                    inc(e.dma_start(out=xq4.ap()[:, k, 0:whi - wlo],
                                    in_=xqs_d[k, :, wlo:whi]), "dq", 16)
            block.sync(f)

            ev = {}
            for hc in range(HCH):
                pp = ps_w0 if hc % 2 == 0 else ps_w1

                def f(e, hc=hc, pp=pp, wlo=wlo, whi=whi, plo=plo):
                    if hc < 2:
                        W(e, "dq", "sv", "ss")
                    else:
                        e.wait_ge(sems["sv"], ev[hc - 2])
                    last = None
                    for k in range(CCH):
                        last = e.matmul(pp[:, plo:plo + (whi - wlo)],
                                        w1q_sb.ap()[:, k, hc * 128:(hc + 1) * 128],
                                        xq4.ap()[:, k, 0:whi - wlo],
                                        start=(k == 0), stop=(k == CCH - 1))
                    inc(last, "st")
                block.tensor(f)
                stv = cnt["st"]

                def f(e, hc=hc, pp=pp, stv=stv, plo=plo, wlo=wlo, whi=whi):
                    e.wait_ge(sems["st"], stv)
                    inc(e.tensor_copy(e4.ap()[:, hc, plo:plo + (whi - wlo)],
                                      pp[:, plo:plo + (whi - wlo)]), "sv")
                block.vector(f)
                ev[hc] = cnt["sv"]

            if t == 0:
                def f(e):
                    W(e, "sv")
                    inc(e.tensor_copy(e4.ap()[:, :, 0], eLh[:]), "sv")
                block.vector(f)
            if t == NT4 - 1:
                def f(e):
                    W(e, "sv")
                    inc(e.tensor_copy(e4.ap()[:, :, 257], eRh[:]), "sv")
                block.vector(f)

            def f(e):
                W(e, "ss", "sv")
                for hc in range(HCH):
                    e.tensor_scalar(z4.ap()[:, hc, :], e4.ap()[:, hc, 1:257],
                                    CV["w1"][:, hc:hc + 1], CV["b"][:, hc:hc + 1],
                                    ALU.mult, ALU.add)
                last = None
                for hc in range(HCH):
                    e.scalar_tensor_tensor(z4.ap()[:, hc, :], e4.ap()[:, hc, 0:256],
                                           CV["w0"][:, hc:hc + 1], z4.ap()[:, hc, :],
                                           ALU.mult, ALU.add)
                    last = e.scalar_tensor_tensor(z4.ap()[:, hc, :],
                                                  e4.ap()[:, hc, 2:258],
                                                  CV["w2"][:, hc:hc + 1],
                                                  z4.ap()[:, hc, :], ALU.mult, ALU.add)
                inc(last, "sv")
            block.vector(f)

            def f(e):
                W(e, "sv")
                last = None
                for hc in range(HCH):
                    last = e.activation(e4.ap()[:, hc, 0:256], z4.ap()[:, hc, :],
                                        AF.Silu)
                inc(last, "ss")
            block.scalar(f)

            def f(e):
                W(e, "ss", "sv", "sg")
                inc(e.tensor_single_scalar(z4.ap().bitcast(I32)[:, :, :],
                                           e4.ap().bitcast(I32)[:, :, 0:256],
                                           0x7FFFFFFF, ALU.bitwise_and), "sv")
            block.vector(f)

            def f(e):
                W(e, "sv")
                e.tensor_copy(macc4[:], z4.ap()[:, 0, :])
                last = None
                for hc in range(1, HCH):
                    last = e.tensor_tensor(macc4[:], macc4[:], z4.ap()[:, hc, :],
                                           ALU.max)
                inc(last, "sv")
            block.vector(f)

            def f(e):
                W(e, "sv")
                e.transpose(ps_tr4[:, 0:128].bitcast(F32), macc4[:, 0:128], ident_f[:])
                inc(e.transpose(ps_tr4[:, 128:256].bitcast(F32), macc4[:, 128:256],
                                ident_f[:]), "st")
            block.tensor(f)

            def f(e, t=t):
                W(e, "st")
                for hh in range(2):
                    e.reduce_max(sc4.ap()[:, hh:hh + 1],
                                 ps_tr4[:, hh * 128:(hh + 1) * 128], AX,
                                 apply_absolute_value=True)
                selfw(e, "sv")
                e.tensor_scalar_max(sc4.ap()[:, 0:2], sc4.ap()[:, 0:2], 1e-5)
                e.reciprocal(sc4.ap()[:, 2:4], sc4.ap()[:, 0:2])
                e.tensor_scalar(c2row.ap()[:, 2 * t:2 * t + 2], sc4.ap()[:, 0:2],
                                mwb.ap()[:, 1:2], 1.0 / 127.0, ALU.mult, ALU.mult)
                inc(e.tensor_scalar_mul(sc4.ap()[:, 2:4], sc4.ap()[:, 2:4], 127.0),
                    "sv")
            block.vector(f)

            def f(e):
                W(e, "sv")
                e.tensor_copy(sbc4[:, 0:128], sc4.ap()[:, 2:3].broadcast_to([128, 128]))
                inc(e.tensor_copy(sbc4[:, 128:256],
                                  sc4.ap()[:, 3:4].broadcast_to([128, 128])), "sv")
            block.vector(f)

            def f(e):
                W(e, "sv")
                e.wait_ge(sems["sv"], qdone4[0])
                e.matmul(ps_bs4[:, 0:128], sbc4[:, 0:128], ident_f[:],
                         start=True, stop=True)
                inc(e.matmul(ps_bs4[:, 128:256], sbc4[:, 128:256], ident_f[:],
                             start=True, stop=True), "st")
            block.tensor(f)

            def f(e, t=t):
                W(e, "st")
                last = None
                for hc in range(HCH):
                    e.tensor_tensor(z4.ap()[:, hc, :], e4.ap()[:, hc, 0:256],
                                    ps_bs4[:], ALU.mult)
                    e.tensor_copy(i32h[:], z4.ap()[:, hc, :])
                    last = e.tensor_copy(xqh_t.ap()[:, hc, :], i32h[:])
                inc(last, "sv")
            block.vector(f)
            qdone4[0] = cnt["sv"]

            def f(e, t=t, lo=lo):
                W(e, "sv")
                inc(e.dma_start(out=xqh_d[:, :, lo:lo + 256].rearrange("h p t -> p h t"),
                                in_=xqh_t.ap()), "dq", 16)
            block.sync(f)

        for t in range(NT4):
            p4_tile(t)

        barrier()
        psP4.close()
        estP4.close()
        if stop_after == "p4":
            done = True

    if not done:
        estP5 = contextlib.ExitStack()
        w2q_sb = SB(estP5, "w2q_sb", [128, HCH, C], BF16)
        xqh5 = SB(estP5, "xqh5", [128, HCH, 256], BF16)
        x25 = SB(estP5, "x25", [128, CCH, 256], F32)
        out5 = SB(estP5, "out5", [128, CCH, 256], F32)
        c2sb = SB(estP5, "c2sb", [128, 256], F32)
        sbc5 = SB(estP5, "sbc5", [128, 256], F32)
        fq5 = SB(estP5, "fq5", [128, 256], F32)

        psP5 = contextlib.ExitStack()
        ps_f0 = PS(psP5, "ps_f0", [128, 512])[:, 0:256]
        ps_f1 = PS(psP5, "ps_f1", [128, 512])[:, 0:256]
        ps_bc2 = PS(psP5, "ps_bc2", [128, 512])[:, 0:256]

        def f(e):
            for hc in range(HCH):
                inc(e.dma_start(out=w2q_sb.ap()[:, hc, :], in_=w2q_d[hc, :, :]),
                    "dq2", 16)
        block.gpsimd(f)

        def p5_tile(t):
            lo = t * 256

            def f(e, t=t, lo=lo):
                W(e, "sv")
                inc(e.dma_start(out=xqh5.ap(),
                                in_=xqh_d[:, :, lo:lo + 256].rearrange("h p t -> p h t")),
                    "dq", 16)
                inc(e.dma_start(out=x25.ap(),
                                in_=x2_d[:, :, lo:lo + 256].rearrange("m p t -> p m t")),
                    "dq", 16)
            block.sync(f)

            def f(e, t=t):
                W(e, "sv")
                e.tensor_copy(sbc5[:, 0:128],
                              c2row.ap()[:, 2 * t:2 * t + 1].broadcast_to([128, 128]))
                inc(e.tensor_copy(sbc5[:, 128:256],
                                  c2row.ap()[:, 2 * t + 1:2 * t + 2]
                                  .broadcast_to([128, 128])), "sv")
            block.vector(f)

            def f(e):
                W(e, "sv")
                e.matmul(ps_bc2[:, 0:128], sbc5[:, 0:128], ident_f[:],
                         start=True, stop=True)
                inc(e.matmul(ps_bc2[:, 128:256], sbc5[:, 128:256], ident_f[:],
                             start=True, stop=True), "st")
            block.tensor(f)

            def f(e):
                W(e, "st")
                inc(e.tensor_copy(c2sb[:], ps_bc2[:]), "sv")
            block.vector(f)

            ev = {}
            for m in range(CCH):
                pp = ps_f0 if m % 2 == 0 else ps_f1

                def f(e, m=m, pp=pp):
                    if m < 2:
                        W(e, "dq", "sv")
                    else:
                        e.wait_ge(sems["sv"], ev[m - 2])
                    last = None
                    for hc in range(HCH):
                        last = e.matmul(pp[:], w2q_sb.ap()[:, hc, m * 128:(m + 1) * 128],
                                        xqh5.ap()[:, hc, :], start=(hc == 0),
                                        stop=(hc == HCH - 1))
                    inc(last, "st")
                block.tensor(f)
                stv = cnt["st"]

                def f(e, m=m, pp=pp, stv=stv):
                    e.wait_ge(sems["st"], stv)
                    e.tensor_tensor(fq5[:], pp[:], c2sb[:], ALU.mult)
                    inc(e.scalar_tensor_tensor(out5.ap()[:, m, :], fq5[:],
                                               V_["gb"][:, m:m + 1], x25.ap()[:, m, :],
                                               ALU.mult, ALU.add), "sv")
                block.vector(f)
                ev[m] = cnt["sv"]

            def f(e, lo=lo):
                W(e, "sv")
                for m in range(CCH):
                    inc(e.dma_start(out=outT[128 * m:128 * (m + 1), lo:lo + 256],
                                    in_=out5.ap()[:, m, :]), "so", 16)
            block.sync(f)

        for t in range(TPC // 256):
            p5_tile(t)

        barrier()
        psP5.close()
        estP5.close()

    barrier()
    blkg.__exit__(None, None, None)
    semstack.close()
    estB.close()
    estC.close()
    est0.close()
    return nc, dbg_map



def host_inputs(L0_attn_x, L0_attn_time_maa_k, L0_attn_time_maa_v, L0_attn_time_maa_r,
                L0_attn_time_decay, L0_attn_time_faaaa, L0_attn_Wr, L0_attn_br,
                L0_attn_Wk, L0_attn_bk, L0_attn_Wv, L0_attn_bv, L0_attn_Wo, L0_attn_bo,
                L0_ffn_w1, L0_ffn_conv_w, L0_ffn_conv_b, L0_ffn_w2, L0_c, L0_wmod):
    f32 = np.float32
    xT = np.ascontiguousarray(np.asarray(L0_attn_x, f32)[0].T)

    def fm(v, nch=CCH):
        return np.ascontiguousarray(np.asarray(v, f32).reshape(nch, 128).T)

    cw = np.asarray(L0_ffn_conv_w, f32)
    heads = (np.arange(C) // N).astype(np.int64)
    w1T = np.ascontiguousarray(np.asarray(L0_ffn_w1, f32).T)
    w2T = np.ascontiguousarray(np.asarray(L0_ffn_w2, f32).T)
    common = dict(
        wrT=np.ascontiguousarray(np.asarray(L0_attn_Wr, f32).T),
        wkT=np.ascontiguousarray(np.asarray(L0_attn_Wk, f32).T),
        wvT=np.ascontiguousarray(np.asarray(L0_attn_Wv, f32).T),
        woT=np.ascontiguousarray(np.asarray(L0_attn_Wo, f32).T),
        w1T=w1T, w2T=w2T,
        c_fm=fm(np.asarray(L0_c, f32)[0]),
        maa_k=fm(L0_attn_time_maa_k), maa_v=fm(L0_attn_time_maa_v),
        maa_r=fm(L0_attn_time_maa_r),
        bv_row=np.ascontiguousarray(np.asarray(L0_attn_bv, f32)[None, :]),
        bo_row=np.ascontiguousarray(np.asarray(L0_attn_bo, f32)[None, :]),
        br_row=np.ascontiguousarray(np.asarray(L0_attn_br, f32)[None, :]),
        bk_row=np.ascontiguousarray(np.asarray(L0_attn_bk, f32)[None, :]),
        td_fm=fm(np.asarray(L0_attn_time_decay, f32)[heads]),
        td_hb=np.ascontiguousarray(
            np.repeat(np.asarray(L0_attn_time_decay, f32)[None, :], 128, 0)),
        uf_hb=np.ascontiguousarray(
            np.repeat(np.asarray(L0_attn_time_faaaa, f32)[None, :], 128, 0)),
        cw0=fm(cw[:, 0, 0], HCH), cw1=fm(cw[:, 0, 1], HCH), cw2=fm(cw[:, 0, 2], HCH),
        cb_fm=fm(L0_ffn_conv_b, HCH),
    )
    wmod = np.asarray(L0_wmod, f32)
    in_maps = []
    for i in range(NCORES):
        s, e = i * TPC, (i + 1) * TPC
        xsl = np.empty((C, TPC + 1), f32)
        xsl[:, 1:] = xT[:, s:e]
        xsl[:, 0] = xT[:, s - 1] if i > 0 else xT[:, 0]
        ew = np.full((128, NCORES), 1e9, f32)
        for j in range(i):
            ew[:, j] = float(Q * NCH * (i - 1 - j))
        sl = np.zeros((128, NCORES), f32); sr = np.zeros((128, NCORES), f32)
        if i > 0:
            sl[:, i - 1] = 1.0
        if i < NCORES - 1:
            sr[:, i + 1] = 1.0
        m = dict(common)
        m.update(xs=np.ascontiguousarray(xsl),
                 wmod_s=np.ascontiguousarray(wmod[:, i * MODS:(i + 1) * MODS]),
                 w1s=np.ascontiguousarray(w1T[:, i * W1SH:(i + 1) * W1SH]),
                 w2s=np.ascontiguousarray(w2T[:, i * W2SH:(i + 1) * W2SH]),
                 expw=ew, selL=sl, selR=sr,
                 c0f=np.full((128, 1), 1.0 if i == 0 else 0.0, f32))
        in_maps.append(m)
    return in_maps


def kernel(**inp):
    in_maps = host_inputs(**inp)
    nc, _ = build()
    res = run_bass_kernel_spmd(nc, in_maps, core_ids=list(range(NCORES)))
    outs = [res.results[i]["outT"] for i in range(NCORES)]
    full = np.concatenate(outs, axis=1)
    return np.ascontiguousarray(full.T)[None].astype(np.float32)
